# version 01
# baseline (speedup 1.0000x reference)
import numpy as np
import contextlib
import concourse.bass as bass
import concourse.mybir as mybir
from concourse.bass_utils import run_bass_kernel_spmd

F32 = mybir.dt.float32
BF16 = mybir.dt.bfloat16
I16 = mybir.dt.int16
I32 = mybir.dt.int32
U32 = mybir.dt.uint32
U8 = mybir.dt.uint8
AF = mybir.ActivationFunctionType
ALU = mybir.AluOpType
AX = mybir.AxisListType
BOOT_MAGIC = 5


class SemT:
    def __init__(self, h, name):
        self.h = h
        self.name = name
        self.final = 0


class EngCtx:
    def __init__(self, eng, tick, name):
        self.e = eng
        self.tick = tick
        self.name = name
        self.last = None

    def __getattr__(self, name):
        fn = getattr(self.e, name)

        def wrap(*a, **k):
            inst = fn(*a, **k)
            self.last = inst
            return inst

        return wrap

    def barrier(self):
        if self.last is None:
            return
        self.last.then_inc(self.tick.h, 1)
        self.tick.final += 1
        self.e.wait_ge(self.tick.h, self.tick.final)
        self.last = None

    def inc(self, sem, n=1):
        if self.last is None:
            self.sinc(sem, n)
            return
        self.last.then_inc(sem.h, n)
        sem.final += n
        self.last = None

    def sinc(self, sem, n=1):
        self.e.sem_inc(sem.h, n)
        sem.final += n

    def wait(self, sem, v):
        self.e.wait_ge(sem.h, v)

    def wait_cur(self, sem):
        self.e.wait_ge(sem.h, sem.final)


class E:
    pass


class Prog:
    def __init__(self):
        self.nc = bass.Bass("TRN2", target_bir_lowering=False, debug=False, dynamic_dma_scratch_size=4096)
        self._es = contextlib.ExitStack()
        self.sems = []
        self._body = None

    def sbuf(self, nm, shape, dt=F32):
        return self._es.enter_context(self.nc.sbuf_tensor(nm, shape, dt))

    def psum(self, nm, shape, dt=F32):
        return self._es.enter_context(self.nc.psum_tensor(nm, shape, dt))

    def sem(self, nm):
        s = self._es.enter_context(self.nc.semaphore(nm))
        st = SemT(s, nm)
        self.sems.append(st)
        return st

    def dram(self, nm, shape, dt=F32, kind=None):
        if kind:
            t = self.nc.dram_tensor(nm, shape, dt, kind=kind)
        else:
            t = self.nc.dram_tensor(nm, shape, dt)
        return t.ap()

    def body(self, f):
        self._body = f
        return f

    def finalize(self):
        nc = self.nc
        boot = self.sem("boot")
        fin = self.sem("fin")
        ticks = {e: self.sem(f"tk_{e}") for e in ["v", "s", "t", "g", "y"]}
        semlo = min(s.h.num for s in self.sems)
        semhi = max(s.h.num for s in self.sems)
        block = self._es.enter_context(nc.Block())

        @block.sync
        def _(v):
            g = nc.gpsimd
            g.sem_clear(range(semlo, semhi + 1))
            g.sem_inc(boot.h, BOOT_MAGIC)
            en = E()
            en.nc = nc
            en.g = EngCtx(nc.gpsimd, ticks["g"], "g")
            en.v = EngCtx(nc.vector, ticks["v"], "v")
            en.s = EngCtx(nc.scalar, ticks["s"], "s")
            en.t = EngCtx(nc.tensor, ticks["t"], "t")
            en.y = EngCtx(nc.sync, ticks["y"], "y")
            for c in [en.v, en.s, en.t, en.y]:
                c.e.wait_op(boot.h, BOOT_MAGIC, "sem-eq")
            self._body(en)
            for c in [en.v, en.s, en.t, en.y]:
                for s_ in self.sems:
                    if s_.final > 0 and s_ is not fin and s_ is not boot:
                        c.e.wait_ge(s_.h, s_.final)
                c.e.sem_inc(fin.h, 1)
            fin.final += 4
            g.wait_ge(fin.h, fin.final)
            for s_ in self.sems:
                if s_.final > 0 and s_ is not fin and s_ is not boot:
                    g.wait_ge(s_.h, s_.final)
        self._es.close()
        return nc




"""Routing: 50-iter coor_descent (iter 1 folded into host fp32 const) + selection.

Gather-free outputs, per route r in {q(K=1024,PAD=256), kv(K=2048,PAD=512)}:
  rpT_q / rpT_kv [128, 16] f32: per own-token local slot (in [0,PAD) if selected,
           else negative/garbage>=PAD) -- partition = token%128, col = token//128.
  ogb  [128, 2] f32: this core's global rank offset o_c per route (broadcast).
  cntb [128, 2] f32: number of valid own slots per route (broadcast).
All cores compute redundantly from s_all [2,16384]; own-range uses s_own [2,2048].
"""

N_TOK = 16384
NQ, NKV = 1024, 2048
QPAD, KVPAD = 256, 512
N_ITER_DEV = 49


def host_consts(core_id):
    L16 = np.zeros((128, 16), np.float32)
    for m in range(16):
        L16[:m, m] = 1.0
    ind16 = np.zeros((128, 16), np.float32)
    ind16[: 16 * core_id, :] = 1.0
    logk = np.log(np.array([1152.0, 2304.0], dtype=np.float32)).astype(np.float32).reshape(1, 2)
    na1 = -(logk - np.float32(np.log(np.float32(16384.0)))).astype(np.float32)
    Krow = np.array([[1024.0, 2048.0]], dtype=np.float32)
    return dict(rt_L16=L16, rt_ind16=ind16, rt_logk=logk, rt_na1=na1, rt_Krow=Krow)


def alloc(p, psums=None):
    B = {}
    for nm, shape in [("rt_L16", [128, 16]), ("rt_ind16", [128, 16]),
                      ("rt_logk", [1, 2]), ("rt_na1", [1, 2]), ("rt_Krow", [1, 2])]:
        B["d_" + nm] = p.dram(nm, shape, F32, kind="ExternalInput")
        B[nm] = p.sbuf(nm + "_s", shape, F32)
    B["s2"] = p.sbuf("rt_s2", [128, 256], F32)
    B["so16"] = p.sbuf("rt_so16", [16, 256], F32)
    B["na2"] = p.sbuf("rt_na2", [128, 2], F32)
    B["nC2"] = p.sbuf("rt_nC2", [128, 2], F32)
    B["t2"] = p.sbuf("rt_t2", [128, 256], F32)
    B["e2"] = p.sbuf("rt_e2", [128, 256], F32)
    B["rows2"] = p.sbuf("rt_rows2", [128, 2], F32)
    B["lnS"] = p.sbuf("rt_lnS", [1, 2], F32)
    B["nan2"] = p.sbuf("rt_nan2", [1, 2], F32)
    B["satg"] = p.sbuf("rt_satg", [128, 256], F32)
    B["rowsg"] = p.sbuf("rt_rowsg", [128, 2], F32)
    B["uo16"] = p.sbuf("rt_uo16", [16, 256], F32)
    B["sato"] = p.sbuf("rt_sato", [16, 256], F32)
    B["incl"] = p.sbuf("rt_incl", [16, 256], F32)
    B["zer16"] = p.sbuf("rt_zer16", [16, 256], F32)
    B["rowtot"] = p.sbuf("rt_rowtot", [16, 2], F32)
    B["offsL"] = p.sbuf("rt_offsL", [16, 2], F32)
    B["og16"] = p.sbuf("rt_og16", [16, 2], F32)
    B["tot"] = p.sbuf("rt_tot", [1, 2], F32)
    B["cnt"] = p.sbuf("rt_cnt", [1, 2], F32)
    B["lrank"] = p.sbuf("rt_lrank", [16, 256], F32)
    B["sel"] = p.sbuf("rt_sel", [16, 256], F32)
    B["rankp"] = p.sbuf("rt_rankp", [16, 256], F32)
    B["rpT_q"] = p.sbuf("rt_rpT_q", [128, 16], F32)
    B["rpT_kv"] = p.sbuf("rt_rpT_kv", [128, 16], F32)
    B["ogb"] = p.sbuf("rt_ogb", [128, 2], F32)
    B["cntb"] = p.sbuf("rt_cntb", [128, 2], F32)
    if psums is None:
        B["pr12"] = p.psum("rt_pr12", [1, 2], F32)
        B["pb"] = p.psum("rt_pb", [128, 2], F32)
        B["p16"] = p.psum("rt_p16", [16, 2], F32)
        B["p11"] = p.psum("rt_p11", [1, 2], F32)
        B["ptT"] = p.psum("rt_ptT", [128, 16], F32)
    else:
        B.update(psums)
    return B


def emit_const_loads(E, p, B, sem_ld):
    y = E.y
    for nm in ["rt_L16", "rt_ind16", "rt_logk", "rt_na1", "rt_Krow"]:
        y.dma_start(out=B[nm][0:B["d_" + nm].shape[0], :], in_=B["d_" + nm][:, :])
        y.inc(sem_ld, 16)
    y.dma_start(out=B["na2"][0:1, :], in_=B["d_rt_na1"][:, :])
    y.inc(sem_ld, 16)


def emit(E, p, B, s_all_dram, s_own_dram, onesr, ones128, ident, sems):
    v, s, t, g, y = E.v, E.s, E.t, E.g, E.y
    ld2 = sems["ld2"]

    y.dma_start(out=B["s2"][:, 0:128], in_=s_all_dram[0, :].rearrange("(p f) -> p f", p=128))
    y.inc(ld2, 16)
    y.dma_start(out=B["s2"][:, 128:256], in_=s_all_dram[1, :].rearrange("(p f) -> p f", p=128))
    y.inc(ld2, 16)
    y.dma_start(out=B["so16"][0:16, 0:128], in_=s_own_dram[0, :].rearrange("(a f) -> a f", a=16))
    y.inc(ld2, 16)
    y.dma_start(out=B["so16"][0:16, 128:256], in_=s_own_dram[1, :].rearrange("(a f) -> a f", a=16))
    y.inc(ld2, 16)

    t.wait_cur(sems["cl"])
    t.matmul(B["pb"][:, :], onesr[0:1, 0:128], B["na2"][0:1, 0:2], start=True, stop=True)
    t.inc(sems["pv"], 1)
    v.wait_cur(ld2)
    v.wait_cur(sems["pv"])
    v.tensor_scalar(out=B["na2"][:, :], in0=B["pb"][:, :], scalar1=1.0, scalar2=None, op0=ALU.mult)
    v.barrier()

    for it in range(N_ITER_DEV):
        v.tensor_scalar(out=B["nC2"][:, :], in0=B["na2"][:, :], scalar1=-1.0, scalar2=None, op0=ALU.mult)
        v.tensor_scalar(out=B["t2"][:, 0:128], in0=B["s2"][:, 0:128], scalar1=B["na2"][:, 0:1], scalar2=None, op0=ALU.min)
        v.tensor_scalar(out=B["t2"][:, 128:256], in0=B["s2"][:, 128:256], scalar1=B["na2"][:, 1:2], scalar2=None, op0=ALU.min)
        v.barrier()
        v.sinc(sems["vs"], 1)
        s.wait(sems["vs"], sems["vs"].final)
        s.activation(out=B["e2"][:, 0:128], in_=B["t2"][:, 0:128], func=AF.Exp, bias=B["nC2"][:, 0:1], scale=1.0, accum_out=B["rows2"][:, 0:1])
        s.activation(out=B["e2"][:, 128:256], in_=B["t2"][:, 128:256], func=AF.Exp, bias=B["nC2"][:, 1:2], scale=1.0, accum_out=B["rows2"][:, 1:2])
        s.inc(sems["sp"], 1)
        t.wait(sems["sp"], sems["sp"].final)
        t.wait(sems["sv"], sems["sv"].final)
        t.matmul(B["pr12"][:, :], ones128[:, 0:1], B["rows2"][:, 0:2], start=True, stop=True)
        t.inc(sems["ps"], 1)
        s.wait(sems["ps"], sems["ps"].final)
        s.activation(out=B["lnS"][0:1, :], in_=B["pr12"][0:1, :], func=AF.Ln, bias=0.0, scale=1.0)
        s.inc(sems["sv"], 1)
        v.wait(sems["sv"], sems["sv"].final)
        v.tensor_tensor(out=B["nan2"][0:1, :], in0=B["lnS"][0:1, :], in1=B["na2"][0:1, 0:2], op=ALU.add)
        v.barrier()
        v.tensor_tensor(out=B["nan2"][0:1, :], in0=B["nan2"][0:1, :], in1=B["rt_logk"][0:1, :], op=ALU.subtract)
        v.inc(sems["vp"], 1)
        t.wait(sems["vp"], sems["vp"].final)
        t.matmul(B["pb"][:, :], onesr[0:1, 0:128], B["nan2"][0:1, 0:2], start=True, stop=True)
        t.inc(sems["pv2"], 1)
        v.wait(sems["pv2"], sems["pv2"].final)
        v.tensor_scalar(out=B["na2"][:, :], in0=B["pb"][:, :], scalar1=1.0, scalar2=None, op0=ALU.mult)
        v.barrier()

    for r in range(2):
        c0, c1 = 128 * r, 128 * (r + 1)
        v.tensor_scalar(out=B["satg"][:, c0:c1], in0=B["s2"][:, c0:c1], scalar1=B["na2"][:, r:r+1], scalar2=None, op0=ALU.subtract)
        v.barrier()
        v.tensor_scalar(out=B["satg"][:, c0:c1], in0=B["satg"][:, c0:c1], scalar1=0.0, scalar2=None, op0=ALU.is_ge)
        v.barrier()
        v.reduce_sum(out=B["rowsg"][:, r:r+1], in_=B["satg"][:, c0:c1], axis=AX.X)
        v.tensor_scalar(out=B["uo16"][0:16, c0:c1], in0=B["so16"][0:16, c0:c1], scalar1=B["na2"][0:16, r:r+1], scalar2=None, op0=ALU.subtract)
        v.barrier()
        v.tensor_scalar(out=B["sato"][0:16, c0:c1], in0=B["uo16"][0:16, c0:c1], scalar1=0.0, scalar2=None, op0=ALU.is_ge)
        v.memset(B["zer16"][0:16, c0:c1], 0.0)
        v.barrier()
        v.tensor_tensor_scan(out=B["incl"][0:16, c0:c1], data0=B["sato"][0:16, c0:c1], data1=B["zer16"][0:16, c0:c1], initial=0.0, op0=ALU.add, op1=ALU.add)
        v.barrier()
        v.tensor_scalar(out=B["rowtot"][0:16, r:r+1], in0=B["incl"][0:16, c1-1:c1], scalar1=1.0, scalar2=None, op0=ALU.mult)
        v.barrier()
    v.sinc(sems["vs"], 1)
    t.wait(sems["vs"], sems["vs"].final)
    t.matmul(B["p16"][:, :], B["rt_L16"][0:16, 0:16], B["rowtot"][0:16, 0:2], start=True, stop=True)
    t.inc(sems["pv"], 1)
    v.wait(sems["pv"], sems["pv"].final)
    v.tensor_scalar(out=B["offsL"][0:16, :], in0=B["p16"][0:16, :], scalar1=1.0, scalar2=None, op0=ALU.mult)
    v.inc(sems["vp"], 1)
    t.wait(sems["vp"], sems["vp"].final)
    t.matmul(B["p16"][:, :], B["rt_ind16"][:, 0:16], B["rowsg"][:, 0:2], start=True, stop=True)
    t.inc(sems["pv"], 1)
    v.wait(sems["pv"], sems["pv"].final)
    v.tensor_scalar(out=B["og16"][0:16, :], in0=B["p16"][0:16, :], scalar1=1.0, scalar2=None, op0=ALU.mult)
    v.inc(sems["vp"], 1)
    t.wait(sems["vp"], sems["vp"].final)
    t.matmul(B["p11"][:, :], ones128[0:16, 0:1], B["rowtot"][0:16, 0:2], start=True, stop=True)
    t.inc(sems["pv"], 1)
    v.wait(sems["pv"], sems["pv"].final)
    v.tensor_scalar(out=B["tot"][0:1, :], in0=B["p11"][0:1, :], scalar1=1.0, scalar2=None, op0=ALU.mult)
    v.barrier()
    v.tensor_tensor(out=B["cnt"][0:1, :], in0=B["rt_Krow"][0:1, :], in1=B["og16"][0:1, :], op=ALU.subtract)
    v.barrier()
    v.tensor_scalar(out=B["cnt"][0:1, :], in0=B["cnt"][0:1, :], scalar1=0.0, scalar2=None, op0=ALU.max)
    v.barrier()
    v.tensor_tensor(out=B["cnt"][0:1, :], in0=B["cnt"][0:1, :], in1=B["tot"][0:1, :], op=ALU.min)
    v.inc(sems["vp"], 1)
    t.wait(sems["vp"], sems["vp"].final)
    t.matmul(B["pb"][:, :], onesr[0:1, 0:128], B["og16"][0:1, 0:2], start=True, stop=True)
    t.inc(sems["pv"], 1)
    v.wait(sems["pv"], sems["pv"].final)
    v.tensor_scalar(out=B["ogb"][:, :], in0=B["pb"][:, :], scalar1=1.0, scalar2=None, op0=ALU.mult)
    v.inc(sems["vp"], 1)
    t.wait(sems["vp"], sems["vp"].final)
    t.matmul(B["pb"][:, :], onesr[0:1, 0:128], B["cnt"][0:1, 0:2], start=True, stop=True)
    t.inc(sems["pv"], 1)
    v.wait(sems["pv"], sems["pv"].final)
    v.tensor_scalar(out=B["cntb"][:, :], in0=B["pb"][:, :], scalar1=1.0, scalar2=None, op0=ALU.mult)
    v.barrier()
    for r, (K, rpT) in enumerate([(NQ, "rpT_q"), (NKV, "rpT_kv")]):
        c0, c1 = 128 * r, 128 * (r + 1)
        v.tensor_tensor(out=B["lrank"][0:16, c0:c1], in0=B["incl"][0:16, c0:c1], in1=B["sato"][0:16, c0:c1], op=ALU.subtract)
        v.barrier()
        v.tensor_scalar(out=B["lrank"][0:16, c0:c1], in0=B["lrank"][0:16, c0:c1], scalar1=B["offsL"][0:16, r:r+1], scalar2=None, op0=ALU.add)
        v.barrier()
        v.tensor_scalar(out=B["sel"][0:16, c0:c1], in0=B["lrank"][0:16, c0:c1], scalar1=B["og16"][0:16, r:r+1], scalar2=float(K), op0=ALU.add, op1=ALU.is_lt)
        v.barrier()
        v.tensor_tensor(out=B["sel"][0:16, c0:c1], in0=B["sel"][0:16, c0:c1], in1=B["sato"][0:16, c0:c1], op=ALU.mult)
        v.barrier()
        v.tensor_scalar(out=B["rankp"][0:16, c0:c1], in0=B["lrank"][0:16, c0:c1], scalar1=1.0, scalar2=None, op0=ALU.add)
        v.barrier()
        v.tensor_tensor(out=B["rankp"][0:16, c0:c1], in0=B["rankp"][0:16, c0:c1], in1=B["sel"][0:16, c0:c1], op=ALU.mult)
        v.barrier()
        v.tensor_scalar(out=B["rankp"][0:16, c0:c1], in0=B["rankp"][0:16, c0:c1], scalar1=-1.0, scalar2=None, op0=ALU.add)
        v.inc(sems["vp"], 1)
        t.wait(sems["vp"], sems["vp"].final)
        t.wait(sems["pv"], sems["pv"].final)
        t.transpose(B["ptT"][0:128, 0:16], B["rankp"][0:16, c0:c1], ident[0:16, 0:16])
        t.inc(sems["ps"], 1)
        v.wait(sems["ps"], sems["ps"].final)
        v.tensor_scalar(out=B[rpT][:, 0:16], in0=B["ptT"][0:128, 0:16], scalar1=1.0, scalar2=None, op0=ALU.mult)
        v.barrier()
        v.sinc(sems["pv"], 1)
    v.sinc(sems["done"], 1)


"""Megakernel for mk_seq_preln_d1 (NA token mixer + light/heavy routed 'ffn').

Per-core token layout (core cid, t0 = 2048*cid):
  staged x0: lam in [0, 2304) <-> global tokens [t0-76, t0+2228), zero-padded OOB.
  x2 grid:   lam in [12, 2188) = 17 tiles of 128 (NA queries, proj rows, light grid).
  own final: lam in [76, 2124).
Block A runs in two column halves to bound SBUF. x2 and x2+light+null go via DRAM.
"""

NT0 = 18
NT = 17
LAM0 = 12
D = 1024
NAB = -1e9


def na_windows():
    return [(max(128 * b - 6, LAM0), min(128 * b + 134, LAM0 + 2176)) for b in range(NT0)]


def light_windows():
    return [(max(LAM0 + 128 * b - 64, LAM0), min(LAM0 + 128 * b + 192, LAM0 + 2176)) for b in range(NT)]


def na_bias_host(core_id, rpb):
    t0 = 2048 * core_id
    out = np.full((NT0, 16, 128, 140), NAB, np.float32)
    L = 16384
    kkv = np.arange(128)
    for b, (q0, q1) in enumerate(na_windows()):
        gk = t0 - 76 + 128 * b + kkv
        for qq in range(q1 - q0):
            gq = t0 - 76 + q0 + qq
            if 0 <= gq < L:
                ni = min(max(gq - 6, 0), L - 13)
                valid = (gk >= ni) & (gk <= ni + 12) & (gk >= 0) & (gk < L)
                if valid.any():
                    idx = 12 + gk - gq
                    for h in range(16):
                        out[b, h, valid, qq] = rpb[h, idx[valid]]
            else:
                ks = (q0 + qq) - 128 * b
                if 0 <= ks < 128:
                    out[b, :, ks, qq] = 0.0
    return out


def light_mask_host(core_id):
    t0 = 2048 * core_id
    out = np.full((NT, 128, 256), NAB, np.float32)
    L = 16384
    kkv = np.arange(128)
    for b, (q0, q1) in enumerate(light_windows()):
        gk = t0 - 76 + LAM0 + 128 * b + kkv
        wk = gk // 64
        kval = (gk >= 0) & (gk < L)
        for qq in range(q1 - q0):
            gq = t0 - 76 + q0 + qq
            if 0 <= gq < L:
                valid = kval & (np.abs(wk - gq // 64) <= 1)
                out[b, valid, qq] = 0.0
            else:
                ks = (q0 + qq) - (LAM0 + 128 * b)
                if 0 <= ks < 128:
                    out[b, ks, qq] = 0.0
    return out


def host_inputs(core_id, z):
    t0 = 2048 * core_id
    m = dict(host_consts(core_id))
    x = np.asarray(z["L0_attn_x"]).reshape(16384, 1024)
    st = np.zeros((2304, 1024), np.float32)
    lo, hi = t0 - 76, t0 + 2228
    slo, shi = max(lo, 0), min(hi, 16384)
    st[slo - lo:shi - lo] = x[slo:shi]
    m["x0"] = st
    m["ident"] = np.eye(128, dtype=np.float32)
    m["w_qkv"] = np.ascontiguousarray(z["L0_attn_w_qkv"])
    m["b_qkv"] = np.asarray(z["L0_attn_b_qkv"]).reshape(1, 3072)
    m["w_proj"] = np.ascontiguousarray(z["L0_attn_w_proj"])
    m["b_proj"] = np.asarray(z["L0_attn_b_proj"]).reshape(1, 1024)
    m["ln_g"] = np.asarray(z["L0_ffn_ln_g"]).reshape(1, 1024)
    m["ln_b"] = np.asarray(z["L0_ffn_ln_b"]).reshape(1, 1024)
    m["w_qkv_l"] = np.ascontiguousarray(z["L0_ffn_w_qkv_l"])
    m["w_out_l"] = np.ascontiguousarray(z["L0_ffn_w_out_l"])
    m["null_q"] = np.asarray(z["L0_ffn_null_q"]).reshape(1, 1024)
    m["rt2"] = np.ascontiguousarray(np.stack([np.asarray(z["L0_ffn_rt_q"]).reshape(1024),
                                              np.asarray(z["L0_ffn_rt_kv"]).reshape(1024)], 0))
    m["rms_g"] = np.asarray(z["L0_ffn_rms_g"]).reshape(1, 1024)
    m["w_q_h"] = np.ascontiguousarray(z["L0_ffn_w_q_h"])
    m["w_kv_h"] = np.ascontiguousarray(np.asarray(z["L0_ffn_w_kv_h"]).reshape(1024, 1024))
    m["w_out_h"] = np.ascontiguousarray(z["L0_ffn_w_out_h"])
    nk = np.asarray(z["L0_ffn_null_kv"])
    nvx = np.zeros((1, 520), np.float32)
    for h in range(8):
        nvx[0, 65 * h + 64] = 1.0
    if core_id == 0:
        m["nullk"] = np.ascontiguousarray(nk[0].T)
        for h in range(8):
            nvx[0, 65 * h:65 * h + 64] = nk[1, h]
        m["nullmask"] = np.zeros((1, 1), np.float32)
    else:
        m["nullk"] = np.zeros((64, 8), np.float32)
        m["nullmask"] = np.full((1, 1), NAB, np.float32)
    m["nullvx"] = nvx
    m["na_bias"] = na_bias_host(core_id, np.asarray(z["L0_attn_rpb"]))
    m["l_mask"] = light_mask_host(core_id)
    m["iota512"] = np.broadcast_to(np.arange(512, dtype=np.float32), (128, 512)).copy()
    m["iotap"] = (np.arange(128, dtype=np.float32)[:, None] + 128.0 * np.arange(4)[None, :]).astype(np.float32)
    ep = np.zeros((16, 1024), np.float32)
    for c in range(8):
        for pp in range(128):
            ep[2 * c + pp // 64, 128 * c + pp] = 1.0
    m["Epair"] = ep
    ch = np.zeros((128, 31), np.float32)
    ch[:, 15] = 1.0
    m["chones"] = ch
    return m


def build(dbg=False, upto=99):
    p = Prog()
    dr, sb, ps = p.dram, p.sbuf, p.psum
    io = {}
    for nm, shape in [("x0", [2304, 1024]), ("ident", [128, 128]), ("w_qkv", [1024, 3072]),
                      ("b_qkv", [1, 3072]), ("w_proj", [1024, 1024]), ("b_proj", [1, 1024]),
                      ("ln_g", [1, 1024]), ("ln_b", [1, 1024]), ("w_qkv_l", [1024, 1536]),
                      ("w_out_l", [512, 1024]), ("null_q", [1, 1024]), ("rt2", [2, 1024]),
                      ("rms_g", [1, 1024]), ("w_q_h", [1024, 512]), ("w_kv_h", [1024, 1024]),
                      ("w_out_h", [512, 1024]), ("nullk", [64, 8]), ("nullvx", [1, 520]),
                      ("nullmask", [1, 1]), ("na_bias", [NT0, 16, 128, 140]),
                      ("l_mask", [NT, 128, 256]), ("iota512", [128, 512]),
                      ("iotap", [128, 4]), ("Epair", [16, 1024]), ("chones", [128, 31])]:
        io[nm] = dr(nm, shape, F32, kind="ExternalInput")
    io["out"] = dr("out", [2048, 1024], F32, kind="ExternalOutput")
    dbk = dict(kind="ExternalOutput") if dbg else {}
    io["d_x2"] = dr("d_x2", [2176, 1024], F32, **dbk)
    io["d_xl"] = dr("d_xl", [2176, 1024], F32, **dbk)
    if dbg:
        io["d_dump"] = dr("d_dump", [128, 142080], U8, kind="ExternalOutput")
        io["d_dump2"] = dr("d_dump2", [128, 20480], U8, kind="ExternalOutput")
    io["d_sown"] = dr("d_sown", [2, 2048], F32)
    io["d_stile"] = dr("d_stile", [2, 17 * 128], F32)
    io["d_sag"] = dr("d_sag", [8, 2, 2048], F32)
    io["d_sall"] = dr("d_sall", [2, 16384], F32)
    io["d_qown"] = dr("d_qown", [512, QPAD], BF16)
    io["d_qag"] = dr("d_qag", [8, 512, QPAD], BF16)
    io["d_part"] = dr("d_part", [2048, 520], F32)
    io["d_rs"] = dr("d_rs", [256, 520], F32)

    identf = sb("identf", [128, 128], F32)
    onesr = sb("onesr", [1, 128], F32)
    onesrb = sb("onesrb", [1, 128], BF16)
    ones128 = sb("ones128", [128, 1], F32)
    iota512 = sb("iota512_s", [128, 512], F32)
    iotap = sb("iotap_s", [128, 4], F32)
    Epair = sb("Epair_s", [16, 1024], F32)
    nullqb = sb("nullqb", [128, 1024], F32)
    rtrep = sb("rtrep", [128, 2048], F32)
    bcol_qkv = sb("bcol_qkv", [128, 16], F32)
    brow_v = sb("brow_v", [1, 1024], BF16)
    brow_p = sb("brow_p", [1, 1024], BF16)
    grow = sb("grow", [128, 8], F32)
    brow = sb("brow", [128, 8], F32)
    grmsc = sb("grmsc", [128, 8], F32)
    nullk = sb("nullk_s", [64, 8], BF16)
    nullvx = sb("nullvx_s", [1, 520], BF16)
    nullmask = sb("nullmask_s", [1, 1], F32)
    nab2 = sb("nab2", [128, 2240], BF16)
    lmk2 = sb("lmk2", [128, 256], BF16)

    HW = 1176
    AR = sb("AR", [128, 142080], U8)

    def view(off_b, cols, dt):
        szb = cols * (4 if dt == F32 else 2)
        assert off_b % 64 == 0 and off_b + szb <= 142080, (off_b, szb)
        return AR[:, off_b:off_b + szb].bitcast(dt)

    xn1T = view(0, 8 * 2304, BF16)
    qTh = view(36864, 8 * HW, BF16)
    kTh = view(55680, 8 * HW, BF16)
    vxh = view(74496, 10 * 1040, BF16)
    oT = view(95296, 8 * 2176, BF16)
    den = view(130112, 2176, F32)
    xnlT = view(0, 8 * 2176, BF16)
    qlT = view(36864, 4 * 2176, BF16)
    klT = view(55680, 4 * 2176, BF16)
    vlx = view(74496, 17 * 520, BF16)
    olT = view(95296, 4 * 2176, BF16)
    denl = view(112704, 2176, F32)
    rbf = view(0, 16 * 1024, BF16)
    ohq = view(32768, 16 * QPAD, BF16)
    ohkv = view(40960, 16 * KVPAD, BF16)
    cTq = view(57344, 8 * QPAD, BF16)
    cTkv = view(61440, 8 * KVPAD, BF16)
    qTo = view(69632, 4 * QPAD, BF16)
    kvTo = view(71680, 8 * KVPAD, BF16)
    vxo = view(79872, 4 * 520, BF16)
    qTall = view(32768, 8 * 2048, BF16)
    pTh = view(100416, 4 * 2048, BF16)
    pThn = view(116800, 2048, BF16)[0:1, :]
    oHs = view(120896, 2048, F32)[0:65, :]
    partsb = view(129088, 520, F32)
    rsb = view(131200, 2 * 520, F32)
    recH = view(135360, 16, F32)
    oTh = view(135424, 4 * 256, BF16)
    Hrow = view(137472, 2 * 1024, BF16)
    ohqT = view(141568, 2 * 128, BF16)

    wS = sb("wS", [128, 2 * 1024], BF16)
    wR = sb("wR", [128, 8 * 1024], BF16)
    xt2 = sb("xt2", [128, 2 * 1024], F32)
    prb = sb("prb", [128, 256], BF16)
    sq = sb("sq", [128, 1024], F32)
    wSf = sb("wSf", [128, 1024], F32)
    acc1 = sb("acc1", [128, 1], F32)
    acc2 = sb("acc2", [128, 1], F32)
    acc3 = sb("acc3", [128, 1], F32)
    acc4 = sb("acc4", [128, 1], F32)
    sownb = sb("sownb", [128, 2 * NT], F32)
    kvmask = sb("kvmask", [128, 4], F32)

    pk0 = ps("pk0", [128, 512], F32)
    pk = [pk0, pk0]
    pw = ps("pw", [128, 1024], F32)
    pna = ps("pna", [128, 512], F32)
    pav = [ps(f"pav{i}", [65, 512], F32) for i in range(2)]
    ptb = ps("ptb", [128, 256], F32)
    ptbb = ps("ptbb", [128, 128], BF16)

    RB = alloc(p, psums={
        "pr12": pk0[0:1, 0:2], "pb": pk0[0:128, 4:6], "p16": pw[0:16, 0:2],
        "p11": pw[0:1, 512:514], "ptT": pna[0:128, 0:16]})

    identb = sb("identb", [128, 128], BF16)
    bcolq8 = sb("bcolq8", [128, 8], F32)
    chb = sb("chb", [128, 31], BF16)

    sems = {nm: p.sem(nm) for nm in
            ["cl", "clg", "ld2", "vs", "sp", "ps", "sv", "vp", "pv", "pv2", "done", "dmp",
             "cv", "wcv", "wlf", "nbc",
             "wlr", "qld", "stl", "sta", "stb",
             "xld", "wld", "vt", "tv", "ts", "st2", "sy", "yv", "vy", "ty", "yt", "cc", "ob", "gl"]}
    sm = sems
    wins = na_windows()
    lwins = light_windows()

    @p.body
    def _(E):
        v, s, t, g, y = E.v, E.s, E.t, E.g, E.y

        if dbg:
            ARf = AR.bitcast(F32)
            for ch in range(5):
                c0 = 7104 * ch
                v.memset(ARf[:, c0:min(c0 + 7104, 35520)], 0.0)
            v.memset(wS[:, :], 0.0)
            v.memset(wR[:, :], 0.0)
            v.memset(wSf[:, :], 0.0)
            v.barrier()
            v.sinc(sm["dmp"], 1)

        def dump():
            if not dbg:
                return
            for c_ in [v, s, t]:
                c_.barrier()
                c_.sinc(sm["dmp"], 1)
            y.wait(sm["dmp"], sm["dmp"].final)
            for s_ in p.sems:
                if hasattr(s_, "h") and s_.final > 0 and s_.name not in ("boot", "fin"):
                    y.wait(s_, s_.final)
            y.dma_start(out=io["d_dump"][:, :], in_=AR[:, :])
            y.inc(sm["dmp"], 16)
            y.dma_start(out=io["d_dump2"][:, 0:4096].bitcast(BF16), in_=wS[:, :])
            y.inc(sm["dmp"], 16)
            y.dma_start(out=io["d_dump2"][:, 4096:20480].bitcast(BF16), in_=wR[:, :])
            y.inc(sm["dmp"], 16)
            y.wait_cur(sm["dmp"])

        def stage_waits():
            for nm_ in ("wlf", "wlr", "wld", "wcv", "yv", "nbc", "vs", "sv"):
                y.wait_cur(sm[nm_])

        def wr_load(chunks):
            for (col, wdt, srcap) in chunks:
                stage_waits()
                y.dma_start(out=wSf[:, 0:wdt], in_=srcap)
                y.inc(sm["wlf"], 16)
                s.wait_cur(sm["wlf"])
                s.activation(out=wR[:, col:col + wdt], in_=wSf[:, 0:wdt], func=AF.Identity, bias=0.0, scale=1.0)
                s.inc(sm["wlr"], 1)

        if upto <= -5:
            dump()
            return
        y.wait_cur(sm["dmp"])
        y.dma_start(out=identf[:, :], in_=io["ident"][:, :]); y.inc(sm["cl"], 16)
        y.dma_start(out=iota512[:, :], in_=io["iota512"][:, :]); y.inc(sm["cl"], 16)
        y.dma_start(out=iotap[:, :], in_=io["iotap"][:, :]); y.inc(sm["cl"], 16)
        y.dma_start(out=Epair[0:16, :], in_=io["Epair"][:, :]); y.inc(sm["cl"], 16)
        y.dma_start(out=wSf[0:1, :], in_=io["null_q"][:, :]); y.inc(sm["cl"], 16)
        y.dma_start(out=nullmask[0:1, :], in_=io["nullmask"][:, :]); y.inc(sm["cl"], 16)
        y.dma_start(out=xt2[0:64, 0:8], in_=io["nullk"][:, :]); y.inc(sm["cl"], 16)
        y.dma_start(out=xt2[0:1, 8:528], in_=io["nullvx"][:, :]); y.inc(sm["cl"], 16)
        y.dma_start(out=xt2[0:1, 1024:2048], in_=io["b_proj"][0:1, :]); y.inc(sm["cl"], 16)
        y.dma_start(out=sq[0:1, 0:1024], in_=io["b_qkv"][0:1, 2048:3072]); y.inc(sm["cl"], 16)
        v.memset(onesr[0:1, :], 1.0)
        v.memset(ones128[:, :], 1.0)
        v.memset(onesrb[0:1, :], 1.0)
        v.barrier()
        v.sinc(sm["cl"], 1)
        if upto <= -4.8:
            dump()
            return
        emit_const_loads(E, p, RB, sm["cl"])
        t.wait_cur(sm["cl"])
        v.wait_cur(sm["cl"])
        s.wait_cur(sm["cl"])
        s.activation(out=identb[:, :], in_=identf[:, :], func=AF.Identity, bias=0.0, scale=1.0)
        s.activation(out=nullk[0:64, :], in_=xt2[0:64, 0:8], func=AF.Identity, bias=0.0, scale=1.0)
        s.activation(out=nullvx[0:1, :], in_=xt2[0:1, 8:528], func=AF.Identity, bias=0.0, scale=1.0)
        s.activation(out=brow_p[0:1, :], in_=xt2[0:1, 1024:2048], func=AF.Identity, bias=0.0, scale=1.0)
        s.activation(out=brow_v[0:1, :], in_=sq[0:1, 0:1024], func=AF.Identity, bias=0.0, scale=1.0)
        s.barrier()
        s.sinc(sm["cv"], 1)
        v.tensor_scalar(out=chb[:, :], in0=iota512[:, 0:31], scalar1=15.0, scalar2=None, op0=ALU.is_equal)
        v.barrier()
        v.sinc(sm["cv"], 1)
        for c_ in [v, s, t, y]:
            c_.wait_cur(sm["cv"])

        if upto <= -4.7:
            dump()
            return
        for n in range(2):
            t.matmul(pw[:, 512 * n:512 * n + 512], onesr[0:1, 0:128], wSf[0:1, 512 * n:512 * n + 512], start=True, stop=True)
        t.inc(sm["tv"], 1)
        v.wait_cur(sm["tv"])
        v.tensor_scalar(out=nullqb[:, :], in0=pw[:, :], scalar1=1.0, scalar2=None, op0=ALU.mult)
        v.barrier(); v.sinc(sm["vt"], 1)
        if upto <= -4.3:
            dump()
            return
        for r in range(2):
            y.wait_cur(sm["vt"])
            y.wait_cur(sm["yt"])
            y.dma_start(out=wSf[0:1, :], in_=io["rt2"][r:r + 1, :]); y.inc(sm["yt"], 16)
            t.wait_cur(sm["yt"])
            t.wait_cur(sm["vt"])
            for n in range(2):
                t.matmul(pw[:, 512 * n:512 * n + 512], onesr[0:1, 0:128], wSf[0:1, 512 * n:512 * n + 512], start=True, stop=True)
            t.inc(sm["tv"], 1)
            v.wait_cur(sm["tv"])
            v.tensor_scalar(out=rtrep[:, 1024 * r:1024 * r + 1024], in0=pw[:, :], scalar1=1.0, scalar2=None, op0=ALU.mult)
            v.barrier(); v.sinc(sm["vt"], 1)

        if upto <= -4:
            dump()
            return

        def brow_to_cols(src_ap_row, dst, nchunk):
            for c in range(nchunk):
                y.wait_cur(sm["vt"])
                y.wait_cur(sm["yt"])
                y.dma_start(out=sq[0:1, 0:128], in_=src_ap_row[0:1, 128 * c:128 * c + 128])
                y.inc(sm["yt"], 16)
                t.wait_cur(sm["yt"])
                t.wait_cur(sm["vt"])
                t.transpose(ptb[0:128, 0:1], sq[0:1, 0:128], identf[0:1, 0:1])
                t.inc(sm["tv"], 1)
                v.wait_cur(sm["tv"])
                v.tensor_scalar(out=dst[:, c:c + 1], in0=ptb[0:128, 0:1], scalar1=1.0, scalar2=None, op0=ALU.mult)
                v.barrier(); v.sinc(sm["vt"], 1)
        brow_to_cols(io["b_qkv"], bcol_qkv, 16)
        brow_to_cols(io["ln_g"], grow, 8)
        brow_to_cols(io["ln_b"], brow, 8)
        brow_to_cols(io["rms_g"], grmsc, 8)
        v.tensor_scalar(out=bcolq8[:, 0:8], in0=bcol_qkv[:, 0:8], scalar1=0.125, scalar2=None, op0=ALU.mult)
        v.barrier()
        v.sinc(sm["vt"], 1)
        t.wait_cur(sm["vt"])
        s.e.wait_ge(sm["vt"].h, sm["vt"].final)

        def ln_tile(xt_ap):
            v.reduce_sum(out=acc1[:, :], in_=xt_ap, axis=AX.X)
            v.barrier()
            v.tensor_scalar(out=acc1[:, :], in0=acc1[:, :], scalar1=1.0 / D, scalar2=None, op0=ALU.mult)
            v.inc(sm["vs"], 1)
            s.wait(sm["vs"], sm["vs"].final)
            s.activation(out=sq[:, :], in_=xt_ap, func=AF.Square, bias=0.0, scale=1.0, accum_out=acc2[:, :])
            s.inc(sm["sv"], 1)
            v.wait(sm["sv"], sm["sv"].final)
            v.tensor_tensor(out=acc3[:, :], in0=acc1[:, :], in1=acc1[:, :], op=ALU.mult)
            v.barrier()
            v.scalar_tensor_tensor(out=acc3[:, :], in0=acc2[:, :], scalar=1.0 / D, in1=acc3[:, :], op0=ALU.mult, op1=ALU.subtract)
            v.barrier()
            v.tensor_scalar(out=acc3[:, :], in0=acc3[:, :], scalar1=1e-5, scalar2=None, op0=ALU.add)
            v.inc(sm["vs"], 1)
            s.wait(sm["vs"], sm["vs"].final)
            s.activation(out=acc4[:, :], in_=acc3[:, :], func=AF.Sqrt, bias=0.0, scale=1.0)
            s.inc(sm["sv"], 1)
            v.wait(sm["sv"], sm["sv"].final)
            v.reciprocal(out=acc2[:, :], in_=acc4[:, :])
            v.barrier()
            v.tensor_tensor(out=acc4[:, :], in0=acc1[:, :], in1=acc2[:, :], op=ALU.mult)
            v.barrier()
            v.tensor_scalar(out=acc4[:, :], in0=acc4[:, :], scalar1=-1.0, scalar2=None, op0=ALU.mult)
            v.barrier()
            v.tensor_scalar(out=sq[:, :], in0=xt_ap, scalar1=acc2[:, 0:1], scalar2=acc4[:, 0:1], op0=ALU.mult, op1=ALU.add)
            v.barrier()

        def tp_tile(dst, colbase, colstride, scale_chunks=None):
            v.sinc(sm["vt"], 1)
            t.wait(sm["vt"], sm["vt"].final)
            for c in range(8):
                t.wait(sm["pv"], sm["pv"].final)
                t.transpose(ptb[0:128, 0:128], sq[:, 128 * c:128 * c + 128], identf[:, :])
                t.inc(sm["ps"], 1)
                v.wait(sm["ps"], sm["ps"].final)
                dsl = dst[:, colstride * c + colbase:colstride * c + colbase + 128]
                if scale_chunks is None:
                    v.tensor_scalar(out=dsl, in0=ptb[0:128, 0:128], scalar1=1.0, scalar2=None, op0=ALU.mult)
                else:
                    gcol, bcol2 = scale_chunks
                    v.tensor_scalar(out=dsl, in0=ptb[0:128, 0:128], scalar1=gcol[:, c:c + 1], scalar2=bcol2[:, c:c + 1], op0=ALU.mult, op1=ALU.add)
                v.barrier()
                v.sinc(sm["pv"], 1)

        if upto <= -3:
            dump()
            return
        for j in range(NT0):
            buf = j % 2
            y.wait(sm["vy"], max(0, sm["vy"].final - 1))
            y.wait_cur(sm["xld"])
            y.dma_start(out=xt2[:, 1024 * buf:1024 * buf + 1024], in_=io["x0"][128 * j:128 * j + 128, :])
            y.inc(sm["xld"], 16)
            v.wait_cur(sm["xld"])
            ln_tile(xt2[:, 1024 * buf:1024 * buf + 1024])
            v.sinc(sm["vy"], 1)
            tp_tile(xn1T, 128 * j, 2304)

        if upto <= -2:
            dump()
            return
        wr_load([(512 * k, 512, io["w_qkv"][128 * k:128 * k + 128, 2048:2560]) for k in range(8)]
                + [(4096 + 512 * k, 512, io["w_qkv"][128 * k:128 * k + 128, 2560:3072]) for k in range(8)])

        if upto <= -1.8:
            dump()
            return
        ts_after_m = {}

        def do_half(half):
            stop = False
            if half == 0:
                kcr = list(range(0, 9))
                qc0, qc1, kc0, kc1, vt0, vt1 = 0, 1158, 0, 1152, 0, 9
            else:
                kcr = list(range(9, NT0))
                qc0, qc1, kc0, kc1, vt0, vt1 = 1146, 2188, 1152, 2304, 9, NT0
            qoff, koff = qc0, kc0
            n_mi = 16
            if upto == -1.55:
                n_mi = 1
            elif upto == -1.54:
                n_mi = 8
            elif upto == -1.53:
                n_mi = 6
            elif upto == -1.52:
                n_mi = 5
            for mi in range(n_mi):
                wbuf = mi % 2
                key = (half, mi - 2)
                stage_waits()
                y.dma_start(out=wSf[:, :].rearrange("p (k m) -> p k m", k=8),
                            in_=io["w_qkv"].rearrange("(k p) n -> p k n", p=128)[:, :, 128 * mi:128 * mi + 128])
                y.inc(sm["wld"], 16)
                if key in ts_after_m:
                    s.wait(sm["ts"], ts_after_m[key])
                s.wait_cur(sm["wld"])
                s.activation(out=wS[:, 1024 * wbuf:1024 * wbuf + 1024], in_=wSf[:, :], func=AF.Identity, bias=0.0, scale=1.0)
                s.inc(sm["wcv"], 1)
                isq = mi < 8
                c0, c1 = (qc0, qc1) if isq else (kc0, kc1)
                dst = qTh if isq else kTh
                off = qoff if isq else koff
                hc = mi if isq else mi - 8
                ncols = c1 - c0
                nt = (ncols + 511) // 512
                for n in range(nt):
                    nn0 = c0 + 512 * n
                    w = min(512, c1 - nn0)
                    pp = pk[n % 2]
                    t.wait(sm["wcv"], sm["wcv"].final)
                    t.wait(sm["tv"], sm["tv"].final)
                    t.wait(sm["sp"], sm["sp"].final)
                    for k in range(8):
                        t.matmul(pp[:, 0:w], wS[:, 1024 * wbuf + 128 * k:1024 * wbuf + 128 * k + 128],
                                 xn1T[:, 2304 * k + nn0:2304 * k + nn0 + w], start=(k == 0), stop=(k == 7))
                    t.inc(sm["ts"], 1)
                    dsl = dst[:, HW * hc + (nn0 - off):HW * hc + (nn0 - off) + w]
                    if upto == -1.50 or upto == -1.49:
                        v.wait(sm["ts"], sm["ts"].final)
                        v.tensor_scalar(out=dsl, in0=pp[:, 0:w], scalar1=(0.125 if isq else 1.0),
                                        scalar2=bcolq8[:, hc:hc + 1] if isq else bcol_qkv[:, 8 + hc:8 + hc + 1],
                                        op0=ALU.mult, op1=ALU.add)
                        v.barrier()
                        v.sinc(sm["sp"], 1)
                    else:
                        s.wait(sm["ts"], sm["ts"].final)
                        if isq:
                            s.activation(out=dsl, in_=pp[:, 0:w], func=AF.Identity, bias=bcolq8[:, hc:hc + 1], scale=0.125)
                        else:
                            s.activation(out=dsl, in_=pp[:, 0:w], func=AF.Identity, bias=bcol_qkv[:, 8 + hc:8 + hc + 1], scale=1.0)
                        s.inc(sm["sp"], 1)
                ts_after_m[(half, mi)] = sm["ts"].final
            if upto <= -1.49 and half == 0:
                dump()
                return True
            for j in range(vt0, vt1):
                jl = j - vt0
                t.wait(sm["wlr"], sm["wlr"].final)
                t.wait(sm["sp"], sm["sp"].final)
                t.wait(sm["vt"], sm["vt"].final)
                for nh in range(2):
                    for k in range(8):
                        t.matmul(pw[:, 512 * nh:512 * nh + 512],
                                 xn1T[:, 2304 * k + 128 * j:2304 * k + 128 * j + 128],
                                 wR[:, 4096 * nh + 512 * k:4096 * nh + 512 * k + 512],
                                 start=(k == 0), stop=False)
                    t.matmul(pw[:, 512 * nh:512 * nh + 512], onesrb[0:1, 0:128],
                             brow_v[0:1, 512 * nh:512 * nh + 512], start=False, stop=True)
                t.inc(sm["ts"], 1)
                s.wait(sm["ts"], sm["ts"].final)
                s.copy(out=vxh[:, 1040 * jl:1040 * jl + 1040].rearrange("p (h i) -> p h i", h=16)[:, :, 0:64],
                       in_=pw[:, 0:1024])
                s.inc(sm["sp"], 1)
            if upto <= -1.4 and half == 0:
                dump()
                return True
            v.wait(sm["sp"], sm["sp"].final)
            for jl in range(vt1 - vt0):
                v.memset(vxh[:, 1040 * jl:1040 * jl + 1040].rearrange("p (h i) -> p h i", h=16)[:, :, 64:65], 1.0)
            v.barrier()
            v.sinc(sm["vt"], 1)
            t.wait(sm["vt"], sm["vt"].final)
            nwin = 99
            if upto <= -1.3:
                nwin = 0
            elif upto <= -1.2:
                nwin = 1
            if nwin < 99 and half == 0:
                stop = True
            for b in (kcr[:nwin] if half == 0 else kcr):
                nb = 0
                y.wait(sm["vs"], sm["vs"].final)
                for (h0, h1) in ((0, 7), (7, 14), (14, 16)):
                    hw_ = 140 * (h1 - h0)
                    stage_waits()
                    y.dma_start(out=wSf[:, 0:hw_].rearrange("p (h w2) -> p h w2", h=h1 - h0),
                                in_=io["na_bias"][b, h0:h1, :, :].rearrange("h p w2 -> p h w2"))
                    y.inc(sm["yv"], 16)
                    s.wait_cur(sm["yv"])
                    s.activation(out=nab2[:, 140 * h0:140 * h1], in_=wSf[:, 0:hw_], func=AF.Identity, bias=0.0, scale=1.0)
                    s.inc(sm["nbc"], 1)
                q0, q1 = wins[b]
                w = q1 - q0
                ov = 12 if b > 0 else 0
                for h in range(16):
                    hc = h // 2
                    hp = (h % 2) * 64
                    pidx = h % 2
                    pb_ = 0
                    t.wait(sm["sp"], sm["sp"].final)
                    t.matmul(pna[0:128, 0:w], kTh[hp:hp + 64, HW * hc + (128 * b - koff):HW * hc + (128 * b - koff) + 128],
                             qTh[hp:hp + 64, HW * hc + (q0 - qoff):HW * hc + (q0 - qoff) + w], start=True, stop=True)
                    t.inc(sm["tv"], 1)
                    v.wait(sm["tv"], sm["tv"].final)
                    v.wait(sm["nbc"], sm["nbc"].final)
                    v.tensor_tensor(out=pna[0:128, 0:w], in0=pna[0:128, 0:w],
                                    in1=nab2[:, 2240 * nb + 140 * h:2240 * nb + 140 * h + w], op=ALU.add)
                    v.inc(sm["vs"], 1)
                    s.wait(sm["vs"], sm["vs"].final)
                    s.wait(sm["ts"], sm["ts"].final)
                    s.activation(out=prb[:, 0:w], in_=pna[0:128, 0:w], func=AF.Exp, bias=0.0, scale=1.0)
                    s.inc(sm["sp"], 1)
                    t.wait(sm["sp"], sm["sp"].final)
                    t.wait(sm["pv2"], sm["pv2"].final)
                    t.matmul(ptb[0:16, 0:w], chb[:, 15 - h:31 - h], prb[:, 0:w], start=(h == 0), stop=(h == 15))
                    t.matmul(pav[pidx][0:65, 0:w], vxh[:, 1040 * (b - vt0) + 65 * h:1040 * (b - vt0) + 65 * h + 65],
                             prb[:, 0:w], start=True, stop=True)
                    t.inc(sm["ts"], 1)
                    v.wait(sm["ts"], sm["ts"].final)
                    rp = hp
                    cb = q0 - LAM0
                    if ov > 0:
                        v.tensor_tensor(out=oT[rp:rp + 64, 2176 * hc + cb:2176 * hc + cb + ov],
                                        in0=oT[rp:rp + 64, 2176 * hc + cb:2176 * hc + cb + ov],
                                        in1=pav[pidx][0:64, 0:ov], op=ALU.add)
                    v.tensor_scalar(out=oT[rp:rp + 64, 2176 * hc + cb + ov:2176 * hc + cb + w],
                                    in0=pav[pidx][0:64, ov:w], scalar1=1.0, scalar2=None, op0=ALU.mult)
                    v.barrier()
                    v.sinc(sm["pv2"], 1)
                cb = q0 - LAM0
                if ov > 0:
                    v.tensor_tensor(out=den[0:16, cb:cb + ov], in0=den[0:16, cb:cb + ov],
                                    in1=ptb[0:16, 0:ov], op=ALU.add)
                v.tensor_scalar(out=den[0:16, cb + ov:cb + w], in0=ptb[0:16, ov:w], scalar1=1.0, scalar2=None, op0=ALU.mult)
                v.barrier()
                v.sinc(sm["pv2"], 1)
            if stop:
                dump()
                return True

        if do_half(0):
            return
        if upto <= -1:
            dump()
            return
        do_half(1)

        if upto <= 0:
            dump()
            return
        v.wait(sm["pv2"], sm["pv2"].final)
        v.reciprocal(out=den[0:16, :], in_=den[0:16, :])
        v.barrier()
        v.sinc(sm["vt"], 1)
        for c in range(8):
            for n in range(5):
                w = min(512, 2176 - 512 * n)
                t.wait(sm["vt"], sm["vt"].final)
                t.wait(sm["sp"], sm["sp"].final)
                t.wait(sm["vs"], sm["vs"].final)
                t.matmul(pna[0:128, 0:w], Epair[0:16, 128 * c:128 * c + 128], den[0:16, 512 * n:512 * n + w], start=True, stop=True)
                t.inc(sm["tv"], 1)
                v.wait(sm["tv"], sm["tv"].final)
                v.tensor_tensor(out=oT[:, 2176 * c + 512 * n:2176 * c + 512 * n + w],
                                in0=oT[:, 2176 * c + 512 * n:2176 * c + 512 * n + w],
                                in1=pna[0:128, 0:w], op=ALU.mult)
                v.barrier()
                v.sinc(sm["vs"], 1)

        wr_load([(1024 * c, 1024, io["w_proj"][128 * c:128 * c + 128, :]) for c in range(8)])
        for j in range(NT):
            buf = j % 2
            par = sm["sta"] if buf == 0 else sm["stb"]
            y.wait(sm["vy"], max(0, sm["vy"].final - 1))
            y.wait(par, par.final)
            y.wait_cur(sm["xld"])
            y.dma_start(out=xt2[:, 1024 * buf:1024 * buf + 1024], in_=io["x0"][LAM0 + 128 * j:LAM0 + 128 * j + 128, :])
            y.inc(sm["xld"], 16)
            t.wait(sm["wlr"], sm["wlr"].final)
            t.wait(sm["vs"], sm["vs"].final)
            t.wait(sm["vy"], sm["vy"].final)
            for nh in range(2):
                for k in range(8):
                    t.matmul(pw[:, 512 * nh:512 * nh + 512], oT[:, 2176 * k + 128 * j:2176 * k + 128 * j + 128],
                             wR[:, 1024 * k + 512 * nh:1024 * k + 512 * nh + 512], start=(k == 0), stop=False)
                t.matmul(pw[:, 512 * nh:512 * nh + 512], onesrb[0:1, 0:128], brow_p[0:1, 512 * nh:512 * nh + 512], start=False, stop=True)
            t.inc(sm["tv"], 1)
            v.wait(sm["tv"], sm["tv"].final)
            v.wait(sm["xld"], sm["xld"].final)
            v.tensor_tensor(out=xt2[:, 1024 * buf:1024 * buf + 1024], in0=xt2[:, 1024 * buf:1024 * buf + 1024], in1=pw[:, :], op=ALU.add)
            v.barrier()
            v.sinc(sm["vt"], 1)
            v.sinc(sm["vy"], 1)
            y.wait(sm["vy"], sm["vy"].final)
            y.dma_start(out=io["d_x2"][128 * j:128 * j + 128, :], in_=xt2[:, 1024 * buf:1024 * buf + 1024])
            y.inc(par, 16)
        y.wait_cur(sm["sta"])
        y.wait_cur(sm["stb"])
        y.sinc(sm["ob"], 1)

        if upto <= 1:
            dump()
            return
        for j in range(NT):
            buf = j % 2
            y.wait(sm["ob"], sm["ob"].final)
            y.wait(sm["vy"], max(0, sm["vy"].final - 1))
            y.wait_cur(sm["xld"])
            y.dma_start(out=xt2[:, 1024 * buf:1024 * buf + 1024], in_=io["d_x2"][128 * j:128 * j + 128, :])
            y.inc(sm["xld"], 16)
            v.wait(sm["xld"], sm["xld"].final)
            xt = xt2[:, 1024 * buf:1024 * buf + 1024]
            ln_tile(xt)
            v.sinc(sm["vy"], 1)
            for r in range(2):
                v.tensor_tensor(out=wSf[:, :], in0=sq[:, :], in1=rtrep[:, 1024 * r:1024 * r + 1024], op=ALU.mult)
                v.barrier()
                v.reduce_sum(out=sownb[:, NT * r + j:NT * r + j + 1], in_=wSf[:, :], axis=AX.X)
                v.barrier()
            v.reduce_sum(out=acc1[:, :], in_=sq[:, :], axis=AX.X)
            v.barrier()
            v.tensor_scalar(out=acc1[:, :], in0=acc1[:, :], scalar1=1.0 / D, scalar2=None, op0=ALU.mult)
            v.inc(sm["vs"], 1)
            s.wait(sm["vs"], sm["vs"].final)
            s.activation(out=wSf[:, :], in_=sq[:, :], func=AF.Square, bias=0.0, scale=1.0, accum_out=acc2[:, :])
            s.inc(sm["sv"], 1)
            v.wait(sm["sv"], sm["sv"].final)
            v.tensor_tensor(out=acc3[:, :], in0=acc1[:, :], in1=acc1[:, :], op=ALU.mult)
            v.barrier()
            v.scalar_tensor_tensor(out=acc3[:, :], in0=acc2[:, :], scalar=1.0 / D, in1=acc3[:, :], op0=ALU.mult, op1=ALU.subtract)
            v.barrier()
            v.tensor_scalar(out=acc3[:, :], in0=acc3[:, :], scalar1=1e-5, scalar2=None, op0=ALU.add)
            v.inc(sm["vs"], 1)
            s.wait(sm["vs"], sm["vs"].final)
            s.activation(out=acc4[:, :], in_=acc3[:, :], func=AF.Sqrt, bias=0.0, scale=1.0)
            s.inc(sm["sv"], 1)
            v.wait(sm["sv"], sm["sv"].final)
            v.reciprocal(out=acc2[:, :], in_=acc4[:, :])
            v.barrier()
            v.tensor_tensor(out=acc4[:, :], in0=acc1[:, :], in1=acc2[:, :], op=ALU.mult)
            v.barrier()
            v.tensor_scalar(out=acc4[:, :], in0=acc4[:, :], scalar1=-1.0, scalar2=None, op0=ALU.mult)
            v.barrier()
            v.tensor_scalar(out=sq[:, :], in0=sq[:, :], scalar1=acc2[:, 0:1], scalar2=acc4[:, 0:1], op0=ALU.mult, op1=ALU.add)
            v.barrier()
            tp_tile(xnlT, 128 * j, 2176, scale_chunks=(grow, brow))

        v.sinc(sm["vy"], 1)
        y.wait(sm["vy"], sm["vy"].final)
        for r in range(2):
            for j in range(NT):
                y.dma_start(out=io["d_stile"][r, 128 * j:128 * j + 128].rearrange("(pp o) -> pp o", o=1),
                            in_=sownb[:, NT * r + j:NT * r + j + 1])
                y.inc(sm["stl"], 16)
        y.wait_cur(sm["stl"])
        for r in range(2):
            y.dma_start(out=io["d_sown"][r, :].rearrange("(a o) -> a o", o=1) if False else io["d_sown"][r:r+1, :],
                        in_=io["d_stile"][r:r+1, 64:2112])
            y.inc(sm["st2"], 16)
        y.wait_cur(sm["st2"])
        g.e.wait_ge(sm["st2"].h, sm["st2"].final)
        g.collective_compute("AllGather", ALU.bypass, replica_groups=[list(range(8))],
                             ins=[io["d_sown"].opt()], outs=[io["d_sag"].opt()])
        g.inc(sm["cc"], 1)
        y.wait_cur(sm["cc"])
        y.dma_start(out=io["d_sall"].rearrange("r (c j) -> r c j", c=8),
                    in_=io["d_sag"].rearrange("c r j -> r c j"))
        y.inc(sm["st2"], 16)
        y.wait_cur(sm["st2"])
        y.sinc(sm["gl"], 1)
        for c_ in [v, s, t]:
            c_.wait(sm["gl"], sm["gl"].final)
        emit(E, p, RB, io["d_sall"], io["d_sown"], onesr, ones128, identf, sm)
        for c_ in [v, s, t, y]:
            c_.wait(sm["done"], sm["done"].final)

        if upto <= 2:
            dump()
            return
        for mi in range(8):
            wbuf = mi % 2
            key = ("L", mi - 2)
            stage_waits()
            y.dma_start(out=wSf[:, :].rearrange("p (k m) -> p k m", k=8),
                        in_=io["w_qkv_l"].rearrange("(k p) n -> p k n", p=128)[:, :, 128 * mi:128 * mi + 128])
            y.inc(sm["wld"], 16)
            if key in ts_after_m:
                s.wait(sm["ts"], ts_after_m[key])
            s.wait_cur(sm["wld"])
            s.activation(out=wS[:, 1024 * wbuf:1024 * wbuf + 1024], in_=wSf[:, :], func=AF.Identity, bias=0.0, scale=1.0)
            s.inc(sm["wcv"], 1)
            isq = mi < 4
            dst = qlT if isq else klT
            hc = mi % 4
            for n in range(5):
                nn0 = 512 * n
                w = min(512, 2176 - nn0)
                pp = pk[n % 2]
                t.wait(sm["wcv"], sm["wcv"].final)
                t.wait(sm["sp"], sm["sp"].final)
                for k in range(8):
                    t.matmul(pp[:, 0:w], wS[:, 1024 * wbuf + 128 * k:1024 * wbuf + 128 * k + 128],
                             xnlT[:, 2176 * k + nn0:2176 * k + nn0 + w], start=(k == 0), stop=(k == 7))
                t.inc(sm["ts"], 1)
                s.wait(sm["ts"], sm["ts"].final)
                if isq:
                    s.activation(out=dst[:, 2176 * hc + nn0:2176 * hc + nn0 + w], in_=pp[:, 0:w], func=AF.Copy, bias=0.0, scale=0.125)
                else:
                    s.copy(out=dst[:, 2176 * hc + nn0:2176 * hc + nn0 + w], in_=pp[:, 0:w])
                s.inc(sm["sp"], 1)
            ts_after_m[("L", mi)] = sm["ts"].final
        wr_load([(4096 + 512 * k, 512, io["w_qkv_l"][128 * k:128 * k + 128, 1024:1536]) for k in range(8)])
        for j in range(NT):
            t.wait(sm["wlr"], sm["wlr"].final)
            t.wait(sm["sp"], sm["sp"].final)
            for k in range(8):
                t.matmul(pw[:, 0:512], xnlT[:, 2176 * k + 128 * j:2176 * k + 128 * j + 128],
                         wR[:, 4096 + 512 * k:4096 + 512 * k + 512], start=(k == 0), stop=(k == 7))
            t.inc(sm["ts"], 1)
            s.wait(sm["ts"], sm["ts"].final)
            s.copy(out=vlx[:, 520 * j:520 * j + 520].rearrange("p (h i) -> p h i", h=8)[:, :, 0:64], in_=pw[:, 0:512])
            s.inc(sm["sp"], 1)
        v.wait(sm["sp"], sm["sp"].final)
        for j in range(NT):
            v.memset(vlx[:, 520 * j:520 * j + 520].rearrange("p (h i) -> p h i", h=8)[:, :, 64:65], 1.0)
        v.barrier()
        v.sinc(sm["vt"], 1)
        t.wait(sm["vt"], sm["vt"].final)

        if upto <= 3:
            dump()
            return
        for b in range(NT):
            nb = 0
            y.wait(sm["vs"], sm["vs"].final)
            stage_waits()
            y.dma_start(out=wSf[:, 0:256], in_=io["l_mask"][b, :, :])
            y.inc(sm["yv"], 16)
            s.wait_cur(sm["yv"])
            s.activation(out=lmk2[:, 256 * nb:256 * nb + 256], in_=wSf[:, 0:256], func=AF.Identity, bias=0.0, scale=1.0)
            s.inc(sm["nbc"], 1)
            q0, q1 = lwins[b]
            w = q1 - q0
            ov = 128 if b > 0 else 0
            for h in range(8):
                hc = h // 2
                hp = (h % 2) * 64
                pidx = h % 2
                t.wait(sm["sp"], sm["sp"].final)
                t.matmul(pna[0:128, 0:w], klT[hp:hp + 64, 2176 * hc + 128 * b:2176 * hc + 128 * b + 128],
                         qlT[hp:hp + 64, 2176 * hc + (q0 - LAM0):2176 * hc + (q0 - LAM0) + w], start=True, stop=True)
                t.inc(sm["tv"], 1)
                v.wait(sm["tv"], sm["tv"].final)
                v.wait(sm["nbc"], sm["nbc"].final)
                v.tensor_tensor(out=pna[0:128, 0:w], in0=pna[0:128, 0:w], in1=lmk2[:, 256 * nb:256 * nb + w], op=ALU.add)
                v.inc(sm["vs"], 1)
                s.wait(sm["vs"], sm["vs"].final)
                s.wait(sm["ts"], sm["ts"].final)
                s.activation(out=prb[:, 0:w], in_=pna[0:128, 0:w], func=AF.Exp, bias=0.0, scale=1.0)
                s.inc(sm["sp"], 1)
                t.wait(sm["sp"], sm["sp"].final)
                t.wait(sm["pv2"], sm["pv2"].final)
                t.matmul(ptb[0:8, 0:w], chb[:, 15 - h:23 - h], prb[:, 0:w], start=(h == 0), stop=(h == 7))
                t.matmul(pav[pidx][0:65, 0:w], vlx[:, 520 * b + 65 * h:520 * b + 65 * h + 65],
                         prb[:, 0:w], start=True, stop=True)
                t.inc(sm["ts"], 1)
                v.wait(sm["ts"], sm["ts"].final)
                rp = hp
                cb = q0 - LAM0
                if ov > 0:
                    v.tensor_tensor(out=olT[rp:rp + 64, 2176 * hc + cb:2176 * hc + cb + ov],
                                    in0=olT[rp:rp + 64, 2176 * hc + cb:2176 * hc + cb + ov],
                                    in1=pav[pidx][0:64, 0:ov], op=ALU.add)
                v.tensor_scalar(out=olT[rp:rp + 64, 2176 * hc + cb + ov:2176 * hc + cb + w],
                                in0=pav[pidx][0:64, ov:w], scalar1=1.0, scalar2=None, op0=ALU.mult)
                v.barrier()
                v.sinc(sm["pv2"], 1)
            cb = q0 - LAM0
            if ov > 0:
                v.tensor_tensor(out=denl[0:8, cb:cb + ov], in0=denl[0:8, cb:cb + ov],
                                in1=ptb[0:8, 0:ov], op=ALU.add)
            v.tensor_scalar(out=denl[0:8, cb + ov:cb + w], in0=ptb[0:8, ov:w], scalar1=1.0, scalar2=None, op0=ALU.mult)
            v.barrier()
            v.sinc(sm["pv2"], 1)
        v.wait(sm["pv2"], sm["pv2"].final)
        v.reciprocal(out=denl[0:8, :], in_=denl[0:8, :])
        v.barrier()
        v.sinc(sm["vt"], 1)
        for c in range(4):
            for n in range(5):
                w = min(512, 2176 - 512 * n)
                t.wait(sm["vt"], sm["vt"].final)
                t.wait(sm["vs"], sm["vs"].final)
                t.matmul(pna[0:128, 0:w], Epair[0:8, 128 * c:128 * c + 128], denl[0:8, 512 * n:512 * n + w], start=True, stop=True)
                t.inc(sm["tv"], 1)
                v.wait(sm["tv"], sm["tv"].final)
                v.tensor_tensor(out=olT[:, 2176 * c + 512 * n:2176 * c + 512 * n + w],
                                in0=olT[:, 2176 * c + 512 * n:2176 * c + 512 * n + w],
                                in1=pna[0:128, 0:w], op=ALU.mult)
                v.barrier()
                v.sinc(sm["vs"], 1)
        wr_load([(1024 * c, 1024, io["w_out_l"][128 * c:128 * c + 128, :]) for c in range(4)])
        for j in range(NT):
            buf = j % 2
            par = sm["sta"] if buf == 0 else sm["stb"]
            y.wait(sm["vy"], max(0, sm["vy"].final - 1))
            y.wait(par, par.final)
            y.wait_cur(sm["xld"])
            y.dma_start(out=xt2[:, 1024 * buf:1024 * buf + 1024], in_=io["d_x2"][128 * j:128 * j + 128, :])
            y.inc(sm["xld"], 16)
            t.wait(sm["wlr"], sm["wlr"].final)
            t.wait(sm["vs"], sm["vs"].final)
            t.wait(sm["vy"], sm["vy"].final)
            for nh in range(2):
                for k in range(4):
                    t.matmul(pw[:, 512 * nh:512 * nh + 512], olT[:, 2176 * k + 128 * j:2176 * k + 128 * j + 128],
                             wR[:, 1024 * k + 512 * nh:1024 * k + 512 * nh + 512], start=(k == 0), stop=(k == 3))
            t.inc(sm["tv"], 1)
            v.wait(sm["tv"], sm["tv"].final)
            v.wait(sm["xld"], sm["xld"].final)
            v.tensor_tensor(out=xt2[:, 1024 * buf:1024 * buf + 1024], in0=xt2[:, 1024 * buf:1024 * buf + 1024], in1=pw[:, :], op=ALU.add)
            v.barrier()
            v.tensor_tensor(out=xt2[:, 1024 * buf:1024 * buf + 1024], in0=xt2[:, 1024 * buf:1024 * buf + 1024], in1=nullqb[:, :], op=ALU.add)
            v.barrier()
            v.sinc(sm["vy"], 1)
            y.wait(sm["vy"], sm["vy"].final)
            y.dma_start(out=io["d_xl"][128 * j:128 * j + 128, :], in_=xt2[:, 1024 * buf:1024 * buf + 1024])
            y.inc(par, 16)

        if upto <= 4:
            dump()
            return
        v.wait(sm["done"], sm["done"].final)
        for tau in range(16):
            v.tensor_scalar(out=ohq[:, QPAD * tau:QPAD * tau + QPAD], in0=iota512[:, 0:QPAD],
                            scalar1=RB["rpT_q"][:, tau:tau + 1], scalar2=None, op0=ALU.is_equal)
            v.tensor_scalar(out=ohkv[:, KVPAD * tau:KVPAD * tau + KVPAD], in0=iota512[:, 0:KVPAD],
                            scalar1=RB["rpT_kv"][:, tau:tau + 1], scalar2=None, op0=ALU.is_equal)
            v.barrier()
        y.wait_cur(sm["sta"])
        y.wait_cur(sm["stb"])
        for tau in range(16):
            buf = tau % 2
            y.wait(sm["vy"], max(0, sm["vy"].final - 1))
            y.wait_cur(sm["xld"])
            y.dma_start(out=xt2[:, 1024 * buf:1024 * buf + 1024], in_=io["d_x2"][64 + 128 * tau:64 + 128 * tau + 128, :])
            y.inc(sm["xld"], 16)
            v.wait(sm["xld"], sm["xld"].final)
            xt = xt2[:, 1024 * buf:1024 * buf + 1024]
            ln_tile(xt)
            v.tensor_scalar(out=acc1[:, :], in0=acc3[:, :], scalar1=-1e-5, scalar2=None, op0=ALU.add)
            v.barrier()
            v.reciprocal(out=acc2[:, :], in_=acc3[:, :])
            v.barrier()
            v.tensor_tensor(out=acc1[:, :], in0=acc1[:, :], in1=acc2[:, :], op=ALU.mult)
            v.barrier()
            v.tensor_scalar(out=acc1[:, :], in0=acc1[:, :], scalar1=1024.0, scalar2=1e-24, op0=ALU.mult, op1=ALU.max)
            v.inc(sm["vs"], 1)
            s.wait(sm["vs"], sm["vs"].final)
            s.activation(out=acc1[:, :], in_=acc1[:, :], func=AF.Sqrt, bias=0.0, scale=1.0)
            s.inc(sm["sv"], 1)
            v.wait(sm["sv"], sm["sv"].final)
            v.reciprocal(out=acc1[:, :], in_=acc1[:, :])
            v.barrier()
            v.tensor_scalar(out=acc1[:, :], in0=acc1[:, :], scalar1=32.0, scalar2=None, op0=ALU.mult)
            v.barrier()
            v.tensor_scalar(out=xn1T[:, 1024 * tau:1024 * tau + 1024], in0=sq[:, :], scalar1=acc1[:, 0:1], scalar2=None, op0=ALU.mult)
            v.barrier()
            v.sinc(sm["vy"], 1)
        v.sinc(sm["vt"], 1)
        for (oh, cT, PAD) in [(ohq, cTq, QPAD), (ohkv, cTkv, KVPAD)]:
            for dch in range(8):
                t.wait(sm["vt"], sm["vt"].final)
                t.wait(sm["sp"], sm["sp"].final)
                for tau in range(16):
                    t.matmul(pna[0:128, 0:PAD],
                             xn1T[:, 1024 * tau + 128 * dch:1024 * tau + 128 * dch + 128],
                             oh[:, PAD * tau:PAD * tau + PAD], start=(tau == 0), stop=(tau == 15))
                t.inc(sm["ts"], 1)
                s.wait(sm["ts"], sm["ts"].final)
                s.copy(out=cT[:, PAD * dch:PAD * dch + PAD], in_=pna[0:128, 0:PAD])
                s.inc(sm["sp"], 1)
        for (wname, dst, nm_, scaleq, cTsrc, PAD) in [("w_q_h", qTo, 4, True, cTq, QPAD),
                                                      ("w_kv_h", kvTo, 8, False, cTkv, KVPAD)]:
            for mi in range(nm_):
                y.wait(sm["ts"], sm["ts"].final)
                stage_waits()
                y.dma_start(out=wSf[:, :].rearrange("p (k m) -> p k m", k=8),
                            in_=io[wname].rearrange("(k p) n -> p k n", p=128)[:, :, 128 * mi:128 * mi + 128])
                y.inc(sm["wld"], 16)
                v.wait(sm["wld"], sm["wld"].final)
                v.wait(sm["ts"], sm["ts"].final)
                for k in range(8):
                    v.tensor_scalar(out=wS[:, 128 * k:128 * k + 128], in0=wSf[:, 128 * k:128 * k + 128],
                                    scalar1=grmsc[:, k:k + 1], scalar2=None, op0=ALU.mult)
                v.barrier()
                v.sinc(sm["vt"], 1)
                t.wait(sm["vt"], sm["vt"].final)
                t.wait(sm["sp"], sm["sp"].final)
                for k in range(8):
                    t.matmul(pna[0:128, 0:PAD], wS[:, 128 * k:128 * k + 128],
                             cTsrc[:, PAD * k:PAD * k + PAD], start=(k == 0), stop=(k == 7))
                t.inc(sm["ts"], 1)
                s.wait(sm["ts"], sm["ts"].final)
                if scaleq:
                    s.activation(out=dst[:, PAD * mi:PAD * mi + PAD], in_=pna[0:128, 0:PAD], func=AF.Copy, bias=0.0, scale=0.125)
                else:
                    s.copy(out=dst[:, PAD * mi:PAD * mi + PAD], in_=pna[0:128, 0:PAD])
                s.inc(sm["sp"], 1)
        t.wait(sm["sp"], sm["sp"].final)
        for h in range(8):
            for sg in range(4):
                t.wait(sm["pv"], sm["pv"].final)
                t.transpose(ptbb[0:128, 0:64], kvTo[64:128, KVPAD * h + 128 * sg:KVPAD * h + 128 * sg + 128], identb[64:128, 64:128])
                t.inc(sm["ps"], 1)
                v.wait(sm["ps"], sm["ps"].final)
                v.tensor_scalar(out=vxo[:, 520 * sg + 65 * h:520 * sg + 65 * h + 64], in0=ptbb[0:128, 0:64],
                                scalar1=1.0, scalar2=None, op0=ALU.mult)
                v.barrier()
                v.sinc(sm["pv"], 1)
        for sg in range(4):
            v.memset(vxo[:, 520 * sg:520 * sg + 520].rearrange("p (h i) -> p h i", h=8)[:, :, 64:65], 1.0)
        v.barrier()
        v.tensor_scalar(out=kvmask[:, :], in0=iotap[:, :], scalar1=RB["cntb"][:, 1:2], scalar2=-1e9, op0=ALU.is_ge, op1=ALU.mult)
        v.barrier()
        v.sinc(sm["vy"], 1)
        y.wait(sm["vy"], sm["vy"].final)
        y.wait(sm["sp"], sm["sp"].final)
        y.dma_start(out=io["d_qown"].rearrange("(c p) n -> p c n", p=128), in_=qTo[:, :].rearrange("p (c n) -> p c n", c=4))
        y.inc(sm["st2"], 16)
        y.wait_cur(sm["st2"])
        g.e.wait_ge(sm["st2"].h, sm["st2"].final)
        g.collective_compute("AllGather", ALU.bypass, replica_groups=[list(range(8))],
                             ins=[io["d_qown"].opt()], outs=[io["d_qag"].opt()])
        g.inc(sm["cc"], 1)
        y.wait_cur(sm["cc"])
        for h in range(8):
            y.dma_start(out=qTall[0:64, 2048 * h:2048 * h + 2048].rearrange("p (rk l) -> p rk l", rk=8),
                        in_=io["d_qag"][:, 64 * h:64 * h + 64, :].rearrange("rk p l -> p rk l"))
            y.inc(sm["qld"], 16)
        y.wait_cur(sm["qld"])
        y.sinc(sm["gl"], 1)
        for c_ in [v, s, t]:
            c_.wait(sm["gl"], sm["gl"].final)
        for h in range(8):
            hc = h // 2
            hp = (h % 2) * 64
            for sg in range(4):
                for qc in range(4):
                    t.wait(sm["sp"], sm["sp"].final)
                    t.matmul(pna[0:128, 0:512], kvTo[0:64, KVPAD * h + 128 * sg:KVPAD * h + 128 * sg + 128],
                             qTall[0:64, 2048 * h + 512 * qc:2048 * h + 512 * qc + 512],
                             start=True, stop=True)
                    t.inc(sm["tv"], 1)
                    v.wait(sm["tv"], sm["tv"].final)
                    v.tensor_scalar(out=pna[0:128, 0:512], in0=pna[0:128, 0:512], scalar1=kvmask[:, sg:sg + 1], scalar2=None, op0=ALU.add)
                    v.inc(sm["vs"], 1)
                    s.wait(sm["vs"], sm["vs"].final)
                    s.activation(out=pTh[:, 2048 * sg + 512 * qc:2048 * sg + 512 * qc + 512], in_=pna[0:128, 0:512], func=AF.Exp, bias=0.0, scale=1.0)
                    s.inc(sm["sp"], 1)
            for qc in range(4):
                t.wait(sm["sp"], sm["sp"].final)
                t.matmul(pna[0:1, 0:512], nullk[0:64, h:h + 1],
                         qTall[0:64, 2048 * h + 512 * qc:2048 * h + 512 * qc + 512], start=True, stop=True)
                t.inc(sm["tv"], 1)
                v.wait(sm["tv"], sm["tv"].final)
                v.tensor_scalar(out=pna[0:1, 0:512], in0=pna[0:1, 0:512], scalar1=nullmask[0:1, 0:1], scalar2=None, op0=ALU.add)
                v.inc(sm["vs"], 1)
                s.wait(sm["vs"], sm["vs"].final)
                s.activation(out=pThn[0:1, 512 * qc:512 * qc + 512], in_=pna[0:1, 0:512], func=AF.Exp, bias=0.0, scale=1.0)
                s.inc(sm["sp"], 1)
            for qc in range(4):
                pidx = qc % 2
                t.wait(sm["sp"], sm["sp"].final)
                t.wait(sm["pv2"], sm["pv2"].final)
                for sg in range(4):
                    t.matmul(pav[pidx][0:65, 0:512], vxo[:, 520 * sg + 65 * h:520 * sg + 65 * h + 65],
                             pTh[:, 2048 * sg + 512 * qc:2048 * sg + 512 * qc + 512], start=(sg == 0), stop=False)
                t.matmul(pav[pidx][0:65, 0:512], nullvx[0:1, 65 * h:65 * h + 65],
                         pThn[0:1, 512 * qc:512 * qc + 512], start=False, stop=True)
                t.inc(sm["ts"], 1)
                v.wait(sm["ts"], sm["ts"].final)
                v.tensor_scalar(out=oHs[0:65, 512 * qc:512 * qc + 512], in0=pav[pidx][0:65, 0:512], scalar1=1.0, scalar2=None, op0=ALU.mult)
                v.barrier()
                v.sinc(sm["pv2"], 1)
            v.sinc(sm["vt"], 1)
            t.wait(sm["vt"], sm["vt"].final)
            for cc_ in range(16):
                t.wait(sm["pv"], sm["pv"].final)
                t.transpose(ptb[0:128, 0:65], oHs[0:65, 128 * cc_:128 * cc_ + 128], identf[0:65, 0:65])
                t.inc(sm["ps"], 1)
                par = sm["sta"] if cc_ % 2 == 0 else sm["stb"]
                v.wait(sm["ps"], sm["ps"].final)
                v.wait(par, par.final)
                v.tensor_scalar(out=partsb[:, 65 * (cc_ % 2):65 * (cc_ % 2) + 65], in0=ptb[0:128, 0:65],
                                scalar1=1.0, scalar2=None, op0=ALU.mult)
                v.barrier()
                v.sinc(sm["pv"], 1)
                v.sinc(sm["vy"], 1)
                y.wait(sm["vy"], sm["vy"].final)
                y.wait(par, par.final)
                y.dma_start(out=io["d_part"][128 * cc_:128 * cc_ + 128, 65 * h:65 * h + 65],
                            in_=partsb[:, 65 * (cc_ % 2):65 * (cc_ % 2) + 65])
                y.inc(par, 16)
        y.wait_cur(sm["st2"])
        y.wait_cur(sm["sta"])
        y.wait_cur(sm["stb"])
        g.e.wait_ge(sm["st2"].h, sm["st2"].final)
        g.e.wait_ge(sm["sta"].h, sm["sta"].final)
        g.e.wait_ge(sm["stb"].h, sm["stb"].final)
        g.collective_compute("ReduceScatter", ALU.add, replica_groups=[list(range(8))],
                             ins=[io["d_part"].opt()], outs=[io["d_rs"].opt()])
        g.inc(sm["cc"], 1)
        y.wait_cur(sm["cc"])
        y.wait_cur(sm["qld"])
        for rr in range(2):
            y.dma_start(out=rsb[:, 520 * rr:520 * rr + 520], in_=io["d_rs"][128 * rr:128 * rr + 128, :])
            y.inc(sm["qld"], 16)
        y.wait_cur(sm["qld"])
        y.sinc(sm["gl"], 1)
        for c_ in [v, s, t]:
            c_.wait(sm["gl"], sm["gl"].final)
        for rr in range(2):
            for h in range(8):
                v.reciprocal(out=recH[:, 8 * rr + h:8 * rr + h + 1], in_=rsb[:, 520 * rr + 65 * h + 64:520 * rr + 65 * h + 65])
                v.barrier()
                v.tensor_scalar(out=rsb[:, 520 * rr + 65 * h:520 * rr + 65 * h + 64],
                                in0=rsb[:, 520 * rr + 65 * h:520 * rr + 65 * h + 64],
                                scalar1=recH[:, 8 * rr + h:8 * rr + h + 1], scalar2=None, op0=ALU.mult)
                v.barrier()
        v.sinc(sm["vt"], 1)
        t.wait(sm["vt"], sm["vt"].final)
        for rr in range(2):
            for h in range(8):
                t.wait(sm["pv"], sm["pv"].final)
                t.transpose(ptb[0:64, 0:128], rsb[:, 520 * rr + 65 * h:520 * rr + 65 * h + 64], identf[:, :])
                t.inc(sm["ps"], 1)
                v.wait(sm["ps"], sm["ps"].final)
                v.tensor_scalar(out=oTh[(h % 2) * 64:(h % 2) * 64 + 64, 256 * (h // 2) + 128 * rr:256 * (h // 2) + 128 * rr + 128],
                                in0=ptb[0:64, 0:128], scalar1=1.0, scalar2=None, op0=ALU.mult)
                v.barrier()
                v.sinc(sm["pv"], 1)
        wr_load([(1024 * c, 1024, io["w_out_h"][128 * c:128 * c + 128, :]) for c in range(4)])
        v.sinc(sm["vt"], 1)
        for m2 in range(2):
            t.wait(sm["wlr"], sm["wlr"].final)
            t.wait(sm["vt"], sm["vt"].final)
            for nh in range(2):
                for k in range(4):
                    t.matmul(pw[:, 512 * nh:512 * nh + 512], oTh[:, 256 * k + 128 * m2:256 * k + 128 * m2 + 128],
                             wR[:, 1024 * k + 512 * nh:1024 * k + 512 * nh + 512], start=(k == 0), stop=(k == 3))
            t.inc(sm["tv"], 1)
            v.wait(sm["tv"], sm["tv"].final)
            v.tensor_tensor(out=Hrow[:, 1024 * m2:1024 * m2 + 1024], in0=pw[:, :], in1=nullqb[:, :], op=ALU.subtract)
            v.barrier()
            v.sinc(sm["vt"], 1)
        for tau in range(16):
            v.tensor_scalar(out=ohq[:, QPAD * tau:QPAD * tau + QPAD], in0=iota512[:, 0:QPAD],
                            scalar1=RB["rpT_q"][:, tau:tau + 1], scalar2=None, op0=ALU.is_equal)
        v.barrier()
        v.sinc(sm["vt"], 1)
        t.wait(sm["vt"], sm["vt"].final)
        y.wait_cur(sm["sta"])
        y.wait_cur(sm["stb"])
        for tau in range(16):
            buf = tau % 2
            par = sm["sta"] if buf == 0 else sm["stb"]
            y.wait(sm["vy"], max(0, sm["vy"].final - 1))
            y.wait(par, par.final)
            y.wait_cur(sm["xld"])
            y.dma_start(out=xt2[:, 1024 * buf:1024 * buf + 1024], in_=io["d_xl"][64 + 128 * tau:64 + 128 * tau + 128, :])
            y.inc(sm["xld"], 16)
            for sc in range(2):
                t.wait(sm["pv"], sm["pv"].final)
                t.transpose(ptbb[0:128, 0:128], ohq[:, QPAD * tau + 128 * sc:QPAD * tau + 128 * sc + 128], identb[:, :])
                t.inc(sm["ps"], 1)
                v.wait(sm["ps"], sm["ps"].final)
                v.wait(sm["tv"], sm["tv"].final)
                v.tensor_scalar(out=ohqT[:, 128 * sc:128 * sc + 128], in0=ptbb[0:128, 0:128],
                                scalar1=1.0, scalar2=None, op0=ALU.mult)
                v.barrier()
                v.sinc(sm["pv"], 1)
                v.sinc(sm["vs"], 1)
            t.wait(sm["vs"], sm["vs"].final)
            for nh in range(2):
                for sc in range(2):
                    t.matmul(pw[:, 512 * nh:512 * nh + 512], ohqT[:, 128 * sc:128 * sc + 128],
                             Hrow[:, 1024 * sc + 512 * nh:1024 * sc + 512 * nh + 512], start=(sc == 0), stop=(sc == 1))
            t.inc(sm["tv"], 1)
            v.wait(sm["tv"], sm["tv"].final)
            v.wait(sm["xld"], sm["xld"].final)
            v.tensor_tensor(out=xt2[:, 1024 * buf:1024 * buf + 1024], in0=xt2[:, 1024 * buf:1024 * buf + 1024], in1=pw[:, :], op=ALU.add)
            v.barrier()
            v.sinc(sm["vy"], 1)
            v.sinc(sm["vs"], 1)
            y.wait(sm["vy"], sm["vy"].final)
            y.dma_start(out=io["out"][128 * tau:128 * tau + 128, :], in_=xt2[:, 1024 * buf:1024 * buf + 1024])
            y.inc(par, 16)
        y.wait_cur(sm["sta"])
        y.wait_cur(sm["stb"])
        dump()

    return p.finalize()


_NC_CACHE = {}


def kernel(**inputs):
    if "nc" not in _NC_CACHE:
        _NC_CACHE["nc"] = build()
    nc = _NC_CACHE["nc"]
    in_maps = [host_inputs(cid, inputs) for cid in range(8)]
    res = run_bass_kernel_spmd(nc, in_maps, core_ids=list(range(8)))
    out = np.concatenate([np.asarray(res.results[c]["out"]) for c in range(8)], 0)
    return out.reshape(1, 16384, 1024).astype(np.float32)


# version 02
# speedup vs baseline: 1.0188x; 1.0188x over previous
import numpy as np
import contextlib
import concourse.bass as bass
import concourse.mybir as mybir
from concourse.bass_utils import run_bass_kernel_spmd

F32 = mybir.dt.float32
BF16 = mybir.dt.bfloat16
I16 = mybir.dt.int16
I32 = mybir.dt.int32
U32 = mybir.dt.uint32
U8 = mybir.dt.uint8
AF = mybir.ActivationFunctionType
ALU = mybir.AluOpType
AX = mybir.AxisListType
BOOT_MAGIC = 5


class SemT:
    def __init__(self, h, name):
        self.h = h
        self.name = name
        self.final = 0


class EngCtx:
    def __init__(self, eng, tick, name):
        self.e = eng
        self.tick = tick
        self.name = name
        self.last = None

    def __getattr__(self, name):
        fn = getattr(self.e, name)

        def wrap(*a, **k):
            inst = fn(*a, **k)
            self.last = inst
            return inst

        return wrap

    def barrier(self):
        if self.last is None:
            return
        self.last.then_inc(self.tick.h, 1)
        self.tick.final += 1
        self.e.wait_ge(self.tick.h, self.tick.final)
        self.last = None

    def inc(self, sem, n=1):
        if self.last is None:
            self.sinc(sem, n)
            return
        self.last.then_inc(sem.h, n)
        sem.final += n
        self.last = None

    def sinc(self, sem, n=1):
        self.e.sem_inc(sem.h, n)
        sem.final += n

    def wait(self, sem, v):
        self.e.wait_ge(sem.h, v)

    def wait_cur(self, sem):
        self.e.wait_ge(sem.h, sem.final)


class E:
    pass


class Prog:
    def __init__(self):
        self.nc = bass.Bass("TRN2", target_bir_lowering=False, debug=False, dynamic_dma_scratch_size=4096)
        self._es = contextlib.ExitStack()
        self.sems = []
        self._body = None

    def sbuf(self, nm, shape, dt=F32):
        return self._es.enter_context(self.nc.sbuf_tensor(nm, shape, dt))

    def psum(self, nm, shape, dt=F32):
        return self._es.enter_context(self.nc.psum_tensor(nm, shape, dt))

    def sem(self, nm):
        s = self._es.enter_context(self.nc.semaphore(nm))
        st = SemT(s, nm)
        self.sems.append(st)
        return st

    def dram(self, nm, shape, dt=F32, kind=None):
        if kind:
            t = self.nc.dram_tensor(nm, shape, dt, kind=kind)
        else:
            t = self.nc.dram_tensor(nm, shape, dt)
        return t.ap()

    def body(self, f):
        self._body = f
        return f

    def finalize(self):
        nc = self.nc
        boot = self.sem("boot")
        fin = self.sem("fin")
        ticks = {e: self.sem(f"tk_{e}") for e in ["v", "s", "t", "g", "y"]}
        semlo = min(s.h.num for s in self.sems)
        semhi = max(s.h.num for s in self.sems)
        block = self._es.enter_context(nc.Block())

        @block.sync
        def _(v):
            g = nc.gpsimd
            g.sem_clear(range(semlo, semhi + 1))
            g.sem_inc(boot.h, BOOT_MAGIC)
            en = E()
            en.nc = nc
            en.g = EngCtx(nc.gpsimd, ticks["g"], "g")
            en.v = EngCtx(nc.vector, ticks["v"], "v")
            en.s = EngCtx(nc.scalar, ticks["s"], "s")
            en.t = EngCtx(nc.tensor, ticks["t"], "t")
            en.y = EngCtx(nc.sync, ticks["y"], "y")
            for c in [en.v, en.s, en.t, en.y]:
                c.e.wait_op(boot.h, BOOT_MAGIC, "sem-eq")
            self._body(en)
            for c in [en.v, en.s, en.t, en.y]:
                for s_ in self.sems:
                    if s_.final > 0 and s_ is not fin and s_ is not boot:
                        c.e.wait_ge(s_.h, s_.final)
                c.e.sem_inc(fin.h, 1)
            fin.final += 4
            g.wait_ge(fin.h, fin.final)
            for s_ in self.sems:
                if s_.final > 0 and s_ is not fin and s_ is not boot:
                    g.wait_ge(s_.h, s_.final)
        self._es.close()
        return nc




"""Routing: 50-iter coor_descent (iter 1 folded into host fp32 const) + selection.

Gather-free outputs, per route r in {q(K=1024,PAD=256), kv(K=2048,PAD=512)}:
  rpT_q / rpT_kv [128, 16] f32: per own-token local slot (in [0,PAD) if selected,
           else negative/garbage>=PAD) -- partition = token%128, col = token//128.
  ogb  [128, 2] f32: this core's global rank offset o_c per route (broadcast).
  cntb [128, 2] f32: number of valid own slots per route (broadcast).
All cores compute redundantly from s_all [2,16384]; own-range uses s_own [2,2048].
"""

N_TOK = 16384
NQ, NKV = 1024, 2048
QPAD, KVPAD = 256, 512
N_ITER_DEV = 49


def host_consts(core_id):
    L16 = np.zeros((128, 16), np.float32)
    for m in range(16):
        L16[:m, m] = 1.0
    ind16 = np.zeros((128, 16), np.float32)
    ind16[: 16 * core_id, :] = 1.0
    logk = np.log(np.array([1152.0, 2304.0], dtype=np.float32)).astype(np.float32).reshape(1, 2)
    na1 = -(logk - np.float32(np.log(np.float32(16384.0)))).astype(np.float32)
    Krow = np.array([[1024.0, 2048.0]], dtype=np.float32)
    return dict(rt_L16=L16, rt_ind16=ind16, rt_logk=logk, rt_na1=na1, rt_Krow=Krow)


def alloc(p, psums=None):
    B = {}
    for nm, shape in [("rt_L16", [128, 16]), ("rt_ind16", [128, 16]),
                      ("rt_logk", [1, 2]), ("rt_na1", [1, 2]), ("rt_Krow", [1, 2])]:
        B["d_" + nm] = p.dram(nm, shape, F32, kind="ExternalInput")
        B[nm] = p.sbuf(nm + "_s", shape, F32)
    B["s2"] = p.sbuf("rt_s2", [128, 256], F32)
    B["so16"] = p.sbuf("rt_so16", [16, 256], F32)
    B["na2"] = p.sbuf("rt_na2", [128, 2], F32)
    B["nC2"] = p.sbuf("rt_nC2", [128, 2], F32)
    B["t2"] = p.sbuf("rt_t2", [128, 256], F32)
    B["e2"] = p.sbuf("rt_e2", [128, 256], F32)
    B["rows2"] = p.sbuf("rt_rows2", [128, 2], F32)
    B["lnS"] = p.sbuf("rt_lnS", [1, 2], F32)
    B["nan2"] = p.sbuf("rt_nan2", [1, 2], F32)
    B["satg"] = p.sbuf("rt_satg", [128, 256], F32)
    B["rowsg"] = p.sbuf("rt_rowsg", [128, 2], F32)
    B["uo16"] = p.sbuf("rt_uo16", [16, 256], F32)
    B["sato"] = p.sbuf("rt_sato", [16, 256], F32)
    B["incl"] = p.sbuf("rt_incl", [16, 256], F32)
    B["zer16"] = p.sbuf("rt_zer16", [16, 256], F32)
    B["rowtot"] = p.sbuf("rt_rowtot", [16, 2], F32)
    B["offsL"] = p.sbuf("rt_offsL", [16, 2], F32)
    B["og16"] = p.sbuf("rt_og16", [16, 2], F32)
    B["tot"] = p.sbuf("rt_tot", [1, 2], F32)
    B["cnt"] = p.sbuf("rt_cnt", [1, 2], F32)
    B["lrank"] = p.sbuf("rt_lrank", [16, 256], F32)
    B["sel"] = p.sbuf("rt_sel", [16, 256], F32)
    B["rankp"] = p.sbuf("rt_rankp", [16, 256], F32)
    B["rpT_q"] = p.sbuf("rt_rpT_q", [128, 16], F32)
    B["rpT_kv"] = p.sbuf("rt_rpT_kv", [128, 16], F32)
    B["ogb"] = p.sbuf("rt_ogb", [128, 2], F32)
    B["cntb"] = p.sbuf("rt_cntb", [128, 2], F32)
    if psums is None:
        B["pr12"] = p.psum("rt_pr12", [1, 2], F32)
        B["pb"] = p.psum("rt_pb", [128, 2], F32)
        B["p16"] = p.psum("rt_p16", [16, 2], F32)
        B["p11"] = p.psum("rt_p11", [1, 2], F32)
        B["ptT"] = p.psum("rt_ptT", [128, 16], F32)
    else:
        B.update(psums)
    return B


def emit_const_loads(E, p, B, sem_ld):
    y = E.y
    for nm in ["rt_L16", "rt_ind16", "rt_logk", "rt_na1", "rt_Krow"]:
        y.dma_start(out=B[nm][0:B["d_" + nm].shape[0], :], in_=B["d_" + nm][:, :])
        y.inc(sem_ld, 16)
    y.dma_start(out=B["na2"][0:1, :], in_=B["d_rt_na1"][:, :])
    y.inc(sem_ld, 16)


def emit(E, p, B, s_all_dram, s_own_dram, onesr, ones128, ident, sems):
    v, s, t, g, y = E.v, E.s, E.t, E.g, E.y
    ld2 = sems["ld2"]

    y.dma_start(out=B["s2"][:, 0:128], in_=s_all_dram[0, :].rearrange("(p f) -> p f", p=128))
    y.inc(ld2, 16)
    y.dma_start(out=B["s2"][:, 128:256], in_=s_all_dram[1, :].rearrange("(p f) -> p f", p=128))
    y.inc(ld2, 16)
    y.dma_start(out=B["so16"][0:16, 0:128], in_=s_own_dram[0, :].rearrange("(a f) -> a f", a=16))
    y.inc(ld2, 16)
    y.dma_start(out=B["so16"][0:16, 128:256], in_=s_own_dram[1, :].rearrange("(a f) -> a f", a=16))
    y.inc(ld2, 16)

    t.wait_cur(sems["cl"])
    t.matmul(B["pb"][:, :], onesr[0:1, 0:128], B["na2"][0:1, 0:2], start=True, stop=True)
    t.inc(sems["pv"], 1)
    v.wait_cur(ld2)
    v.wait_cur(sems["pv"])
    v.tensor_scalar(out=B["na2"][:, :], in0=B["pb"][:, :], scalar1=1.0, scalar2=None, op0=ALU.mult)
    v.barrier()

    for it in range(N_ITER_DEV):
        v.tensor_scalar(out=B["nC2"][:, :], in0=B["na2"][:, :], scalar1=-1.0, scalar2=None, op0=ALU.mult)
        v.tensor_scalar(out=B["t2"][:, 0:128], in0=B["s2"][:, 0:128], scalar1=B["na2"][:, 0:1], scalar2=None, op0=ALU.min)
        v.tensor_scalar(out=B["t2"][:, 128:256], in0=B["s2"][:, 128:256], scalar1=B["na2"][:, 1:2], scalar2=None, op0=ALU.min)
        v.barrier()
        v.sinc(sems["vs"], 1)
        s.wait(sems["vs"], sems["vs"].final)
        s.activation(out=B["e2"][:, 0:128], in_=B["t2"][:, 0:128], func=AF.Exp, bias=B["nC2"][:, 0:1], scale=1.0, accum_out=B["rows2"][:, 0:1])
        s.activation(out=B["e2"][:, 128:256], in_=B["t2"][:, 128:256], func=AF.Exp, bias=B["nC2"][:, 1:2], scale=1.0, accum_out=B["rows2"][:, 1:2])
        s.inc(sems["sp"], 1)
        t.wait(sems["sp"], sems["sp"].final)
        t.wait(sems["sv"], sems["sv"].final)
        t.matmul(B["pr12"][:, :], ones128[:, 0:1], B["rows2"][:, 0:2], start=True, stop=True)
        t.inc(sems["ps"], 1)
        s.wait(sems["ps"], sems["ps"].final)
        s.activation(out=B["lnS"][0:1, :], in_=B["pr12"][0:1, :], func=AF.Ln, bias=0.0, scale=1.0)
        s.inc(sems["sv"], 1)
        v.wait(sems["sv"], sems["sv"].final)
        v.tensor_tensor(out=B["nan2"][0:1, :], in0=B["lnS"][0:1, :], in1=B["na2"][0:1, 0:2], op=ALU.add)
        v.barrier()
        v.tensor_tensor(out=B["nan2"][0:1, :], in0=B["nan2"][0:1, :], in1=B["rt_logk"][0:1, :], op=ALU.subtract)
        v.inc(sems["vp"], 1)
        t.wait(sems["vp"], sems["vp"].final)
        t.matmul(B["pb"][:, :], onesr[0:1, 0:128], B["nan2"][0:1, 0:2], start=True, stop=True)
        t.inc(sems["pv2"], 1)
        v.wait(sems["pv2"], sems["pv2"].final)
        v.tensor_scalar(out=B["na2"][:, :], in0=B["pb"][:, :], scalar1=1.0, scalar2=None, op0=ALU.mult)
        v.barrier()

    for r in range(2):
        c0, c1 = 128 * r, 128 * (r + 1)
        v.tensor_scalar(out=B["satg"][:, c0:c1], in0=B["s2"][:, c0:c1], scalar1=B["na2"][:, r:r+1], scalar2=None, op0=ALU.subtract)
        v.barrier()
        v.tensor_scalar(out=B["satg"][:, c0:c1], in0=B["satg"][:, c0:c1], scalar1=0.0, scalar2=None, op0=ALU.is_ge)
        v.barrier()
        v.reduce_sum(out=B["rowsg"][:, r:r+1], in_=B["satg"][:, c0:c1], axis=AX.X)
        v.tensor_scalar(out=B["uo16"][0:16, c0:c1], in0=B["so16"][0:16, c0:c1], scalar1=B["na2"][0:16, r:r+1], scalar2=None, op0=ALU.subtract)
        v.barrier()
        v.tensor_scalar(out=B["sato"][0:16, c0:c1], in0=B["uo16"][0:16, c0:c1], scalar1=0.0, scalar2=None, op0=ALU.is_ge)
        v.memset(B["zer16"][0:16, c0:c1], 0.0)
        v.barrier()
        v.tensor_tensor_scan(out=B["incl"][0:16, c0:c1], data0=B["sato"][0:16, c0:c1], data1=B["zer16"][0:16, c0:c1], initial=0.0, op0=ALU.add, op1=ALU.add)
        v.barrier()
        v.tensor_scalar(out=B["rowtot"][0:16, r:r+1], in0=B["incl"][0:16, c1-1:c1], scalar1=1.0, scalar2=None, op0=ALU.mult)
        v.barrier()
    v.sinc(sems["vs"], 1)
    t.wait(sems["vs"], sems["vs"].final)
    t.matmul(B["p16"][:, :], B["rt_L16"][0:16, 0:16], B["rowtot"][0:16, 0:2], start=True, stop=True)
    t.inc(sems["pv"], 1)
    v.wait(sems["pv"], sems["pv"].final)
    v.tensor_scalar(out=B["offsL"][0:16, :], in0=B["p16"][0:16, :], scalar1=1.0, scalar2=None, op0=ALU.mult)
    v.inc(sems["vp"], 1)
    t.wait(sems["vp"], sems["vp"].final)
    t.matmul(B["p16"][:, :], B["rt_ind16"][:, 0:16], B["rowsg"][:, 0:2], start=True, stop=True)
    t.inc(sems["pv"], 1)
    v.wait(sems["pv"], sems["pv"].final)
    v.tensor_scalar(out=B["og16"][0:16, :], in0=B["p16"][0:16, :], scalar1=1.0, scalar2=None, op0=ALU.mult)
    v.inc(sems["vp"], 1)
    t.wait(sems["vp"], sems["vp"].final)
    t.matmul(B["p11"][:, :], ones128[0:16, 0:1], B["rowtot"][0:16, 0:2], start=True, stop=True)
    t.inc(sems["pv"], 1)
    v.wait(sems["pv"], sems["pv"].final)
    v.tensor_scalar(out=B["tot"][0:1, :], in0=B["p11"][0:1, :], scalar1=1.0, scalar2=None, op0=ALU.mult)
    v.barrier()
    v.tensor_tensor(out=B["cnt"][0:1, :], in0=B["rt_Krow"][0:1, :], in1=B["og16"][0:1, :], op=ALU.subtract)
    v.barrier()
    v.tensor_scalar(out=B["cnt"][0:1, :], in0=B["cnt"][0:1, :], scalar1=0.0, scalar2=None, op0=ALU.max)
    v.barrier()
    v.tensor_tensor(out=B["cnt"][0:1, :], in0=B["cnt"][0:1, :], in1=B["tot"][0:1, :], op=ALU.min)
    v.inc(sems["vp"], 1)
    t.wait(sems["vp"], sems["vp"].final)
    t.matmul(B["pb"][:, :], onesr[0:1, 0:128], B["og16"][0:1, 0:2], start=True, stop=True)
    t.inc(sems["pv"], 1)
    v.wait(sems["pv"], sems["pv"].final)
    v.tensor_scalar(out=B["ogb"][:, :], in0=B["pb"][:, :], scalar1=1.0, scalar2=None, op0=ALU.mult)
    v.inc(sems["vp"], 1)
    t.wait(sems["vp"], sems["vp"].final)
    t.matmul(B["pb"][:, :], onesr[0:1, 0:128], B["cnt"][0:1, 0:2], start=True, stop=True)
    t.inc(sems["pv"], 1)
    v.wait(sems["pv"], sems["pv"].final)
    v.tensor_scalar(out=B["cntb"][:, :], in0=B["pb"][:, :], scalar1=1.0, scalar2=None, op0=ALU.mult)
    v.barrier()
    for r, (K, rpT) in enumerate([(NQ, "rpT_q"), (NKV, "rpT_kv")]):
        c0, c1 = 128 * r, 128 * (r + 1)
        v.tensor_tensor(out=B["lrank"][0:16, c0:c1], in0=B["incl"][0:16, c0:c1], in1=B["sato"][0:16, c0:c1], op=ALU.subtract)
        v.barrier()
        v.tensor_scalar(out=B["lrank"][0:16, c0:c1], in0=B["lrank"][0:16, c0:c1], scalar1=B["offsL"][0:16, r:r+1], scalar2=None, op0=ALU.add)
        v.barrier()
        v.tensor_scalar(out=B["sel"][0:16, c0:c1], in0=B["lrank"][0:16, c0:c1], scalar1=B["og16"][0:16, r:r+1], scalar2=float(K), op0=ALU.add, op1=ALU.is_lt)
        v.barrier()
        v.tensor_tensor(out=B["sel"][0:16, c0:c1], in0=B["sel"][0:16, c0:c1], in1=B["sato"][0:16, c0:c1], op=ALU.mult)
        v.barrier()
        v.tensor_scalar(out=B["rankp"][0:16, c0:c1], in0=B["lrank"][0:16, c0:c1], scalar1=1.0, scalar2=None, op0=ALU.add)
        v.barrier()
        v.tensor_tensor(out=B["rankp"][0:16, c0:c1], in0=B["rankp"][0:16, c0:c1], in1=B["sel"][0:16, c0:c1], op=ALU.mult)
        v.barrier()
        v.tensor_scalar(out=B["rankp"][0:16, c0:c1], in0=B["rankp"][0:16, c0:c1], scalar1=-1.0, scalar2=None, op0=ALU.add)
        v.inc(sems["vp"], 1)
        t.wait(sems["vp"], sems["vp"].final)
        t.wait(sems["pv"], sems["pv"].final)
        t.transpose(B["ptT"][0:128, 0:16], B["rankp"][0:16, c0:c1], ident[0:16, 0:16])
        t.inc(sems["ps"], 1)
        v.wait(sems["ps"], sems["ps"].final)
        v.tensor_scalar(out=B[rpT][:, 0:16], in0=B["ptT"][0:128, 0:16], scalar1=1.0, scalar2=None, op0=ALU.mult)
        v.barrier()
        v.sinc(sems["pv"], 1)
    v.sinc(sems["done"], 1)


"""Megakernel for mk_seq_preln_d1 (NA token mixer + light/heavy routed 'ffn').

Per-core token layout (core cid, t0 = 2048*cid):
  staged x0: lam in [0, 2304) <-> global tokens [t0-76, t0+2228), zero-padded OOB.
  x2 grid:   lam in [12, 2188) = 17 tiles of 128 (NA queries, proj rows, light grid).
  own final: lam in [76, 2124).
Block A runs in two column halves to bound SBUF. x2 and x2+light+null go via DRAM.
"""

NT0 = 18
NT = 17
LAM0 = 12
D = 1024
NAB = -1e9


def na_windows():
    return [(max(128 * b - 6, LAM0), min(128 * b + 134, LAM0 + 2176)) for b in range(NT0)]


def light_windows():
    return [(max(LAM0 + 128 * b - 64, LAM0), min(LAM0 + 128 * b + 192, LAM0 + 2176)) for b in range(NT)]


def na_bias_host(core_id, rpb):
    t0 = 2048 * core_id
    out = np.full((NT0, 16, 128, 140), NAB, np.float32)
    L = 16384
    kkv = np.arange(128)
    for b, (q0, q1) in enumerate(na_windows()):
        gk = t0 - 76 + 128 * b + kkv
        for qq in range(q1 - q0):
            gq = t0 - 76 + q0 + qq
            if 0 <= gq < L:
                ni = min(max(gq - 6, 0), L - 13)
                valid = (gk >= ni) & (gk <= ni + 12) & (gk >= 0) & (gk < L)
                if valid.any():
                    idx = 12 + gk - gq
                    for h in range(16):
                        out[b, h, valid, qq] = rpb[h, idx[valid]]
            else:
                ks = (q0 + qq) - 128 * b
                if 0 <= ks < 128:
                    out[b, :, ks, qq] = 0.0
    return out


def light_mask_host(core_id):
    t0 = 2048 * core_id
    out = np.full((NT, 128, 256), NAB, np.float32)
    L = 16384
    kkv = np.arange(128)
    for b, (q0, q1) in enumerate(light_windows()):
        gk = t0 - 76 + LAM0 + 128 * b + kkv
        wk = gk // 64
        kval = (gk >= 0) & (gk < L)
        for qq in range(q1 - q0):
            gq = t0 - 76 + q0 + qq
            if 0 <= gq < L:
                valid = kval & (np.abs(wk - gq // 64) <= 1)
                out[b, valid, qq] = 0.0
            else:
                ks = (q0 + qq) - (LAM0 + 128 * b)
                if 0 <= ks < 128:
                    out[b, ks, qq] = 0.0
    return out


def host_inputs(core_id, z):
    t0 = 2048 * core_id
    m = dict(host_consts(core_id))
    x = np.asarray(z["L0_attn_x"]).reshape(16384, 1024)
    st = np.zeros((2304, 1024), np.float32)
    lo, hi = t0 - 76, t0 + 2228
    slo, shi = max(lo, 0), min(hi, 16384)
    st[slo - lo:shi - lo] = x[slo:shi]
    m["x0"] = st
    m["ident"] = np.eye(128, dtype=np.float32)
    m["w_qkv"] = np.ascontiguousarray(z["L0_attn_w_qkv"])
    m["b_qkv"] = np.asarray(z["L0_attn_b_qkv"]).reshape(1, 3072)
    m["w_proj"] = np.ascontiguousarray(z["L0_attn_w_proj"])
    m["b_proj"] = np.asarray(z["L0_attn_b_proj"]).reshape(1, 1024)
    m["ln_g"] = np.asarray(z["L0_ffn_ln_g"]).reshape(1, 1024)
    m["ln_b"] = np.asarray(z["L0_ffn_ln_b"]).reshape(1, 1024)
    m["w_qkv_l"] = np.ascontiguousarray(z["L0_ffn_w_qkv_l"])
    m["w_out_l"] = np.ascontiguousarray(z["L0_ffn_w_out_l"])
    m["null_q"] = np.asarray(z["L0_ffn_null_q"]).reshape(1, 1024)
    m["rt2"] = np.ascontiguousarray(np.stack([np.asarray(z["L0_ffn_rt_q"]).reshape(1024),
                                              np.asarray(z["L0_ffn_rt_kv"]).reshape(1024)], 0))
    m["rms_g"] = np.asarray(z["L0_ffn_rms_g"]).reshape(1, 1024)
    m["w_q_h"] = np.ascontiguousarray(z["L0_ffn_w_q_h"])
    m["w_kv_h"] = np.ascontiguousarray(np.asarray(z["L0_ffn_w_kv_h"]).reshape(1024, 1024))
    m["w_out_h"] = np.ascontiguousarray(z["L0_ffn_w_out_h"])
    nk = np.asarray(z["L0_ffn_null_kv"])
    nvx = np.zeros((1, 520), np.float32)
    for h in range(8):
        nvx[0, 65 * h + 64] = 1.0
    if core_id == 0:
        m["nullk"] = np.ascontiguousarray(nk[0].T)
        for h in range(8):
            nvx[0, 65 * h:65 * h + 64] = nk[1, h]
        m["nullmask"] = np.zeros((1, 1), np.float32)
    else:
        m["nullk"] = np.zeros((64, 8), np.float32)
        m["nullmask"] = np.full((1, 1), NAB, np.float32)
    m["nullvx"] = nvx
    m["na_bias"] = na_bias_host(core_id, np.asarray(z["L0_attn_rpb"]))
    m["l_mask"] = light_mask_host(core_id)
    m["iota512"] = np.broadcast_to(np.arange(512, dtype=np.float32), (128, 512)).copy()
    m["iotap"] = (np.arange(128, dtype=np.float32)[:, None] + 128.0 * np.arange(4)[None, :]).astype(np.float32)
    ep = np.zeros((16, 1024), np.float32)
    for c in range(8):
        for pp in range(128):
            ep[2 * c + pp // 64, 128 * c + pp] = 1.0
    m["Epair"] = ep
    ch = np.zeros((128, 31), np.float32)
    ch[:, 15] = 1.0
    m["chones"] = ch
    return m


def build(dbg=False, upto=99):
    p = Prog()
    dr, sb, ps = p.dram, p.sbuf, p.psum
    io = {}
    for nm, shape in [("x0", [2304, 1024]), ("ident", [128, 128]), ("w_qkv", [1024, 3072]),
                      ("b_qkv", [1, 3072]), ("w_proj", [1024, 1024]), ("b_proj", [1, 1024]),
                      ("ln_g", [1, 1024]), ("ln_b", [1, 1024]), ("w_qkv_l", [1024, 1536]),
                      ("w_out_l", [512, 1024]), ("null_q", [1, 1024]), ("rt2", [2, 1024]),
                      ("rms_g", [1, 1024]), ("w_q_h", [1024, 512]), ("w_kv_h", [1024, 1024]),
                      ("w_out_h", [512, 1024]), ("nullk", [64, 8]), ("nullvx", [1, 520]),
                      ("nullmask", [1, 1]), ("na_bias", [NT0, 16, 128, 140]),
                      ("l_mask", [NT, 128, 256]), ("iota512", [128, 512]),
                      ("iotap", [128, 4]), ("Epair", [16, 1024]), ("chones", [128, 31])]:
        io[nm] = dr(nm, shape, F32, kind="ExternalInput")
    io["out"] = dr("out", [2048, 1024], F32, kind="ExternalOutput")
    dbk = dict(kind="ExternalOutput") if dbg else {}
    io["d_x2"] = dr("d_x2", [2176, 1024], F32, **dbk)
    io["d_xl"] = dr("d_xl", [2176, 1024], F32, **dbk)
    if dbg:
        io["d_dump"] = dr("d_dump", [128, 142080], U8, kind="ExternalOutput")
        io["d_dump2"] = dr("d_dump2", [128, 20480], U8, kind="ExternalOutput")
    io["d_sown"] = dr("d_sown", [2, 2048], F32)
    io["d_stile"] = dr("d_stile", [2, 17 * 128], F32)
    io["d_sag"] = dr("d_sag", [8, 2, 2048], F32)
    io["d_sall"] = dr("d_sall", [2, 16384], F32)
    io["d_qown"] = dr("d_qown", [512, QPAD], BF16)
    io["d_qag"] = dr("d_qag", [8, 512, QPAD], BF16)
    io["d_part"] = dr("d_part", [2048, 520], F32)
    io["d_rs"] = dr("d_rs", [256, 520], F32)

    identf = sb("identf", [128, 128], F32)
    onesr = sb("onesr", [1, 128], F32)
    onesrb = sb("onesrb", [1, 128], BF16)
    ones128 = sb("ones128", [128, 1], F32)
    iota512 = sb("iota512_s", [128, 512], F32)
    iotap = sb("iotap_s", [128, 4], F32)
    Epair = sb("Epair_s", [16, 1024], F32)
    nullqb = sb("nullqb", [128, 1024], F32)
    rtrep = sb("rtrep", [128, 2048], F32)
    bcol_qkv = sb("bcol_qkv", [128, 16], F32)
    brow_v = sb("brow_v", [1, 1024], BF16)
    brow_p = sb("brow_p", [1, 1024], BF16)
    grow = sb("grow", [128, 8], F32)
    brow = sb("brow", [128, 8], F32)
    grmsc = sb("grmsc", [128, 8], F32)
    nullk = sb("nullk_s", [64, 8], BF16)
    nullvx = sb("nullvx_s", [1, 520], BF16)
    nullmask = sb("nullmask_s", [1, 1], F32)
    nab2 = sb("nab2", [128, 2240], BF16)
    lmk2 = sb("lmk2", [128, 256], BF16)

    HW = 1176
    AR = sb("AR", [128, 142080], U8)

    def view(off_b, cols, dt):
        szb = cols * (4 if dt == F32 else 2)
        assert off_b % 64 == 0 and off_b + szb <= 142080, (off_b, szb)
        return AR[:, off_b:off_b + szb].bitcast(dt)

    xn1T = view(0, 8 * 2304, BF16)
    qTh = view(36864, 8 * HW, BF16)
    kTh = view(55680, 8 * HW, BF16)
    vxh = view(74496, 10 * 1040, BF16)
    oT = view(95296, 8 * 2176, BF16)
    den = view(130112, 2176, F32)
    xnlT = view(0, 8 * 2176, BF16)
    qlT = view(36864, 4 * 2176, BF16)
    klT = view(55680, 4 * 2176, BF16)
    vlx = view(74496, 17 * 520, BF16)
    olT = view(95296, 4 * 2176, BF16)
    denl = view(112704, 2176, F32)
    rbf = view(0, 16 * 1024, BF16)
    ohq = view(32768, 16 * QPAD, BF16)
    ohkv = view(40960, 16 * KVPAD, BF16)
    cTq = view(57344, 8 * QPAD, BF16)
    cTkv = view(61440, 8 * KVPAD, BF16)
    qTo = view(69632, 4 * QPAD, BF16)
    kvTo = view(71680, 8 * KVPAD, BF16)
    vxo = view(79872, 4 * 520, BF16)
    qTall = view(32768, 8 * 2048, BF16)
    pTh = view(100416, 4 * 2048, BF16)
    pThn = view(116800, 2048, BF16)[0:1, :]
    oHs = view(120896, 2048, F32)[0:65, :]
    partsb = view(129088, 520, F32)
    rsb = view(131200, 2 * 520, F32)
    recH = view(135360, 16, F32)
    oTh = view(135424, 4 * 256, BF16)
    Hrow = view(137472, 2 * 1024, BF16)
    ohqT = view(141568, 2 * 128, BF16)

    wS = sb("wS", [128, 2 * 1024], BF16)
    wR = sb("wR", [128, 8 * 1024], BF16)
    xt2 = sb("xt2", [128, 2 * 1024], F32)
    prb = sb("prb", [128, 256], BF16)
    sq = sb("sq", [128, 1024], F32)
    wSf = sb("wSf", [128, 1024], F32)
    acc1 = sb("acc1", [128, 1], F32)
    acc2 = sb("acc2", [128, 1], F32)
    acc3 = sb("acc3", [128, 1], F32)
    acc4 = sb("acc4", [128, 1], F32)
    sownb = sb("sownb", [128, 2 * NT], F32)
    kvmask = sb("kvmask", [128, 4], F32)

    pk0 = ps("pk0", [128, 512], F32)
    pk = [pk0, pk0]
    pw = ps("pw", [128, 1024], F32)
    pna = ps("pna", [128, 512], F32)
    pav = [ps(f"pav{i}", [65, 512], F32) for i in range(2)]
    ptb = ps("ptb", [128, 256], F32)
    ptbb = ps("ptbb", [128, 128], BF16)

    RB = alloc(p, psums={
        "pr12": pk0[0:1, 0:2], "pb": pk0[0:128, 4:6], "p16": pw[0:16, 0:2],
        "p11": pw[0:1, 512:514], "ptT": pna[0:128, 0:16]})

    identb = sb("identb", [128, 128], BF16)
    bcolq8 = sb("bcolq8", [128, 8], F32)
    chb = sb("chb", [128, 31], BF16)

    sems = {nm: p.sem(nm) for nm in
            ["cl", "clg", "ld2", "vs", "sp", "ps", "sv", "vp", "pv", "pv2", "done", "dmp",
             "cv", "wcv", "wlf", "nbc",
             "wlr", "qld", "stl", "sta", "stb",
             "xld", "wld", "vt", "tv", "ts", "st2", "sy", "yv", "vy", "ty", "yt", "cc", "ob", "gl"]}
    sm = sems
    wins = na_windows()
    lwins = light_windows()

    @p.body
    def _(E):
        v, s, t, g, y = E.v, E.s, E.t, E.g, E.y

        if dbg:
            ARf = AR.bitcast(F32)
            for ch in range(5):
                c0 = 7104 * ch
                v.memset(ARf[:, c0:min(c0 + 7104, 35520)], 0.0)
            v.memset(wS[:, :], 0.0)
            v.memset(wR[:, :], 0.0)
            v.memset(wSf[:, :], 0.0)
            v.barrier()
            v.sinc(sm["dmp"], 1)

        def dump():
            if not dbg:
                return
            for c_ in [v, s, t]:
                c_.barrier()
                c_.sinc(sm["dmp"], 1)
            y.wait(sm["dmp"], sm["dmp"].final)
            for s_ in p.sems:
                if hasattr(s_, "h") and s_.final > 0 and s_.name not in ("boot", "fin"):
                    y.wait(s_, s_.final)
            y.dma_start(out=io["d_dump"][:, :], in_=AR[:, :])
            y.inc(sm["dmp"], 16)
            y.dma_start(out=io["d_dump2"][:, 0:4096].bitcast(BF16), in_=wS[:, :])
            y.inc(sm["dmp"], 16)
            y.dma_start(out=io["d_dump2"][:, 4096:20480].bitcast(BF16), in_=wR[:, :])
            y.inc(sm["dmp"], 16)
            y.wait_cur(sm["dmp"])

        def stage_waits(full=False):
            nms = ("wlf", "wlr", "wld", "wcv", "yv", "nbc", "vs", "sv") if full else ("wlf", "wlr", "wld", "wcv", "yv", "nbc")
            for nm_ in nms:
                y.wait_cur(sm[nm_])

        def wr_load(chunks):
            for (col, wdt, srcap) in chunks:
                stage_waits()
                y.dma_start(out=wSf[:, 0:wdt], in_=srcap)
                y.inc(sm["wlf"], 16)
                s.wait_cur(sm["wlf"])
                s.activation(out=wR[:, col:col + wdt], in_=wSf[:, 0:wdt], func=AF.Identity, bias=0.0, scale=1.0)
                s.inc(sm["wlr"], 1)

        if upto <= -5:
            dump()
            return
        y.wait_cur(sm["dmp"])
        y.dma_start(out=identf[:, :], in_=io["ident"][:, :]); y.inc(sm["cl"], 16)
        y.dma_start(out=iota512[:, :], in_=io["iota512"][:, :]); y.inc(sm["cl"], 16)
        y.dma_start(out=iotap[:, :], in_=io["iotap"][:, :]); y.inc(sm["cl"], 16)
        y.dma_start(out=Epair[0:16, :], in_=io["Epair"][:, :]); y.inc(sm["cl"], 16)
        y.dma_start(out=wSf[0:1, :], in_=io["null_q"][:, :]); y.inc(sm["cl"], 16)
        y.dma_start(out=nullmask[0:1, :], in_=io["nullmask"][:, :]); y.inc(sm["cl"], 16)
        y.dma_start(out=xt2[0:64, 0:8], in_=io["nullk"][:, :]); y.inc(sm["cl"], 16)
        y.dma_start(out=xt2[0:1, 8:528], in_=io["nullvx"][:, :]); y.inc(sm["cl"], 16)
        y.dma_start(out=xt2[0:1, 1024:2048], in_=io["b_proj"][0:1, :]); y.inc(sm["cl"], 16)
        y.dma_start(out=sq[0:1, 0:1024], in_=io["b_qkv"][0:1, 2048:3072]); y.inc(sm["cl"], 16)
        v.memset(onesr[0:1, :], 1.0)
        v.memset(ones128[:, :], 1.0)
        v.memset(onesrb[0:1, :], 1.0)
        v.barrier()
        v.sinc(sm["cl"], 1)
        if upto <= -4.8:
            dump()
            return
        emit_const_loads(E, p, RB, sm["cl"])
        t.wait_cur(sm["cl"])
        v.wait_cur(sm["cl"])
        s.wait_cur(sm["cl"])
        s.activation(out=identb[:, :], in_=identf[:, :], func=AF.Identity, bias=0.0, scale=1.0)
        s.activation(out=nullk[0:64, :], in_=xt2[0:64, 0:8], func=AF.Identity, bias=0.0, scale=1.0)
        s.activation(out=nullvx[0:1, :], in_=xt2[0:1, 8:528], func=AF.Identity, bias=0.0, scale=1.0)
        s.activation(out=brow_p[0:1, :], in_=xt2[0:1, 1024:2048], func=AF.Identity, bias=0.0, scale=1.0)
        s.activation(out=brow_v[0:1, :], in_=sq[0:1, 0:1024], func=AF.Identity, bias=0.0, scale=1.0)
        s.barrier()
        s.sinc(sm["cv"], 1)
        v.tensor_scalar(out=chb[:, :], in0=iota512[:, 0:31], scalar1=15.0, scalar2=None, op0=ALU.is_equal)
        v.barrier()
        v.sinc(sm["cv"], 1)
        for c_ in [v, s, t, y]:
            c_.wait_cur(sm["cv"])

        if upto <= -4.7:
            dump()
            return
        for n in range(2):
            t.matmul(pw[:, 512 * n:512 * n + 512], onesr[0:1, 0:128], wSf[0:1, 512 * n:512 * n + 512], start=True, stop=True)
        t.inc(sm["tv"], 1)
        v.wait_cur(sm["tv"])
        v.tensor_scalar(out=nullqb[:, :], in0=pw[:, :], scalar1=1.0, scalar2=None, op0=ALU.mult)
        v.barrier(); v.sinc(sm["vt"], 1)
        if upto <= -4.3:
            dump()
            return
        for r in range(2):
            y.wait_cur(sm["vt"])
            y.wait_cur(sm["yt"])
            y.dma_start(out=wSf[0:1, :], in_=io["rt2"][r:r + 1, :]); y.inc(sm["yt"], 16)
            t.wait_cur(sm["yt"])
            t.wait_cur(sm["vt"])
            for n in range(2):
                t.matmul(pw[:, 512 * n:512 * n + 512], onesr[0:1, 0:128], wSf[0:1, 512 * n:512 * n + 512], start=True, stop=True)
            t.inc(sm["tv"], 1)
            v.wait_cur(sm["tv"])
            v.tensor_scalar(out=rtrep[:, 1024 * r:1024 * r + 1024], in0=pw[:, :], scalar1=1.0, scalar2=None, op0=ALU.mult)
            v.barrier(); v.sinc(sm["vt"], 1)

        if upto <= -4:
            dump()
            return

        def brow_to_cols(src_ap_row, dst, nchunk):
            for c in range(nchunk):
                y.wait_cur(sm["vt"])
                y.wait_cur(sm["yt"])
                y.dma_start(out=sq[0:1, 0:128], in_=src_ap_row[0:1, 128 * c:128 * c + 128])
                y.inc(sm["yt"], 16)
                t.wait_cur(sm["yt"])
                t.wait_cur(sm["vt"])
                t.transpose(ptb[0:128, 0:1], sq[0:1, 0:128], identf[0:1, 0:1])
                t.inc(sm["tv"], 1)
                v.wait_cur(sm["tv"])
                v.tensor_scalar(out=dst[:, c:c + 1], in0=ptb[0:128, 0:1], scalar1=1.0, scalar2=None, op0=ALU.mult)
                v.barrier(); v.sinc(sm["vt"], 1)
        brow_to_cols(io["b_qkv"], bcol_qkv, 16)
        brow_to_cols(io["ln_g"], grow, 8)
        brow_to_cols(io["ln_b"], brow, 8)
        brow_to_cols(io["rms_g"], grmsc, 8)
        v.tensor_scalar(out=bcolq8[:, 0:8], in0=bcol_qkv[:, 0:8], scalar1=0.125, scalar2=None, op0=ALU.mult)
        v.barrier()
        v.sinc(sm["vt"], 1)
        t.wait_cur(sm["vt"])
        s.e.wait_ge(sm["vt"].h, sm["vt"].final)

        def ln_tile(xt_ap):
            v.reduce_sum(out=acc1[:, :], in_=xt_ap, axis=AX.X)
            v.barrier()
            v.tensor_scalar(out=acc1[:, :], in0=acc1[:, :], scalar1=1.0 / D, scalar2=None, op0=ALU.mult)
            v.inc(sm["vs"], 1)
            s.wait(sm["vs"], sm["vs"].final)
            s.activation(out=sq[:, :], in_=xt_ap, func=AF.Square, bias=0.0, scale=1.0, accum_out=acc2[:, :])
            s.inc(sm["sv"], 1)
            v.wait(sm["sv"], sm["sv"].final)
            v.tensor_tensor(out=acc3[:, :], in0=acc1[:, :], in1=acc1[:, :], op=ALU.mult)
            v.barrier()
            v.scalar_tensor_tensor(out=acc3[:, :], in0=acc2[:, :], scalar=1.0 / D, in1=acc3[:, :], op0=ALU.mult, op1=ALU.subtract)
            v.barrier()
            v.tensor_scalar(out=acc3[:, :], in0=acc3[:, :], scalar1=1e-5, scalar2=None, op0=ALU.add)
            v.inc(sm["vs"], 1)
            s.wait(sm["vs"], sm["vs"].final)
            s.activation(out=acc4[:, :], in_=acc3[:, :], func=AF.Sqrt, bias=0.0, scale=1.0)
            s.inc(sm["sv"], 1)
            v.wait(sm["sv"], sm["sv"].final)
            v.reciprocal(out=acc2[:, :], in_=acc4[:, :])
            v.barrier()
            v.tensor_tensor(out=acc4[:, :], in0=acc1[:, :], in1=acc2[:, :], op=ALU.mult)
            v.barrier()
            v.tensor_scalar(out=acc4[:, :], in0=acc4[:, :], scalar1=-1.0, scalar2=None, op0=ALU.mult)
            v.barrier()
            v.tensor_scalar(out=sq[:, :], in0=xt_ap, scalar1=acc2[:, 0:1], scalar2=acc4[:, 0:1], op0=ALU.mult, op1=ALU.add)
            v.barrier()

        def tp_tile(dst, colbase, colstride, scale_chunks=None):
            v.sinc(sm["vt"], 1)
            t.wait(sm["vt"], sm["vt"].final)
            for c in range(8):
                t.wait(sm["pv"], sm["pv"].final)
                t.transpose(ptb[0:128, 0:128], sq[:, 128 * c:128 * c + 128], identf[:, :])
                t.inc(sm["ps"], 1)
                v.wait(sm["ps"], sm["ps"].final)
                dsl = dst[:, colstride * c + colbase:colstride * c + colbase + 128]
                if scale_chunks is None:
                    v.tensor_scalar(out=dsl, in0=ptb[0:128, 0:128], scalar1=1.0, scalar2=None, op0=ALU.mult)
                else:
                    gcol, bcol2 = scale_chunks
                    v.tensor_scalar(out=dsl, in0=ptb[0:128, 0:128], scalar1=gcol[:, c:c + 1], scalar2=bcol2[:, c:c + 1], op0=ALU.mult, op1=ALU.add)
                v.barrier()
                v.sinc(sm["pv"], 1)

        if upto <= -3:
            dump()
            return
        for j in range(NT0):
            buf = j % 2
            y.wait(sm["vy"], max(0, sm["vy"].final - 1))
            y.wait_cur(sm["xld"])
            y.dma_start(out=xt2[:, 1024 * buf:1024 * buf + 1024], in_=io["x0"][128 * j:128 * j + 128, :])
            y.inc(sm["xld"], 16)
            v.wait_cur(sm["xld"])
            ln_tile(xt2[:, 1024 * buf:1024 * buf + 1024])
            v.sinc(sm["vy"], 1)
            tp_tile(xn1T, 128 * j, 2304)

        if upto <= -2:
            dump()
            return
        wr_load([(512 * k, 512, io["w_qkv"][128 * k:128 * k + 128, 2048:2560]) for k in range(8)]
                + [(4096 + 512 * k, 512, io["w_qkv"][128 * k:128 * k + 128, 2560:3072]) for k in range(8)])

        if upto <= -1.8:
            dump()
            return
        ts_after_m = {}

        def do_half(half):
            stop = False
            if half == 0:
                kcr = list(range(0, 9))
                qc0, qc1, kc0, kc1, vt0, vt1 = 0, 1158, 0, 1152, 0, 9
            else:
                kcr = list(range(9, NT0))
                qc0, qc1, kc0, kc1, vt0, vt1 = 1146, 2188, 1152, 2304, 9, NT0
            qoff, koff = qc0, kc0
            n_mi = 16
            if upto == -1.55:
                n_mi = 1
            elif upto == -1.54:
                n_mi = 8
            elif upto == -1.53:
                n_mi = 6
            elif upto == -1.52:
                n_mi = 5
            for mi in range(n_mi):
                wbuf = mi % 2
                key = (half, mi - 2)
                stage_waits()
                y.dma_start(out=wSf[:, :].rearrange("p (k m) -> p k m", k=8),
                            in_=io["w_qkv"].rearrange("(k p) n -> p k n", p=128)[:, :, 128 * mi:128 * mi + 128])
                y.inc(sm["wld"], 16)
                if key in ts_after_m:
                    s.wait(sm["ts"], ts_after_m[key])
                s.wait_cur(sm["wld"])
                s.activation(out=wS[:, 1024 * wbuf:1024 * wbuf + 1024], in_=wSf[:, :], func=AF.Identity, bias=0.0, scale=1.0)
                s.inc(sm["wcv"], 1)
                isq = mi < 8
                c0, c1 = (qc0, qc1) if isq else (kc0, kc1)
                dst = qTh if isq else kTh
                off = qoff if isq else koff
                hc = mi if isq else mi - 8
                ncols = c1 - c0
                nt = (ncols + 511) // 512
                for n in range(nt):
                    nn0 = c0 + 512 * n
                    w = min(512, c1 - nn0)
                    pp = pk[n % 2]
                    t.wait(sm["wcv"], sm["wcv"].final)
                    t.wait(sm["tv"], sm["tv"].final)
                    t.wait(sm["sp"], sm["sp"].final)
                    for k in range(8):
                        t.matmul(pp[:, 0:w], wS[:, 1024 * wbuf + 128 * k:1024 * wbuf + 128 * k + 128],
                                 xn1T[:, 2304 * k + nn0:2304 * k + nn0 + w], start=(k == 0), stop=(k == 7))
                    t.inc(sm["ts"], 1)
                    dsl = dst[:, HW * hc + (nn0 - off):HW * hc + (nn0 - off) + w]
                    if upto == -1.50 or upto == -1.49:
                        v.wait(sm["ts"], sm["ts"].final)
                        v.tensor_scalar(out=dsl, in0=pp[:, 0:w], scalar1=(0.125 if isq else 1.0),
                                        scalar2=bcolq8[:, hc:hc + 1] if isq else bcol_qkv[:, 8 + hc:8 + hc + 1],
                                        op0=ALU.mult, op1=ALU.add)
                        v.barrier()
                        v.sinc(sm["sp"], 1)
                    else:
                        s.wait(sm["ts"], sm["ts"].final)
                        if isq:
                            s.activation(out=dsl, in_=pp[:, 0:w], func=AF.Identity, bias=bcolq8[:, hc:hc + 1], scale=0.125)
                        else:
                            s.activation(out=dsl, in_=pp[:, 0:w], func=AF.Identity, bias=bcol_qkv[:, 8 + hc:8 + hc + 1], scale=1.0)
                        s.inc(sm["sp"], 1)
                ts_after_m[(half, mi)] = sm["ts"].final
            if upto <= -1.49 and half == 0:
                dump()
                return True
            for j in range(vt0, vt1):
                jl = j - vt0
                t.wait(sm["wlr"], sm["wlr"].final)
                t.wait(sm["sp"], sm["sp"].final)
                t.wait(sm["vt"], sm["vt"].final)
                for nh in range(2):
                    for k in range(8):
                        t.matmul(pw[:, 512 * nh:512 * nh + 512],
                                 xn1T[:, 2304 * k + 128 * j:2304 * k + 128 * j + 128],
                                 wR[:, 4096 * nh + 512 * k:4096 * nh + 512 * k + 512],
                                 start=(k == 0), stop=False)
                    t.matmul(pw[:, 512 * nh:512 * nh + 512], onesrb[0:1, 0:128],
                             brow_v[0:1, 512 * nh:512 * nh + 512], start=False, stop=True)
                t.inc(sm["ts"], 1)
                s.wait(sm["ts"], sm["ts"].final)
                s.copy(out=vxh[:, 1040 * jl:1040 * jl + 1040].rearrange("p (h i) -> p h i", h=16)[:, :, 0:64],
                       in_=pw[:, 0:1024])
                s.inc(sm["sp"], 1)
            if upto <= -1.4 and half == 0:
                dump()
                return True
            v.wait(sm["sp"], sm["sp"].final)
            for jl in range(vt1 - vt0):
                v.memset(vxh[:, 1040 * jl:1040 * jl + 1040].rearrange("p (h i) -> p h i", h=16)[:, :, 64:65], 1.0)
            v.barrier()
            v.sinc(sm["vt"], 1)
            t.wait(sm["vt"], sm["vt"].final)
            nwin = 99
            if upto <= -1.3:
                nwin = 0
            elif upto <= -1.2:
                nwin = 1
            if nwin < 99 and half == 0:
                stop = True
            for b in (kcr[:nwin] if half == 0 else kcr):
                nb = 0
                y.wait(sm["vs"], sm["vs"].final)
                for (h0, h1) in ((0, 7), (7, 14), (14, 16)):
                    hw_ = 140 * (h1 - h0)
                    stage_waits()
                    y.dma_start(out=wSf[:, 0:hw_].rearrange("p (h w2) -> p h w2", h=h1 - h0),
                                in_=io["na_bias"][b, h0:h1, :, :].rearrange("h p w2 -> p h w2"))
                    y.inc(sm["yv"], 16)
                    s.wait_cur(sm["yv"])
                    s.activation(out=nab2[:, 140 * h0:140 * h1], in_=wSf[:, 0:hw_], func=AF.Identity, bias=0.0, scale=1.0)
                    s.inc(sm["nbc"], 1)
                q0, q1 = wins[b]
                w = q1 - q0
                ov = 12 if b > 0 else 0
                for h in range(16):
                    hc = h // 2
                    hp = (h % 2) * 64
                    pidx = h % 2
                    pb_ = 0
                    t.wait(sm["sp"], sm["sp"].final)
                    t.matmul(pna[0:128, 0:w], kTh[hp:hp + 64, HW * hc + (128 * b - koff):HW * hc + (128 * b - koff) + 128],
                             qTh[hp:hp + 64, HW * hc + (q0 - qoff):HW * hc + (q0 - qoff) + w], start=True, stop=True)
                    t.inc(sm["tv"], 1)
                    v.wait(sm["tv"], sm["tv"].final)
                    v.wait(sm["nbc"], sm["nbc"].final)
                    v.tensor_tensor(out=pna[0:128, 0:w], in0=pna[0:128, 0:w],
                                    in1=nab2[:, 2240 * nb + 140 * h:2240 * nb + 140 * h + w], op=ALU.add)
                    v.inc(sm["vs"], 1)
                    s.wait(sm["vs"], sm["vs"].final)
                    s.wait(sm["ts"], sm["ts"].final)
                    s.activation(out=prb[:, 0:w], in_=pna[0:128, 0:w], func=AF.Exp, bias=0.0, scale=1.0)
                    s.inc(sm["sp"], 1)
                    t.wait(sm["sp"], sm["sp"].final)
                    t.wait(sm["pv2"], sm["pv2"].final)
                    t.matmul(ptb[0:16, 0:w], chb[:, 15 - h:31 - h], prb[:, 0:w], start=(h == 0), stop=(h == 15))
                    t.matmul(pav[pidx][0:65, 0:w], vxh[:, 1040 * (b - vt0) + 65 * h:1040 * (b - vt0) + 65 * h + 65],
                             prb[:, 0:w], start=True, stop=True)
                    t.inc(sm["ts"], 1)
                    v.wait(sm["ts"], sm["ts"].final)
                    rp = hp
                    cb = q0 - LAM0
                    if ov > 0:
                        v.tensor_tensor(out=oT[rp:rp + 64, 2176 * hc + cb:2176 * hc + cb + ov],
                                        in0=oT[rp:rp + 64, 2176 * hc + cb:2176 * hc + cb + ov],
                                        in1=pav[pidx][0:64, 0:ov], op=ALU.add)
                    v.tensor_scalar(out=oT[rp:rp + 64, 2176 * hc + cb + ov:2176 * hc + cb + w],
                                    in0=pav[pidx][0:64, ov:w], scalar1=1.0, scalar2=None, op0=ALU.mult)
                    v.barrier()
                    v.sinc(sm["pv2"], 1)
                cb = q0 - LAM0
                if ov > 0:
                    v.tensor_tensor(out=den[0:16, cb:cb + ov], in0=den[0:16, cb:cb + ov],
                                    in1=ptb[0:16, 0:ov], op=ALU.add)
                v.tensor_scalar(out=den[0:16, cb + ov:cb + w], in0=ptb[0:16, ov:w], scalar1=1.0, scalar2=None, op0=ALU.mult)
                v.barrier()
                v.sinc(sm["pv2"], 1)
            if stop:
                dump()
                return True

        if do_half(0):
            return
        if upto <= -1:
            dump()
            return
        do_half(1)

        if upto <= 0:
            dump()
            return
        v.wait(sm["pv2"], sm["pv2"].final)
        v.reciprocal(out=den[0:16, :], in_=den[0:16, :])
        v.barrier()
        v.sinc(sm["vt"], 1)
        for c in range(8):
            for n in range(5):
                w = min(512, 2176 - 512 * n)
                t.wait(sm["vt"], sm["vt"].final)
                t.wait(sm["sp"], sm["sp"].final)
                t.wait(sm["vs"], sm["vs"].final)
                t.matmul(pna[0:128, 0:w], Epair[0:16, 128 * c:128 * c + 128], den[0:16, 512 * n:512 * n + w], start=True, stop=True)
                t.inc(sm["tv"], 1)
                v.wait(sm["tv"], sm["tv"].final)
                v.tensor_tensor(out=oT[:, 2176 * c + 512 * n:2176 * c + 512 * n + w],
                                in0=oT[:, 2176 * c + 512 * n:2176 * c + 512 * n + w],
                                in1=pna[0:128, 0:w], op=ALU.mult)
                v.barrier()
                v.sinc(sm["vs"], 1)

        wr_load([(1024 * c, 1024, io["w_proj"][128 * c:128 * c + 128, :]) for c in range(8)])
        for j in range(NT):
            buf = j % 2
            par = sm["sta"] if buf == 0 else sm["stb"]
            y.wait(sm["vy"], max(0, sm["vy"].final - 1))
            y.wait(par, par.final)
            y.wait_cur(sm["xld"])
            y.dma_start(out=xt2[:, 1024 * buf:1024 * buf + 1024], in_=io["x0"][LAM0 + 128 * j:LAM0 + 128 * j + 128, :])
            y.inc(sm["xld"], 16)
            t.wait(sm["wlr"], sm["wlr"].final)
            t.wait(sm["vs"], sm["vs"].final)
            t.wait(sm["vy"], sm["vy"].final)
            for nh in range(2):
                for k in range(8):
                    t.matmul(pw[:, 512 * nh:512 * nh + 512], oT[:, 2176 * k + 128 * j:2176 * k + 128 * j + 128],
                             wR[:, 1024 * k + 512 * nh:1024 * k + 512 * nh + 512], start=(k == 0), stop=False)
                t.matmul(pw[:, 512 * nh:512 * nh + 512], onesrb[0:1, 0:128], brow_p[0:1, 512 * nh:512 * nh + 512], start=False, stop=True)
            t.inc(sm["tv"], 1)
            v.wait(sm["tv"], sm["tv"].final)
            v.wait(sm["xld"], sm["xld"].final)
            v.tensor_tensor(out=xt2[:, 1024 * buf:1024 * buf + 1024], in0=xt2[:, 1024 * buf:1024 * buf + 1024], in1=pw[:, :], op=ALU.add)
            v.barrier()
            v.sinc(sm["vt"], 1)
            v.sinc(sm["vy"], 1)
            y.wait(sm["vy"], sm["vy"].final)
            y.dma_start(out=io["d_x2"][128 * j:128 * j + 128, :], in_=xt2[:, 1024 * buf:1024 * buf + 1024])
            y.inc(par, 16)
        y.wait_cur(sm["sta"])
        y.wait_cur(sm["stb"])
        y.sinc(sm["ob"], 1)

        if upto <= 1:
            dump()
            return
        for j in range(NT):
            buf = j % 2
            y.wait(sm["ob"], sm["ob"].final)
            y.wait(sm["vy"], max(0, sm["vy"].final - 1))
            y.wait_cur(sm["xld"])
            y.dma_start(out=xt2[:, 1024 * buf:1024 * buf + 1024], in_=io["d_x2"][128 * j:128 * j + 128, :])
            y.inc(sm["xld"], 16)
            v.wait(sm["xld"], sm["xld"].final)
            xt = xt2[:, 1024 * buf:1024 * buf + 1024]
            ln_tile(xt)
            v.sinc(sm["vy"], 1)
            for r in range(2):
                v.tensor_tensor(out=wSf[:, :], in0=sq[:, :], in1=rtrep[:, 1024 * r:1024 * r + 1024], op=ALU.mult)
                v.barrier()
                v.reduce_sum(out=sownb[:, NT * r + j:NT * r + j + 1], in_=wSf[:, :], axis=AX.X)
                v.barrier()
            v.reduce_sum(out=acc1[:, :], in_=sq[:, :], axis=AX.X)
            v.barrier()
            v.tensor_scalar(out=acc1[:, :], in0=acc1[:, :], scalar1=1.0 / D, scalar2=None, op0=ALU.mult)
            v.inc(sm["vs"], 1)
            s.wait(sm["vs"], sm["vs"].final)
            s.activation(out=wSf[:, :], in_=sq[:, :], func=AF.Square, bias=0.0, scale=1.0, accum_out=acc2[:, :])
            s.inc(sm["sv"], 1)
            v.wait(sm["sv"], sm["sv"].final)
            v.tensor_tensor(out=acc3[:, :], in0=acc1[:, :], in1=acc1[:, :], op=ALU.mult)
            v.barrier()
            v.scalar_tensor_tensor(out=acc3[:, :], in0=acc2[:, :], scalar=1.0 / D, in1=acc3[:, :], op0=ALU.mult, op1=ALU.subtract)
            v.barrier()
            v.tensor_scalar(out=acc3[:, :], in0=acc3[:, :], scalar1=1e-5, scalar2=None, op0=ALU.add)
            v.inc(sm["vs"], 1)
            s.wait(sm["vs"], sm["vs"].final)
            s.activation(out=acc4[:, :], in_=acc3[:, :], func=AF.Sqrt, bias=0.0, scale=1.0)
            s.inc(sm["sv"], 1)
            v.wait(sm["sv"], sm["sv"].final)
            v.reciprocal(out=acc2[:, :], in_=acc4[:, :])
            v.barrier()
            v.tensor_tensor(out=acc4[:, :], in0=acc1[:, :], in1=acc2[:, :], op=ALU.mult)
            v.barrier()
            v.tensor_scalar(out=acc4[:, :], in0=acc4[:, :], scalar1=-1.0, scalar2=None, op0=ALU.mult)
            v.barrier()
            v.tensor_scalar(out=sq[:, :], in0=sq[:, :], scalar1=acc2[:, 0:1], scalar2=acc4[:, 0:1], op0=ALU.mult, op1=ALU.add)
            v.barrier()
            tp_tile(xnlT, 128 * j, 2176, scale_chunks=(grow, brow))

        v.sinc(sm["vy"], 1)
        y.wait(sm["vy"], sm["vy"].final)
        for r in range(2):
            for j in range(NT):
                y.dma_start(out=io["d_stile"][r, 128 * j:128 * j + 128].rearrange("(pp o) -> pp o", o=1),
                            in_=sownb[:, NT * r + j:NT * r + j + 1])
                y.inc(sm["stl"], 16)
        y.wait_cur(sm["stl"])
        for r in range(2):
            y.dma_start(out=io["d_sown"][r, :].rearrange("(a o) -> a o", o=1) if False else io["d_sown"][r:r+1, :],
                        in_=io["d_stile"][r:r+1, 64:2112])
            y.inc(sm["st2"], 16)
        y.wait_cur(sm["st2"])
        g.e.wait_ge(sm["st2"].h, sm["st2"].final)
        g.collective_compute("AllGather", ALU.bypass, replica_groups=[list(range(8))],
                             ins=[io["d_sown"].opt()], outs=[io["d_sag"].opt()])
        g.inc(sm["cc"], 1)
        y.wait_cur(sm["cc"])
        y.dma_start(out=io["d_sall"].rearrange("r (c j) -> r c j", c=8),
                    in_=io["d_sag"].rearrange("c r j -> r c j"))
        y.inc(sm["st2"], 16)
        y.wait_cur(sm["st2"])
        y.sinc(sm["gl"], 1)
        for c_ in [v, s, t]:
            c_.wait(sm["gl"], sm["gl"].final)
        emit(E, p, RB, io["d_sall"], io["d_sown"], onesr, ones128, identf, sm)
        for c_ in [v, s, t, y]:
            c_.wait(sm["done"], sm["done"].final)

        if upto <= 2:
            dump()
            return
        for mi in range(8):
            wbuf = mi % 2
            key = ("L", mi - 2)
            stage_waits(full=(mi == 0))
            y.dma_start(out=wSf[:, :].rearrange("p (k m) -> p k m", k=8),
                        in_=io["w_qkv_l"].rearrange("(k p) n -> p k n", p=128)[:, :, 128 * mi:128 * mi + 128])
            y.inc(sm["wld"], 16)
            if key in ts_after_m:
                s.wait(sm["ts"], ts_after_m[key])
            s.wait_cur(sm["wld"])
            s.activation(out=wS[:, 1024 * wbuf:1024 * wbuf + 1024], in_=wSf[:, :], func=AF.Identity, bias=0.0, scale=1.0)
            s.inc(sm["wcv"], 1)
            isq = mi < 4
            dst = qlT if isq else klT
            hc = mi % 4
            for n in range(5):
                nn0 = 512 * n
                w = min(512, 2176 - nn0)
                pp = pk[n % 2]
                t.wait(sm["wcv"], sm["wcv"].final)
                t.wait(sm["sp"], sm["sp"].final)
                for k in range(8):
                    t.matmul(pp[:, 0:w], wS[:, 1024 * wbuf + 128 * k:1024 * wbuf + 128 * k + 128],
                             xnlT[:, 2176 * k + nn0:2176 * k + nn0 + w], start=(k == 0), stop=(k == 7))
                t.inc(sm["ts"], 1)
                s.wait(sm["ts"], sm["ts"].final)
                if isq:
                    s.activation(out=dst[:, 2176 * hc + nn0:2176 * hc + nn0 + w], in_=pp[:, 0:w], func=AF.Copy, bias=0.0, scale=0.125)
                else:
                    s.copy(out=dst[:, 2176 * hc + nn0:2176 * hc + nn0 + w], in_=pp[:, 0:w])
                s.inc(sm["sp"], 1)
            ts_after_m[("L", mi)] = sm["ts"].final
        wr_load([(4096 + 512 * k, 512, io["w_qkv_l"][128 * k:128 * k + 128, 1024:1536]) for k in range(8)])
        for j in range(NT):
            t.wait(sm["wlr"], sm["wlr"].final)
            t.wait(sm["sp"], sm["sp"].final)
            for k in range(8):
                t.matmul(pw[:, 0:512], xnlT[:, 2176 * k + 128 * j:2176 * k + 128 * j + 128],
                         wR[:, 4096 + 512 * k:4096 + 512 * k + 512], start=(k == 0), stop=(k == 7))
            t.inc(sm["ts"], 1)
            s.wait(sm["ts"], sm["ts"].final)
            s.copy(out=vlx[:, 520 * j:520 * j + 520].rearrange("p (h i) -> p h i", h=8)[:, :, 0:64], in_=pw[:, 0:512])
            s.inc(sm["sp"], 1)
        v.wait(sm["sp"], sm["sp"].final)
        for j in range(NT):
            v.memset(vlx[:, 520 * j:520 * j + 520].rearrange("p (h i) -> p h i", h=8)[:, :, 64:65], 1.0)
        v.barrier()
        v.sinc(sm["vt"], 1)
        t.wait(sm["vt"], sm["vt"].final)

        if upto <= 3:
            dump()
            return
        for b in range(NT):
            nb = 0
            y.wait(sm["vs"], sm["vs"].final)
            stage_waits()
            y.dma_start(out=wSf[:, 0:256], in_=io["l_mask"][b, :, :])
            y.inc(sm["yv"], 16)
            s.wait_cur(sm["yv"])
            s.activation(out=lmk2[:, 256 * nb:256 * nb + 256], in_=wSf[:, 0:256], func=AF.Identity, bias=0.0, scale=1.0)
            s.inc(sm["nbc"], 1)
            q0, q1 = lwins[b]
            w = q1 - q0
            ov = 128 if b > 0 else 0
            for h in range(8):
                hc = h // 2
                hp = (h % 2) * 64
                pidx = h % 2
                t.wait(sm["sp"], sm["sp"].final)
                t.matmul(pna[0:128, 0:w], klT[hp:hp + 64, 2176 * hc + 128 * b:2176 * hc + 128 * b + 128],
                         qlT[hp:hp + 64, 2176 * hc + (q0 - LAM0):2176 * hc + (q0 - LAM0) + w], start=True, stop=True)
                t.inc(sm["tv"], 1)
                v.wait(sm["tv"], sm["tv"].final)
                v.wait(sm["nbc"], sm["nbc"].final)
                v.tensor_tensor(out=pna[0:128, 0:w], in0=pna[0:128, 0:w], in1=lmk2[:, 256 * nb:256 * nb + w], op=ALU.add)
                v.inc(sm["vs"], 1)
                s.wait(sm["vs"], sm["vs"].final)
                s.wait(sm["ts"], sm["ts"].final)
                s.activation(out=prb[:, 0:w], in_=pna[0:128, 0:w], func=AF.Exp, bias=0.0, scale=1.0)
                s.inc(sm["sp"], 1)
                t.wait(sm["sp"], sm["sp"].final)
                t.wait(sm["pv2"], sm["pv2"].final)
                t.matmul(ptb[0:8, 0:w], chb[:, 15 - h:23 - h], prb[:, 0:w], start=(h == 0), stop=(h == 7))
                t.matmul(pav[pidx][0:65, 0:w], vlx[:, 520 * b + 65 * h:520 * b + 65 * h + 65],
                         prb[:, 0:w], start=True, stop=True)
                t.inc(sm["ts"], 1)
                v.wait(sm["ts"], sm["ts"].final)
                rp = hp
                cb = q0 - LAM0
                if ov > 0:
                    v.tensor_tensor(out=olT[rp:rp + 64, 2176 * hc + cb:2176 * hc + cb + ov],
                                    in0=olT[rp:rp + 64, 2176 * hc + cb:2176 * hc + cb + ov],
                                    in1=pav[pidx][0:64, 0:ov], op=ALU.add)
                v.tensor_scalar(out=olT[rp:rp + 64, 2176 * hc + cb + ov:2176 * hc + cb + w],
                                in0=pav[pidx][0:64, ov:w], scalar1=1.0, scalar2=None, op0=ALU.mult)
                v.barrier()
                v.sinc(sm["pv2"], 1)
            cb = q0 - LAM0
            if ov > 0:
                v.tensor_tensor(out=denl[0:8, cb:cb + ov], in0=denl[0:8, cb:cb + ov],
                                in1=ptb[0:8, 0:ov], op=ALU.add)
            v.tensor_scalar(out=denl[0:8, cb + ov:cb + w], in0=ptb[0:8, ov:w], scalar1=1.0, scalar2=None, op0=ALU.mult)
            v.barrier()
            v.sinc(sm["pv2"], 1)
        v.wait(sm["pv2"], sm["pv2"].final)
        v.reciprocal(out=denl[0:8, :], in_=denl[0:8, :])
        v.barrier()
        v.sinc(sm["vt"], 1)
        for c in range(4):
            for n in range(5):
                w = min(512, 2176 - 512 * n)
                t.wait(sm["vt"], sm["vt"].final)
                t.wait(sm["vs"], sm["vs"].final)
                t.matmul(pna[0:128, 0:w], Epair[0:8, 128 * c:128 * c + 128], denl[0:8, 512 * n:512 * n + w], start=True, stop=True)
                t.inc(sm["tv"], 1)
                v.wait(sm["tv"], sm["tv"].final)
                v.tensor_tensor(out=olT[:, 2176 * c + 512 * n:2176 * c + 512 * n + w],
                                in0=olT[:, 2176 * c + 512 * n:2176 * c + 512 * n + w],
                                in1=pna[0:128, 0:w], op=ALU.mult)
                v.barrier()
                v.sinc(sm["vs"], 1)
        wr_load([(1024 * c, 1024, io["w_out_l"][128 * c:128 * c + 128, :]) for c in range(4)])
        for j in range(NT):
            buf = j % 2
            par = sm["sta"] if buf == 0 else sm["stb"]
            y.wait(sm["vy"], max(0, sm["vy"].final - 1))
            y.wait(par, par.final)
            y.wait_cur(sm["xld"])
            y.dma_start(out=xt2[:, 1024 * buf:1024 * buf + 1024], in_=io["d_x2"][128 * j:128 * j + 128, :])
            y.inc(sm["xld"], 16)
            t.wait(sm["wlr"], sm["wlr"].final)
            t.wait(sm["vs"], sm["vs"].final)
            t.wait(sm["vy"], sm["vy"].final)
            for nh in range(2):
                for k in range(4):
                    t.matmul(pw[:, 512 * nh:512 * nh + 512], olT[:, 2176 * k + 128 * j:2176 * k + 128 * j + 128],
                             wR[:, 1024 * k + 512 * nh:1024 * k + 512 * nh + 512], start=(k == 0), stop=(k == 3))
            t.inc(sm["tv"], 1)
            v.wait(sm["tv"], sm["tv"].final)
            v.wait(sm["xld"], sm["xld"].final)
            v.tensor_tensor(out=xt2[:, 1024 * buf:1024 * buf + 1024], in0=xt2[:, 1024 * buf:1024 * buf + 1024], in1=pw[:, :], op=ALU.add)
            v.barrier()
            v.tensor_tensor(out=xt2[:, 1024 * buf:1024 * buf + 1024], in0=xt2[:, 1024 * buf:1024 * buf + 1024], in1=nullqb[:, :], op=ALU.add)
            v.barrier()
            v.sinc(sm["vy"], 1)
            y.wait(sm["vy"], sm["vy"].final)
            y.dma_start(out=io["d_xl"][128 * j:128 * j + 128, :], in_=xt2[:, 1024 * buf:1024 * buf + 1024])
            y.inc(par, 16)

        if upto <= 4:
            dump()
            return
        v.wait(sm["done"], sm["done"].final)
        for tau in range(16):
            v.tensor_scalar(out=ohq[:, QPAD * tau:QPAD * tau + QPAD], in0=iota512[:, 0:QPAD],
                            scalar1=RB["rpT_q"][:, tau:tau + 1], scalar2=None, op0=ALU.is_equal)
            v.tensor_scalar(out=ohkv[:, KVPAD * tau:KVPAD * tau + KVPAD], in0=iota512[:, 0:KVPAD],
                            scalar1=RB["rpT_kv"][:, tau:tau + 1], scalar2=None, op0=ALU.is_equal)
            v.barrier()
        y.wait_cur(sm["sta"])
        y.wait_cur(sm["stb"])
        for tau in range(16):
            buf = tau % 2
            y.wait(sm["vy"], max(0, sm["vy"].final - 1))
            y.wait_cur(sm["xld"])
            y.dma_start(out=xt2[:, 1024 * buf:1024 * buf + 1024], in_=io["d_x2"][64 + 128 * tau:64 + 128 * tau + 128, :])
            y.inc(sm["xld"], 16)
            v.wait(sm["xld"], sm["xld"].final)
            xt = xt2[:, 1024 * buf:1024 * buf + 1024]
            ln_tile(xt)
            v.tensor_scalar(out=acc1[:, :], in0=acc3[:, :], scalar1=-1e-5, scalar2=None, op0=ALU.add)
            v.barrier()
            v.reciprocal(out=acc2[:, :], in_=acc3[:, :])
            v.barrier()
            v.tensor_tensor(out=acc1[:, :], in0=acc1[:, :], in1=acc2[:, :], op=ALU.mult)
            v.barrier()
            v.tensor_scalar(out=acc1[:, :], in0=acc1[:, :], scalar1=1024.0, scalar2=1e-24, op0=ALU.mult, op1=ALU.max)
            v.inc(sm["vs"], 1)
            s.wait(sm["vs"], sm["vs"].final)
            s.activation(out=acc1[:, :], in_=acc1[:, :], func=AF.Sqrt, bias=0.0, scale=1.0)
            s.inc(sm["sv"], 1)
            v.wait(sm["sv"], sm["sv"].final)
            v.reciprocal(out=acc1[:, :], in_=acc1[:, :])
            v.barrier()
            v.tensor_scalar(out=acc1[:, :], in0=acc1[:, :], scalar1=32.0, scalar2=None, op0=ALU.mult)
            v.barrier()
            v.tensor_scalar(out=xn1T[:, 1024 * tau:1024 * tau + 1024], in0=sq[:, :], scalar1=acc1[:, 0:1], scalar2=None, op0=ALU.mult)
            v.barrier()
            v.sinc(sm["vy"], 1)
        v.sinc(sm["vt"], 1)
        for (oh, cT, PAD) in [(ohq, cTq, QPAD), (ohkv, cTkv, KVPAD)]:
            for dch in range(8):
                t.wait(sm["vt"], sm["vt"].final)
                t.wait(sm["sp"], sm["sp"].final)
                for tau in range(16):
                    t.matmul(pna[0:128, 0:PAD],
                             xn1T[:, 1024 * tau + 128 * dch:1024 * tau + 128 * dch + 128],
                             oh[:, PAD * tau:PAD * tau + PAD], start=(tau == 0), stop=(tau == 15))
                t.inc(sm["ts"], 1)
                s.wait(sm["ts"], sm["ts"].final)
                s.copy(out=cT[:, PAD * dch:PAD * dch + PAD], in_=pna[0:128, 0:PAD])
                s.inc(sm["sp"], 1)
        hfirst = [True]
        for (wname, dst, nm_, scaleq, cTsrc, PAD) in [("w_q_h", qTo, 4, True, cTq, QPAD),
                                                      ("w_kv_h", kvTo, 8, False, cTkv, KVPAD)]:
            for mi in range(nm_):
                y.wait(sm["ts"], sm["ts"].final)
                stage_waits(full=hfirst[0])
                hfirst[0] = False
                y.dma_start(out=wSf[:, :].rearrange("p (k m) -> p k m", k=8),
                            in_=io[wname].rearrange("(k p) n -> p k n", p=128)[:, :, 128 * mi:128 * mi + 128])
                y.inc(sm["wld"], 16)
                v.wait(sm["wld"], sm["wld"].final)
                v.wait(sm["ts"], sm["ts"].final)
                for k in range(8):
                    v.tensor_scalar(out=wS[:, 128 * k:128 * k + 128], in0=wSf[:, 128 * k:128 * k + 128],
                                    scalar1=grmsc[:, k:k + 1], scalar2=None, op0=ALU.mult)
                v.barrier()
                v.sinc(sm["vt"], 1)
                t.wait(sm["vt"], sm["vt"].final)
                t.wait(sm["sp"], sm["sp"].final)
                for k in range(8):
                    t.matmul(pna[0:128, 0:PAD], wS[:, 128 * k:128 * k + 128],
                             cTsrc[:, PAD * k:PAD * k + PAD], start=(k == 0), stop=(k == 7))
                t.inc(sm["ts"], 1)
                s.wait(sm["ts"], sm["ts"].final)
                if scaleq:
                    s.activation(out=dst[:, PAD * mi:PAD * mi + PAD], in_=pna[0:128, 0:PAD], func=AF.Copy, bias=0.0, scale=0.125)
                else:
                    s.copy(out=dst[:, PAD * mi:PAD * mi + PAD], in_=pna[0:128, 0:PAD])
                s.inc(sm["sp"], 1)
        t.wait(sm["sp"], sm["sp"].final)
        for h in range(8):
            for sg in range(4):
                t.wait(sm["pv"], sm["pv"].final)
                t.transpose(ptbb[0:128, 0:64], kvTo[64:128, KVPAD * h + 128 * sg:KVPAD * h + 128 * sg + 128], identb[64:128, 64:128])
                t.inc(sm["ps"], 1)
                v.wait(sm["ps"], sm["ps"].final)
                v.tensor_scalar(out=vxo[:, 520 * sg + 65 * h:520 * sg + 65 * h + 64], in0=ptbb[0:128, 0:64],
                                scalar1=1.0, scalar2=None, op0=ALU.mult)
                v.barrier()
                v.sinc(sm["pv"], 1)
        for sg in range(4):
            v.memset(vxo[:, 520 * sg:520 * sg + 520].rearrange("p (h i) -> p h i", h=8)[:, :, 64:65], 1.0)
        v.barrier()
        v.tensor_scalar(out=kvmask[:, :], in0=iotap[:, :], scalar1=RB["cntb"][:, 1:2], scalar2=-1e9, op0=ALU.is_ge, op1=ALU.mult)
        v.barrier()
        v.sinc(sm["vy"], 1)
        y.wait(sm["vy"], sm["vy"].final)
        y.wait(sm["sp"], sm["sp"].final)
        y.dma_start(out=io["d_qown"].rearrange("(c p) n -> p c n", p=128), in_=qTo[:, :].rearrange("p (c n) -> p c n", c=4))
        y.inc(sm["st2"], 16)
        y.wait_cur(sm["st2"])
        g.e.wait_ge(sm["st2"].h, sm["st2"].final)
        g.collective_compute("AllGather", ALU.bypass, replica_groups=[list(range(8))],
                             ins=[io["d_qown"].opt()], outs=[io["d_qag"].opt()])
        g.inc(sm["cc"], 1)
        y.wait_cur(sm["cc"])
        for h in range(8):
            y.dma_start(out=qTall[0:64, 2048 * h:2048 * h + 2048].rearrange("p (rk l) -> p rk l", rk=8),
                        in_=io["d_qag"][:, 64 * h:64 * h + 64, :].rearrange("rk p l -> p rk l"))
            y.inc(sm["qld"], 16)
        y.wait_cur(sm["qld"])
        y.sinc(sm["gl"], 1)
        for c_ in [v, s, t]:
            c_.wait(sm["gl"], sm["gl"].final)
        for h in range(8):
            hc = h // 2
            hp = (h % 2) * 64
            for sg in range(4):
                for qc in range(4):
                    t.wait(sm["sp"], sm["sp"].final)
                    t.matmul(pna[0:128, 0:512], kvTo[0:64, KVPAD * h + 128 * sg:KVPAD * h + 128 * sg + 128],
                             qTall[0:64, 2048 * h + 512 * qc:2048 * h + 512 * qc + 512],
                             start=True, stop=True)
                    t.inc(sm["tv"], 1)
                    v.wait(sm["tv"], sm["tv"].final)
                    v.tensor_scalar(out=pna[0:128, 0:512], in0=pna[0:128, 0:512], scalar1=kvmask[:, sg:sg + 1], scalar2=None, op0=ALU.add)
                    v.inc(sm["vs"], 1)
                    s.wait(sm["vs"], sm["vs"].final)
                    s.activation(out=pTh[:, 2048 * sg + 512 * qc:2048 * sg + 512 * qc + 512], in_=pna[0:128, 0:512], func=AF.Exp, bias=0.0, scale=1.0)
                    s.inc(sm["sp"], 1)
            for qc in range(4):
                t.wait(sm["sp"], sm["sp"].final)
                t.matmul(pna[0:1, 0:512], nullk[0:64, h:h + 1],
                         qTall[0:64, 2048 * h + 512 * qc:2048 * h + 512 * qc + 512], start=True, stop=True)
                t.inc(sm["tv"], 1)
                v.wait(sm["tv"], sm["tv"].final)
                v.tensor_scalar(out=pna[0:1, 0:512], in0=pna[0:1, 0:512], scalar1=nullmask[0:1, 0:1], scalar2=None, op0=ALU.add)
                v.inc(sm["vs"], 1)
                s.wait(sm["vs"], sm["vs"].final)
                s.activation(out=pThn[0:1, 512 * qc:512 * qc + 512], in_=pna[0:1, 0:512], func=AF.Exp, bias=0.0, scale=1.0)
                s.inc(sm["sp"], 1)
            for qc in range(4):
                pidx = qc % 2
                t.wait(sm["sp"], sm["sp"].final)
                t.wait(sm["pv2"], sm["pv2"].final)
                for sg in range(4):
                    t.matmul(pav[pidx][0:65, 0:512], vxo[:, 520 * sg + 65 * h:520 * sg + 65 * h + 65],
                             pTh[:, 2048 * sg + 512 * qc:2048 * sg + 512 * qc + 512], start=(sg == 0), stop=False)
                t.matmul(pav[pidx][0:65, 0:512], nullvx[0:1, 65 * h:65 * h + 65],
                         pThn[0:1, 512 * qc:512 * qc + 512], start=False, stop=True)
                t.inc(sm["ts"], 1)
                v.wait(sm["ts"], sm["ts"].final)
                v.tensor_scalar(out=oHs[0:65, 512 * qc:512 * qc + 512], in0=pav[pidx][0:65, 0:512], scalar1=1.0, scalar2=None, op0=ALU.mult)
                v.barrier()
                v.sinc(sm["pv2"], 1)
            v.sinc(sm["vt"], 1)
            t.wait(sm["vt"], sm["vt"].final)
            for cc_ in range(16):
                t.wait(sm["pv"], sm["pv"].final)
                t.transpose(ptb[0:128, 0:65], oHs[0:65, 128 * cc_:128 * cc_ + 128], identf[0:65, 0:65])
                t.inc(sm["ps"], 1)
                par = sm["sta"] if cc_ % 2 == 0 else sm["stb"]
                v.wait(sm["ps"], sm["ps"].final)
                v.wait(par, par.final)
                v.tensor_scalar(out=partsb[:, 65 * (cc_ % 2):65 * (cc_ % 2) + 65], in0=ptb[0:128, 0:65],
                                scalar1=1.0, scalar2=None, op0=ALU.mult)
                v.barrier()
                v.sinc(sm["pv"], 1)
                v.sinc(sm["vy"], 1)
                y.wait(sm["vy"], sm["vy"].final)
                y.wait(par, par.final)
                y.dma_start(out=io["d_part"][128 * cc_:128 * cc_ + 128, 65 * h:65 * h + 65],
                            in_=partsb[:, 65 * (cc_ % 2):65 * (cc_ % 2) + 65])
                y.inc(par, 16)
        y.wait_cur(sm["st2"])
        y.wait_cur(sm["sta"])
        y.wait_cur(sm["stb"])
        g.e.wait_ge(sm["st2"].h, sm["st2"].final)
        g.e.wait_ge(sm["sta"].h, sm["sta"].final)
        g.e.wait_ge(sm["stb"].h, sm["stb"].final)
        g.collective_compute("ReduceScatter", ALU.add, replica_groups=[list(range(8))],
                             ins=[io["d_part"].opt()], outs=[io["d_rs"].opt()])
        g.inc(sm["cc"], 1)
        y.wait_cur(sm["cc"])
        y.wait_cur(sm["qld"])
        for rr in range(2):
            y.dma_start(out=rsb[:, 520 * rr:520 * rr + 520], in_=io["d_rs"][128 * rr:128 * rr + 128, :])
            y.inc(sm["qld"], 16)
        y.wait_cur(sm["qld"])
        y.sinc(sm["gl"], 1)
        for c_ in [v, s, t]:
            c_.wait(sm["gl"], sm["gl"].final)
        for rr in range(2):
            for h in range(8):
                v.reciprocal(out=recH[:, 8 * rr + h:8 * rr + h + 1], in_=rsb[:, 520 * rr + 65 * h + 64:520 * rr + 65 * h + 65])
                v.barrier()
                v.tensor_scalar(out=rsb[:, 520 * rr + 65 * h:520 * rr + 65 * h + 64],
                                in0=rsb[:, 520 * rr + 65 * h:520 * rr + 65 * h + 64],
                                scalar1=recH[:, 8 * rr + h:8 * rr + h + 1], scalar2=None, op0=ALU.mult)
                v.barrier()
        v.sinc(sm["vt"], 1)
        t.wait(sm["vt"], sm["vt"].final)
        for rr in range(2):
            for h in range(8):
                t.wait(sm["pv"], sm["pv"].final)
                t.transpose(ptb[0:64, 0:128], rsb[:, 520 * rr + 65 * h:520 * rr + 65 * h + 64], identf[:, :])
                t.inc(sm["ps"], 1)
                v.wait(sm["ps"], sm["ps"].final)
                v.tensor_scalar(out=oTh[(h % 2) * 64:(h % 2) * 64 + 64, 256 * (h // 2) + 128 * rr:256 * (h // 2) + 128 * rr + 128],
                                in0=ptb[0:64, 0:128], scalar1=1.0, scalar2=None, op0=ALU.mult)
                v.barrier()
                v.sinc(sm["pv"], 1)
        wr_load([(1024 * c, 1024, io["w_out_h"][128 * c:128 * c + 128, :]) for c in range(4)])
        v.sinc(sm["vt"], 1)
        for m2 in range(2):
            t.wait(sm["wlr"], sm["wlr"].final)
            t.wait(sm["vt"], sm["vt"].final)
            for nh in range(2):
                for k in range(4):
                    t.matmul(pw[:, 512 * nh:512 * nh + 512], oTh[:, 256 * k + 128 * m2:256 * k + 128 * m2 + 128],
                             wR[:, 1024 * k + 512 * nh:1024 * k + 512 * nh + 512], start=(k == 0), stop=(k == 3))
            t.inc(sm["tv"], 1)
            v.wait(sm["tv"], sm["tv"].final)
            v.tensor_tensor(out=Hrow[:, 1024 * m2:1024 * m2 + 1024], in0=pw[:, :], in1=nullqb[:, :], op=ALU.subtract)
            v.barrier()
            v.sinc(sm["vt"], 1)
        for tau in range(16):
            v.tensor_scalar(out=ohq[:, QPAD * tau:QPAD * tau + QPAD], in0=iota512[:, 0:QPAD],
                            scalar1=RB["rpT_q"][:, tau:tau + 1], scalar2=None, op0=ALU.is_equal)
        v.barrier()
        v.sinc(sm["vt"], 1)
        t.wait(sm["vt"], sm["vt"].final)
        y.wait_cur(sm["sta"])
        y.wait_cur(sm["stb"])
        for tau in range(16):
            buf = tau % 2
            par = sm["sta"] if buf == 0 else sm["stb"]
            y.wait(sm["vy"], max(0, sm["vy"].final - 1))
            y.wait(par, par.final)
            y.wait_cur(sm["xld"])
            y.dma_start(out=xt2[:, 1024 * buf:1024 * buf + 1024], in_=io["d_xl"][64 + 128 * tau:64 + 128 * tau + 128, :])
            y.inc(sm["xld"], 16)
            for sc in range(2):
                t.wait(sm["pv"], sm["pv"].final)
                t.transpose(ptbb[0:128, 0:128], ohq[:, QPAD * tau + 128 * sc:QPAD * tau + 128 * sc + 128], identb[:, :])
                t.inc(sm["ps"], 1)
                v.wait(sm["ps"], sm["ps"].final)
                v.wait(sm["tv"], sm["tv"].final)
                v.tensor_scalar(out=ohqT[:, 128 * sc:128 * sc + 128], in0=ptbb[0:128, 0:128],
                                scalar1=1.0, scalar2=None, op0=ALU.mult)
                v.barrier()
                v.sinc(sm["pv"], 1)
                v.sinc(sm["vs"], 1)
            t.wait(sm["vs"], sm["vs"].final)
            for nh in range(2):
                for sc in range(2):
                    t.matmul(pw[:, 512 * nh:512 * nh + 512], ohqT[:, 128 * sc:128 * sc + 128],
                             Hrow[:, 1024 * sc + 512 * nh:1024 * sc + 512 * nh + 512], start=(sc == 0), stop=(sc == 1))
            t.inc(sm["tv"], 1)
            v.wait(sm["tv"], sm["tv"].final)
            v.wait(sm["xld"], sm["xld"].final)
            v.tensor_tensor(out=xt2[:, 1024 * buf:1024 * buf + 1024], in0=xt2[:, 1024 * buf:1024 * buf + 1024], in1=pw[:, :], op=ALU.add)
            v.barrier()
            v.sinc(sm["vy"], 1)
            v.sinc(sm["vs"], 1)
            y.wait(sm["vy"], sm["vy"].final)
            y.dma_start(out=io["out"][128 * tau:128 * tau + 128, :], in_=xt2[:, 1024 * buf:1024 * buf + 1024])
            y.inc(par, 16)
        y.wait_cur(sm["sta"])
        y.wait_cur(sm["stb"])
        dump()

    return p.finalize()


_NC_CACHE = {}


def kernel(**inputs):
    if "nc" not in _NC_CACHE:
        _NC_CACHE["nc"] = build()
    nc = _NC_CACHE["nc"]
    in_maps = [host_inputs(cid, inputs) for cid in range(8)]
    res = run_bass_kernel_spmd(nc, in_maps, core_ids=list(range(8)))
    out = np.concatenate([np.asarray(res.results[c]["out"]) for c in range(8)], 0)
    return out.reshape(1, 16384, 1024).astype(np.float32)


# version 04 (final)
# speedup vs baseline: 1.0297x; 1.0108x over previous
import numpy as np
import contextlib
import concourse.bass as bass
import concourse.mybir as mybir
from concourse.bass_utils import run_bass_kernel_spmd

F32 = mybir.dt.float32
BF16 = mybir.dt.bfloat16
I16 = mybir.dt.int16
I32 = mybir.dt.int32
U32 = mybir.dt.uint32
U8 = mybir.dt.uint8
AF = mybir.ActivationFunctionType
ALU = mybir.AluOpType
AX = mybir.AxisListType
BOOT_MAGIC = 5


class SemT:
    def __init__(self, h, name):
        self.h = h
        self.name = name
        self.final = 0


class EngCtx:
    def __init__(self, eng, tick, name):
        self.e = eng
        self.tick = tick
        self.name = name
        self.last = None

    def __getattr__(self, name):
        fn = getattr(self.e, name)

        def wrap(*a, **k):
            inst = fn(*a, **k)
            self.last = inst
            return inst

        return wrap

    def barrier(self):
        if self.last is None:
            return
        self.last.then_inc(self.tick.h, 1)
        self.tick.final += 1
        self.e.wait_ge(self.tick.h, self.tick.final)
        self.last = None

    def inc(self, sem, n=1):
        if self.last is None:
            self.sinc(sem, n)
            return
        self.last.then_inc(sem.h, n)
        sem.final += n
        self.last = None

    def sinc(self, sem, n=1):
        self.e.sem_inc(sem.h, n)
        sem.final += n

    def wait(self, sem, v):
        self.e.wait_ge(sem.h, v)

    def wait_cur(self, sem):
        self.e.wait_ge(sem.h, sem.final)


class E:
    pass


class Prog:
    def __init__(self):
        self.nc = bass.Bass("TRN2", target_bir_lowering=False, debug=False, dynamic_dma_scratch_size=4096)
        self._es = contextlib.ExitStack()
        self.sems = []
        self._body = None

    def sbuf(self, nm, shape, dt=F32):
        return self._es.enter_context(self.nc.sbuf_tensor(nm, shape, dt))

    def psum(self, nm, shape, dt=F32):
        return self._es.enter_context(self.nc.psum_tensor(nm, shape, dt))

    def sem(self, nm):
        s = self._es.enter_context(self.nc.semaphore(nm))
        st = SemT(s, nm)
        self.sems.append(st)
        return st

    def dram(self, nm, shape, dt=F32, kind=None):
        if kind:
            t = self.nc.dram_tensor(nm, shape, dt, kind=kind)
        else:
            t = self.nc.dram_tensor(nm, shape, dt)
        return t.ap()

    def body(self, f):
        self._body = f
        return f

    def finalize(self):
        nc = self.nc
        boot = self.sem("boot")
        fin = self.sem("fin")
        ticks = {e: self.sem(f"tk_{e}") for e in ["v", "s", "t", "g", "y"]}
        semlo = min(s.h.num for s in self.sems)
        semhi = max(s.h.num for s in self.sems)
        block = self._es.enter_context(nc.Block())

        @block.sync
        def _(v):
            g = nc.gpsimd
            g.sem_clear(range(semlo, semhi + 1))
            g.sem_inc(boot.h, BOOT_MAGIC)
            en = E()
            en.nc = nc
            en.g = EngCtx(nc.gpsimd, ticks["g"], "g")
            en.v = EngCtx(nc.vector, ticks["v"], "v")
            en.s = EngCtx(nc.scalar, ticks["s"], "s")
            en.t = EngCtx(nc.tensor, ticks["t"], "t")
            en.y = EngCtx(nc.sync, ticks["y"], "y")
            for c in [en.v, en.s, en.t, en.y]:
                c.e.wait_op(boot.h, BOOT_MAGIC, "sem-eq")
            self._body(en)
            for c in [en.v, en.s, en.t, en.y]:
                for s_ in self.sems:
                    if s_.final > 0 and s_ is not fin and s_ is not boot:
                        c.e.wait_ge(s_.h, s_.final)
                c.e.sem_inc(fin.h, 1)
            fin.final += 4
            g.wait_ge(fin.h, fin.final)
            for s_ in self.sems:
                if s_.final > 0 and s_ is not fin and s_ is not boot:
                    g.wait_ge(s_.h, s_.final)
        self._es.close()
        return nc




"""Routing: 50-iter coor_descent (iter 1 folded into host fp32 const) + selection.

Gather-free outputs, per route r in {q(K=1024,PAD=256), kv(K=2048,PAD=512)}:
  rpT_q / rpT_kv [128, 16] f32: per own-token local slot (in [0,PAD) if selected,
           else negative/garbage>=PAD) -- partition = token%128, col = token//128.
  ogb  [128, 2] f32: this core's global rank offset o_c per route (broadcast).
  cntb [128, 2] f32: number of valid own slots per route (broadcast).
All cores compute redundantly from s_all [2,16384]; own-range uses s_own [2,2048].
"""

N_TOK = 16384
NQ, NKV = 1024, 2048
QPAD, KVPAD = 256, 512
N_ITER_DEV = 49


def host_consts(core_id):
    L16 = np.zeros((128, 16), np.float32)
    for m in range(16):
        L16[:m, m] = 1.0
    ind16 = np.zeros((128, 16), np.float32)
    ind16[: 16 * core_id, :] = 1.0
    logk = np.log(np.array([1152.0, 2304.0], dtype=np.float32)).astype(np.float32).reshape(1, 2)
    na1 = -(logk - np.float32(np.log(np.float32(16384.0)))).astype(np.float32)
    Krow = np.array([[1024.0, 2048.0]], dtype=np.float32)
    return dict(rt_L16=L16, rt_ind16=ind16, rt_logk=logk, rt_na1=na1, rt_Krow=Krow)


def alloc(p, psums=None):
    B = {}
    for nm, shape in [("rt_L16", [128, 16]), ("rt_ind16", [128, 16]),
                      ("rt_logk", [1, 2]), ("rt_na1", [1, 2]), ("rt_Krow", [1, 2])]:
        B["d_" + nm] = p.dram(nm, shape, F32, kind="ExternalInput")
        B[nm] = p.sbuf(nm + "_s", shape, F32)
    B["s2"] = p.sbuf("rt_s2", [128, 256], F32)
    B["so16"] = p.sbuf("rt_so16", [16, 256], F32)
    B["na2"] = p.sbuf("rt_na2", [128, 2], F32)
    B["nC2"] = p.sbuf("rt_nC2", [128, 2], F32)
    B["t2"] = p.sbuf("rt_t2", [128, 256], F32)
    B["e2"] = p.sbuf("rt_e2", [128, 256], F32)
    B["rows2"] = p.sbuf("rt_rows2", [128, 2], F32)
    B["lnS"] = p.sbuf("rt_lnS", [1, 2], F32)
    B["nan2"] = p.sbuf("rt_nan2", [1, 2], F32)
    B["satg"] = p.sbuf("rt_satg", [128, 256], F32)
    B["rowsg"] = p.sbuf("rt_rowsg", [128, 2], F32)
    B["uo16"] = p.sbuf("rt_uo16", [16, 256], F32)
    B["sato"] = p.sbuf("rt_sato", [16, 256], F32)
    B["incl"] = p.sbuf("rt_incl", [16, 256], F32)
    B["zer16"] = p.sbuf("rt_zer16", [16, 256], F32)
    B["rowtot"] = p.sbuf("rt_rowtot", [16, 2], F32)
    B["offsL"] = p.sbuf("rt_offsL", [16, 2], F32)
    B["og16"] = p.sbuf("rt_og16", [16, 2], F32)
    B["tot"] = p.sbuf("rt_tot", [1, 2], F32)
    B["cnt"] = p.sbuf("rt_cnt", [1, 2], F32)
    B["lrank"] = p.sbuf("rt_lrank", [16, 256], F32)
    B["sel"] = p.sbuf("rt_sel", [16, 256], F32)
    B["rankp"] = p.sbuf("rt_rankp", [16, 256], F32)
    B["rpT_q"] = p.sbuf("rt_rpT_q", [128, 16], F32)
    B["rpT_kv"] = p.sbuf("rt_rpT_kv", [128, 16], F32)
    B["ogb"] = p.sbuf("rt_ogb", [128, 2], F32)
    B["cntb"] = p.sbuf("rt_cntb", [128, 2], F32)
    if psums is None:
        B["pr12"] = p.psum("rt_pr12", [1, 2], F32)
        B["pb"] = p.psum("rt_pb", [128, 2], F32)
        B["p16"] = p.psum("rt_p16", [16, 2], F32)
        B["p11"] = p.psum("rt_p11", [1, 2], F32)
        B["ptT"] = p.psum("rt_ptT", [128, 16], F32)
    else:
        B.update(psums)
    return B


def emit_const_loads(E, p, B, sem_ld):
    y = E.y
    for nm in ["rt_L16", "rt_ind16", "rt_logk", "rt_na1", "rt_Krow"]:
        y.dma_start(out=B[nm][0:B["d_" + nm].shape[0], :], in_=B["d_" + nm][:, :])
        y.inc(sem_ld, 16)
    y.dma_start(out=B["na2"][0:1, :], in_=B["d_rt_na1"][:, :])
    y.inc(sem_ld, 16)


def emit(E, p, B, s_all_dram, s_own_dram, onesr, ones128, ident, sems):
    v, s, t, g, y = E.v, E.s, E.t, E.g, E.y
    ld2 = sems["ld2"]

    y.dma_start(out=B["s2"][:, 0:128], in_=s_all_dram[0, :].rearrange("(p f) -> p f", p=128))
    y.inc(ld2, 16)
    y.dma_start(out=B["s2"][:, 128:256], in_=s_all_dram[1, :].rearrange("(p f) -> p f", p=128))
    y.inc(ld2, 16)
    y.dma_start(out=B["so16"][0:16, 0:128], in_=s_own_dram[0, :].rearrange("(a f) -> a f", a=16))
    y.inc(ld2, 16)
    y.dma_start(out=B["so16"][0:16, 128:256], in_=s_own_dram[1, :].rearrange("(a f) -> a f", a=16))
    y.inc(ld2, 16)

    t.wait_cur(sems["cl"])
    t.matmul(B["pb"][:, :], onesr[0:1, 0:128], B["na2"][0:1, 0:2], start=True, stop=True)
    t.inc(sems["pv"], 1)
    v.wait_cur(ld2)
    v.wait_cur(sems["pv"])
    v.tensor_scalar(out=B["na2"][:, :], in0=B["pb"][:, :], scalar1=1.0, scalar2=None, op0=ALU.mult)
    v.barrier()

    for it in range(N_ITER_DEV):
        v.tensor_scalar(out=B["nC2"][:, :], in0=B["na2"][:, :], scalar1=-1.0, scalar2=None, op0=ALU.mult)
        v.tensor_scalar(out=B["t2"][:, 0:128], in0=B["s2"][:, 0:128], scalar1=B["na2"][:, 0:1], scalar2=None, op0=ALU.min)
        v.tensor_scalar(out=B["t2"][:, 128:256], in0=B["s2"][:, 128:256], scalar1=B["na2"][:, 1:2], scalar2=None, op0=ALU.min)
        v.barrier()
        v.sinc(sems["vs"], 1)
        s.wait(sems["vs"], sems["vs"].final)
        s.activation(out=B["e2"][:, 0:128], in_=B["t2"][:, 0:128], func=AF.Exp, bias=B["nC2"][:, 0:1], scale=1.0, accum_out=B["rows2"][:, 0:1])
        s.activation(out=B["e2"][:, 128:256], in_=B["t2"][:, 128:256], func=AF.Exp, bias=B["nC2"][:, 1:2], scale=1.0, accum_out=B["rows2"][:, 1:2])
        s.inc(sems["sp"], 1)
        t.wait(sems["sp"], sems["sp"].final)
        t.wait(sems["sv"], sems["sv"].final)
        t.matmul(B["pr12"][:, :], ones128[:, 0:1], B["rows2"][:, 0:2], start=True, stop=True)
        t.inc(sems["ps"], 1)
        s.wait(sems["ps"], sems["ps"].final)
        s.activation(out=B["lnS"][0:1, :], in_=B["pr12"][0:1, :], func=AF.Ln, bias=0.0, scale=1.0)
        s.inc(sems["sv"], 1)
        v.wait(sems["sv"], sems["sv"].final)
        v.tensor_tensor(out=B["nan2"][0:1, :], in0=B["lnS"][0:1, :], in1=B["na2"][0:1, 0:2], op=ALU.add)
        v.barrier()
        v.tensor_tensor(out=B["nan2"][0:1, :], in0=B["nan2"][0:1, :], in1=B["rt_logk"][0:1, :], op=ALU.subtract)
        v.inc(sems["vp"], 1)
        t.wait(sems["vp"], sems["vp"].final)
        t.matmul(B["pb"][:, :], onesr[0:1, 0:128], B["nan2"][0:1, 0:2], start=True, stop=True)
        t.inc(sems["pv2"], 1)
        v.wait(sems["pv2"], sems["pv2"].final)
        v.tensor_scalar(out=B["na2"][:, :], in0=B["pb"][:, :], scalar1=1.0, scalar2=None, op0=ALU.mult)
        v.barrier()

    for r in range(2):
        c0, c1 = 128 * r, 128 * (r + 1)
        v.tensor_scalar(out=B["satg"][:, c0:c1], in0=B["s2"][:, c0:c1], scalar1=B["na2"][:, r:r+1], scalar2=None, op0=ALU.subtract)
        v.barrier()
        v.tensor_scalar(out=B["satg"][:, c0:c1], in0=B["satg"][:, c0:c1], scalar1=0.0, scalar2=None, op0=ALU.is_ge)
        v.barrier()
        v.reduce_sum(out=B["rowsg"][:, r:r+1], in_=B["satg"][:, c0:c1], axis=AX.X)
        v.tensor_scalar(out=B["uo16"][0:16, c0:c1], in0=B["so16"][0:16, c0:c1], scalar1=B["na2"][0:16, r:r+1], scalar2=None, op0=ALU.subtract)
        v.barrier()
        v.tensor_scalar(out=B["sato"][0:16, c0:c1], in0=B["uo16"][0:16, c0:c1], scalar1=0.0, scalar2=None, op0=ALU.is_ge)
        v.memset(B["zer16"][0:16, c0:c1], 0.0)
        v.barrier()
        v.tensor_tensor_scan(out=B["incl"][0:16, c0:c1], data0=B["sato"][0:16, c0:c1], data1=B["zer16"][0:16, c0:c1], initial=0.0, op0=ALU.add, op1=ALU.add)
        v.barrier()
        v.tensor_scalar(out=B["rowtot"][0:16, r:r+1], in0=B["incl"][0:16, c1-1:c1], scalar1=1.0, scalar2=None, op0=ALU.mult)
        v.barrier()
    v.sinc(sems["vs"], 1)
    t.wait(sems["vs"], sems["vs"].final)
    t.matmul(B["p16"][:, :], B["rt_L16"][0:16, 0:16], B["rowtot"][0:16, 0:2], start=True, stop=True)
    t.inc(sems["pv"], 1)
    v.wait(sems["pv"], sems["pv"].final)
    v.tensor_scalar(out=B["offsL"][0:16, :], in0=B["p16"][0:16, :], scalar1=1.0, scalar2=None, op0=ALU.mult)
    v.inc(sems["vp"], 1)
    t.wait(sems["vp"], sems["vp"].final)
    t.matmul(B["p16"][:, :], B["rt_ind16"][:, 0:16], B["rowsg"][:, 0:2], start=True, stop=True)
    t.inc(sems["pv"], 1)
    v.wait(sems["pv"], sems["pv"].final)
    v.tensor_scalar(out=B["og16"][0:16, :], in0=B["p16"][0:16, :], scalar1=1.0, scalar2=None, op0=ALU.mult)
    v.inc(sems["vp"], 1)
    t.wait(sems["vp"], sems["vp"].final)
    t.matmul(B["p11"][:, :], ones128[0:16, 0:1], B["rowtot"][0:16, 0:2], start=True, stop=True)
    t.inc(sems["pv"], 1)
    v.wait(sems["pv"], sems["pv"].final)
    v.tensor_scalar(out=B["tot"][0:1, :], in0=B["p11"][0:1, :], scalar1=1.0, scalar2=None, op0=ALU.mult)
    v.barrier()
    v.tensor_tensor(out=B["cnt"][0:1, :], in0=B["rt_Krow"][0:1, :], in1=B["og16"][0:1, :], op=ALU.subtract)
    v.barrier()
    v.tensor_scalar(out=B["cnt"][0:1, :], in0=B["cnt"][0:1, :], scalar1=0.0, scalar2=None, op0=ALU.max)
    v.barrier()
    v.tensor_tensor(out=B["cnt"][0:1, :], in0=B["cnt"][0:1, :], in1=B["tot"][0:1, :], op=ALU.min)
    v.inc(sems["vp"], 1)
    t.wait(sems["vp"], sems["vp"].final)
    t.matmul(B["pb"][:, :], onesr[0:1, 0:128], B["og16"][0:1, 0:2], start=True, stop=True)
    t.inc(sems["pv"], 1)
    v.wait(sems["pv"], sems["pv"].final)
    v.tensor_scalar(out=B["ogb"][:, :], in0=B["pb"][:, :], scalar1=1.0, scalar2=None, op0=ALU.mult)
    v.inc(sems["vp"], 1)
    t.wait(sems["vp"], sems["vp"].final)
    t.matmul(B["pb"][:, :], onesr[0:1, 0:128], B["cnt"][0:1, 0:2], start=True, stop=True)
    t.inc(sems["pv"], 1)
    v.wait(sems["pv"], sems["pv"].final)
    v.tensor_scalar(out=B["cntb"][:, :], in0=B["pb"][:, :], scalar1=1.0, scalar2=None, op0=ALU.mult)
    v.barrier()
    for r, (K, rpT) in enumerate([(NQ, "rpT_q"), (NKV, "rpT_kv")]):
        c0, c1 = 128 * r, 128 * (r + 1)
        v.tensor_tensor(out=B["lrank"][0:16, c0:c1], in0=B["incl"][0:16, c0:c1], in1=B["sato"][0:16, c0:c1], op=ALU.subtract)
        v.barrier()
        v.tensor_scalar(out=B["lrank"][0:16, c0:c1], in0=B["lrank"][0:16, c0:c1], scalar1=B["offsL"][0:16, r:r+1], scalar2=None, op0=ALU.add)
        v.barrier()
        v.tensor_scalar(out=B["sel"][0:16, c0:c1], in0=B["lrank"][0:16, c0:c1], scalar1=B["og16"][0:16, r:r+1], scalar2=float(K), op0=ALU.add, op1=ALU.is_lt)
        v.barrier()
        v.tensor_tensor(out=B["sel"][0:16, c0:c1], in0=B["sel"][0:16, c0:c1], in1=B["sato"][0:16, c0:c1], op=ALU.mult)
        v.barrier()
        v.tensor_scalar(out=B["rankp"][0:16, c0:c1], in0=B["lrank"][0:16, c0:c1], scalar1=1.0, scalar2=None, op0=ALU.add)
        v.barrier()
        v.tensor_tensor(out=B["rankp"][0:16, c0:c1], in0=B["rankp"][0:16, c0:c1], in1=B["sel"][0:16, c0:c1], op=ALU.mult)
        v.barrier()
        v.tensor_scalar(out=B["rankp"][0:16, c0:c1], in0=B["rankp"][0:16, c0:c1], scalar1=-1.0, scalar2=None, op0=ALU.add)
        v.inc(sems["vp"], 1)
        t.wait(sems["vp"], sems["vp"].final)
        t.wait(sems["pv"], sems["pv"].final)
        t.transpose(B["ptT"][0:128, 0:16], B["rankp"][0:16, c0:c1], ident[0:16, 0:16])
        t.inc(sems["ps"], 1)
        v.wait(sems["ps"], sems["ps"].final)
        v.tensor_scalar(out=B[rpT][:, 0:16], in0=B["ptT"][0:128, 0:16], scalar1=1.0, scalar2=None, op0=ALU.mult)
        v.barrier()
        v.sinc(sems["pv"], 1)
    v.sinc(sems["done"], 1)


"""Megakernel for mk_seq_preln_d1 (NA token mixer + light/heavy routed 'ffn').

Per-core token layout (core cid, t0 = 2048*cid):
  staged x0: lam in [0, 2304) <-> global tokens [t0-76, t0+2228), zero-padded OOB.
  x2 grid:   lam in [12, 2188) = 17 tiles of 128 (NA queries, proj rows, light grid).
  own final: lam in [76, 2124).
Block A runs in two column halves to bound SBUF. x2 and x2+light+null go via DRAM.
"""

NT0 = 18
NT = 17
LAM0 = 12
D = 1024
NAB = -1e9


def na_windows():
    return [(max(128 * b - 6, LAM0), min(128 * b + 134, LAM0 + 2176)) for b in range(NT0)]


def light_windows():
    return [(max(LAM0 + 128 * b - 64, LAM0), min(LAM0 + 128 * b + 192, LAM0 + 2176)) for b in range(NT)]


def na_bias_host(core_id, rpb):
    t0 = 2048 * core_id
    out = np.full((NT0, 16, 128, 140), NAB, np.float32)
    L = 16384
    kkv = np.arange(128)
    for b, (q0, q1) in enumerate(na_windows()):
        gk = t0 - 76 + 128 * b + kkv
        for qq in range(q1 - q0):
            gq = t0 - 76 + q0 + qq
            if 0 <= gq < L:
                ni = min(max(gq - 6, 0), L - 13)
                valid = (gk >= ni) & (gk <= ni + 12) & (gk >= 0) & (gk < L)
                if valid.any():
                    idx = 12 + gk - gq
                    for h in range(16):
                        out[b, h, valid, qq] = rpb[h, idx[valid]]
            else:
                ks = (q0 + qq) - 128 * b
                if 0 <= ks < 128:
                    out[b, :, ks, qq] = 0.0
    return out


def light_mask_host(core_id):
    t0 = 2048 * core_id
    out = np.full((NT, 128, 256), NAB, np.float32)
    L = 16384
    kkv = np.arange(128)
    for b, (q0, q1) in enumerate(light_windows()):
        gk = t0 - 76 + LAM0 + 128 * b + kkv
        wk = gk // 64
        kval = (gk >= 0) & (gk < L)
        for qq in range(q1 - q0):
            gq = t0 - 76 + q0 + qq
            if 0 <= gq < L:
                valid = kval & (np.abs(wk - gq // 64) <= 1)
                out[b, valid, qq] = 0.0
            else:
                ks = (q0 + qq) - (LAM0 + 128 * b)
                if 0 <= ks < 128:
                    out[b, ks, qq] = 0.0
    return out


def host_inputs(core_id, z):
    t0 = 2048 * core_id
    m = dict(host_consts(core_id))
    x = np.asarray(z["L0_attn_x"]).reshape(16384, 1024)
    st = np.zeros((2304, 1024), np.float32)
    lo, hi = t0 - 76, t0 + 2228
    slo, shi = max(lo, 0), min(hi, 16384)
    st[slo - lo:shi - lo] = x[slo:shi]
    m["x0"] = st
    m["ident"] = np.eye(128, dtype=np.float32)
    m["w_qkv"] = np.ascontiguousarray(z["L0_attn_w_qkv"])
    m["b_qkv"] = np.asarray(z["L0_attn_b_qkv"]).reshape(1, 3072)
    m["w_proj"] = np.ascontiguousarray(z["L0_attn_w_proj"])
    m["b_proj"] = np.asarray(z["L0_attn_b_proj"]).reshape(1, 1024)
    m["ln_g"] = np.asarray(z["L0_ffn_ln_g"]).reshape(1, 1024)
    m["ln_b"] = np.asarray(z["L0_ffn_ln_b"]).reshape(1, 1024)
    m["w_qkv_l"] = np.ascontiguousarray(z["L0_ffn_w_qkv_l"])
    m["w_out_l"] = np.ascontiguousarray(z["L0_ffn_w_out_l"])
    m["null_q"] = np.asarray(z["L0_ffn_null_q"]).reshape(1, 1024)
    m["rt2"] = np.ascontiguousarray(np.stack([np.asarray(z["L0_ffn_rt_q"]).reshape(1024),
                                              np.asarray(z["L0_ffn_rt_kv"]).reshape(1024)], 0))
    m["rms_g"] = np.asarray(z["L0_ffn_rms_g"]).reshape(1, 1024)
    m["w_q_h"] = np.ascontiguousarray(z["L0_ffn_w_q_h"])
    m["w_kv_h"] = np.ascontiguousarray(np.asarray(z["L0_ffn_w_kv_h"]).reshape(1024, 1024))
    m["w_out_h"] = np.ascontiguousarray(z["L0_ffn_w_out_h"])
    nk = np.asarray(z["L0_ffn_null_kv"])
    nvx = np.zeros((1, 520), np.float32)
    for h in range(8):
        nvx[0, 65 * h + 64] = 1.0
    if core_id == 0:
        m["nullk"] = np.ascontiguousarray(nk[0].T)
        for h in range(8):
            nvx[0, 65 * h:65 * h + 64] = nk[1, h]
        m["nullmask"] = np.zeros((1, 1), np.float32)
    else:
        m["nullk"] = np.zeros((64, 8), np.float32)
        m["nullmask"] = np.full((1, 1), NAB, np.float32)
    m["nullvx"] = nvx
    m["na_bias"] = na_bias_host(core_id, np.asarray(z["L0_attn_rpb"]))
    m["l_mask"] = light_mask_host(core_id)
    m["iota512"] = np.broadcast_to(np.arange(512, dtype=np.float32), (128, 512)).copy()
    m["iotap"] = (np.arange(128, dtype=np.float32)[:, None] + 128.0 * np.arange(4)[None, :]).astype(np.float32)
    ep = np.zeros((16, 1024), np.float32)
    for c in range(8):
        for pp in range(128):
            ep[2 * c + pp // 64, 128 * c + pp] = 1.0
    m["Epair"] = ep
    ch = np.zeros((128, 31), np.float32)
    ch[:, 15] = 1.0
    m["chones"] = ch
    return m


def build(dbg=False, upto=99):
    p = Prog()
    dr, sb, ps = p.dram, p.sbuf, p.psum
    io = {}
    for nm, shape in [("x0", [2304, 1024]), ("ident", [128, 128]), ("w_qkv", [1024, 3072]),
                      ("b_qkv", [1, 3072]), ("w_proj", [1024, 1024]), ("b_proj", [1, 1024]),
                      ("ln_g", [1, 1024]), ("ln_b", [1, 1024]), ("w_qkv_l", [1024, 1536]),
                      ("w_out_l", [512, 1024]), ("null_q", [1, 1024]), ("rt2", [2, 1024]),
                      ("rms_g", [1, 1024]), ("w_q_h", [1024, 512]), ("w_kv_h", [1024, 1024]),
                      ("w_out_h", [512, 1024]), ("nullk", [64, 8]), ("nullvx", [1, 520]),
                      ("nullmask", [1, 1]), ("na_bias", [NT0, 16, 128, 140]),
                      ("l_mask", [NT, 128, 256]), ("iota512", [128, 512]),
                      ("iotap", [128, 4]), ("Epair", [16, 1024]), ("chones", [128, 31])]:
        io[nm] = dr(nm, shape, F32, kind="ExternalInput")
    io["out"] = dr("out", [2048, 1024], F32, kind="ExternalOutput")
    dbk = dict(kind="ExternalOutput") if dbg else {}
    io["d_x2"] = dr("d_x2", [2176, 1024], F32, **dbk)
    io["d_xl"] = dr("d_xl", [2176, 1024], F32, **dbk)
    if dbg:
        io["d_dump"] = dr("d_dump", [128, 142080], U8, kind="ExternalOutput")
        io["d_dump2"] = dr("d_dump2", [128, 20480], U8, kind="ExternalOutput")
    io["d_sown"] = dr("d_sown", [2, 2048], F32)
    io["d_stile"] = dr("d_stile", [2, 17 * 128], F32)
    io["d_sag"] = dr("d_sag", [8, 2, 2048], F32)
    io["d_sall"] = dr("d_sall", [2, 16384], F32)
    io["d_qown"] = dr("d_qown", [512, QPAD], BF16)
    io["d_qag"] = dr("d_qag", [8, 512, QPAD], BF16)
    io["d_part"] = dr("d_part", [2048, 520], F32)
    io["d_rs"] = dr("d_rs", [256, 520], F32)

    identf = sb("identf", [128, 128], F32)
    onesr = sb("onesr", [1, 128], F32)
    onesrb = sb("onesrb", [1, 128], BF16)
    ones128 = sb("ones128", [128, 1], F32)
    iota512 = sb("iota512_s", [128, 512], F32)
    iotap = sb("iotap_s", [128, 4], F32)
    Epair = sb("Epair_s", [16, 1024], F32)
    nullqb = sb("nullqb", [128, 1024], F32)
    rtrep = sb("rtrep", [128, 2048], F32)
    bcol_qkv = sb("bcol_qkv", [128, 16], F32)
    brow_v = sb("brow_v", [1, 1024], BF16)
    brow_p = sb("brow_p", [1, 1024], BF16)
    grow = sb("grow", [128, 8], F32)
    brow = sb("brow", [128, 8], F32)
    grmsc = sb("grmsc", [128, 8], F32)
    nullk = sb("nullk_s", [64, 8], BF16)
    nullvx = sb("nullvx_s", [1, 520], BF16)
    nullmask = sb("nullmask_s", [1, 1], F32)
    nab2 = sb("nab2", [128, 2240], BF16)
    lmk2 = sb("lmk2", [128, 256], BF16)

    HW = 1176
    AR = sb("AR", [128, 142080], U8)

    def view(off_b, cols, dt):
        szb = cols * (4 if dt == F32 else 2)
        assert off_b % 64 == 0 and off_b + szb <= 142080, (off_b, szb)
        return AR[:, off_b:off_b + szb].bitcast(dt)

    xn1T = view(0, 8 * 2304, BF16)
    qTh = view(36864, 8 * HW, BF16)
    kTh = view(55680, 8 * HW, BF16)
    vxh = view(74496, 10 * 1040, BF16)
    oT = view(95296, 8 * 2176, BF16)
    den = view(130112, 2176, F32)
    xnlT = view(0, 8 * 2176, BF16)
    qlT = view(36864, 4 * 2176, BF16)
    klT = view(55680, 4 * 2176, BF16)
    vlx = view(74496, 17 * 520, BF16)
    olT = view(95296, 4 * 2176, BF16)
    denl = view(112704, 2176, F32)
    rbf = view(0, 16 * 1024, BF16)
    ohq = view(32768, 16 * QPAD, BF16)
    ohkv = view(40960, 16 * KVPAD, BF16)
    cTq = view(57344, 8 * QPAD, BF16)
    cTkv = view(61440, 8 * KVPAD, BF16)
    qTo = view(69632, 4 * QPAD, BF16)
    kvTo = view(71680, 8 * KVPAD, BF16)
    vxo = view(79872, 4 * 520, BF16)
    qTall = view(32768, 8 * 2048, BF16)
    pTh = view(100416, 4 * 2048, BF16)
    pThn = view(116800, 2048, BF16)[0:1, :]
    oHs = view(120896, 2048, F32)[0:65, :]
    partsb = view(129088, 520, F32)
    rsb = view(131200, 2 * 520, F32)
    recH = view(135360, 16, F32)
    oTh = view(135424, 4 * 256, BF16)
    Hrow = view(137472, 2 * 1024, BF16)
    ohqT = view(141568, 2 * 128, BF16)

    wS = sb("wS", [128, 2 * 1024], BF16)
    wR = sb("wR", [128, 8 * 1024], BF16)
    xt2 = sb("xt2", [128, 2 * 1024], F32)
    prb = sb("prb", [128, 256], BF16)
    sq = sb("sq", [128, 1024], F32)
    wSf = sb("wSf", [128, 1024], F32)
    acc1 = sb("acc1", [128, 1], F32)
    acc2 = sb("acc2", [128, 1], F32)
    acc3 = sb("acc3", [128, 1], F32)
    acc4 = sb("acc4", [128, 1], F32)
    sownb = sb("sownb", [128, 2 * NT], F32)
    kvmask = sb("kvmask", [128, 4], F32)

    pk0 = ps("pk0", [128, 512], F32)
    pk = [pk0, pk0]
    pw = ps("pw", [128, 1024], F32)
    pna = ps("pna", [128, 512], F32)
    pav = [ps(f"pav{i}", [65, 512], F32) for i in range(2)]
    ptb = ps("ptb", [128, 256], F32)
    ptbb = ps("ptbb", [128, 128], BF16)

    RB = alloc(p, psums={
        "pr12": pk0[0:1, 0:2], "pb": pk0[0:128, 4:6], "p16": pw[0:16, 0:2],
        "p11": pw[0:1, 512:514], "ptT": pna[0:128, 0:16]})

    identb = sb("identb", [128, 128], BF16)
    bcolq8 = sb("bcolq8", [128, 8], F32)
    chb = sb("chb", [128, 31], BF16)

    sems = {nm: p.sem(nm) for nm in
            ["cl", "clg", "ld2", "vs", "sp", "ps", "sv", "vp", "pv", "pv2", "done", "dmp",
             "cv", "wcv", "wlf", "nbc",
             "wlr", "qld", "stl", "sta", "stb",
             "xld", "wld", "vt", "tv", "ts", "st2", "sy", "yv", "vy", "ty", "yt", "cc", "ob", "gl"]}
    sm = sems
    wins = na_windows()
    lwins = light_windows()

    @p.body
    def _(E):
        v, s, t, g, y = E.v, E.s, E.t, E.g, E.y

        if dbg:
            ARf = AR.bitcast(F32)
            for ch in range(5):
                c0 = 7104 * ch
                v.memset(ARf[:, c0:min(c0 + 7104, 35520)], 0.0)
            v.memset(wS[:, :], 0.0)
            v.memset(wR[:, :], 0.0)
            v.memset(wSf[:, :], 0.0)
            v.barrier()
            v.sinc(sm["dmp"], 1)

        def dump():
            if not dbg:
                return
            for c_ in [v, s, t]:
                c_.barrier()
                c_.sinc(sm["dmp"], 1)
            y.wait(sm["dmp"], sm["dmp"].final)
            for s_ in p.sems:
                if hasattr(s_, "h") and s_.final > 0 and s_.name not in ("boot", "fin"):
                    y.wait(s_, s_.final)
            y.dma_start(out=io["d_dump"][:, :], in_=AR[:, :])
            y.inc(sm["dmp"], 16)
            y.dma_start(out=io["d_dump2"][:, 0:4096].bitcast(BF16), in_=wS[:, :])
            y.inc(sm["dmp"], 16)
            y.dma_start(out=io["d_dump2"][:, 4096:20480].bitcast(BF16), in_=wR[:, :])
            y.inc(sm["dmp"], 16)
            y.wait_cur(sm["dmp"])

        def stage_waits(full=False):
            nms = ("wlf", "wlr", "wld", "wcv", "yv", "nbc", "vs", "sv") if full else ("wlf", "wlr", "wld", "wcv", "yv", "nbc")
            for nm_ in nms:
                y.wait_cur(sm[nm_])

        def wr_load(chunks):
            for (col, wdt, srcap) in chunks:
                stage_waits()
                y.dma_start(out=wSf[:, 0:wdt], in_=srcap)
                y.inc(sm["wlf"], 16)
                s.wait_cur(sm["wlf"])
                s.activation(out=wR[:, col:col + wdt], in_=wSf[:, 0:wdt], func=AF.Identity, bias=0.0, scale=1.0)
                s.inc(sm["wlr"], 1)

        if upto <= -5:
            dump()
            return
        y.wait_cur(sm["dmp"])
        y.dma_start(out=identf[:, :], in_=io["ident"][:, :]); y.inc(sm["cl"], 16)
        y.dma_start(out=iota512[:, :], in_=io["iota512"][:, :]); y.inc(sm["cl"], 16)
        y.dma_start(out=iotap[:, :], in_=io["iotap"][:, :]); y.inc(sm["cl"], 16)
        y.dma_start(out=Epair[0:16, :], in_=io["Epair"][:, :]); y.inc(sm["cl"], 16)
        y.dma_start(out=wSf[0:1, :], in_=io["null_q"][:, :]); y.inc(sm["cl"], 16)
        y.dma_start(out=nullmask[0:1, :], in_=io["nullmask"][:, :]); y.inc(sm["cl"], 16)
        y.dma_start(out=xt2[0:64, 0:8], in_=io["nullk"][:, :]); y.inc(sm["cl"], 16)
        y.dma_start(out=xt2[0:1, 8:528], in_=io["nullvx"][:, :]); y.inc(sm["cl"], 16)
        y.dma_start(out=xt2[0:1, 1024:2048], in_=io["b_proj"][0:1, :]); y.inc(sm["cl"], 16)
        y.dma_start(out=sq[0:1, 0:1024], in_=io["b_qkv"][0:1, 2048:3072]); y.inc(sm["cl"], 16)
        v.memset(onesr[0:1, :], 1.0)
        v.memset(ones128[:, :], 1.0)
        v.memset(onesrb[0:1, :], 1.0)
        v.barrier()
        v.sinc(sm["cl"], 1)
        if upto <= -4.8:
            dump()
            return
        emit_const_loads(E, p, RB, sm["cl"])
        t.wait_cur(sm["cl"])
        v.wait_cur(sm["cl"])
        s.wait_cur(sm["cl"])
        s.activation(out=identb[:, :], in_=identf[:, :], func=AF.Identity, bias=0.0, scale=1.0)
        s.activation(out=nullk[0:64, :], in_=xt2[0:64, 0:8], func=AF.Identity, bias=0.0, scale=1.0)
        s.activation(out=nullvx[0:1, :], in_=xt2[0:1, 8:528], func=AF.Identity, bias=0.0, scale=1.0)
        s.activation(out=brow_p[0:1, :], in_=xt2[0:1, 1024:2048], func=AF.Identity, bias=0.0, scale=1.0)
        s.activation(out=brow_v[0:1, :], in_=sq[0:1, 0:1024], func=AF.Identity, bias=0.0, scale=1.0)
        s.barrier()
        s.sinc(sm["cv"], 1)
        v.tensor_scalar(out=chb[:, :], in0=iota512[:, 0:31], scalar1=15.0, scalar2=None, op0=ALU.is_equal)
        v.barrier()
        v.sinc(sm["cv"], 1)
        for c_ in [v, s, t, y]:
            c_.wait_cur(sm["cv"])

        if upto <= -4.7:
            dump()
            return
        for n in range(2):
            t.matmul(pw[:, 512 * n:512 * n + 512], onesr[0:1, 0:128], wSf[0:1, 512 * n:512 * n + 512], start=True, stop=True)
        t.inc(sm["tv"], 1)
        v.wait_cur(sm["tv"])
        v.tensor_scalar(out=nullqb[:, :], in0=pw[:, :], scalar1=1.0, scalar2=None, op0=ALU.mult)
        v.barrier(); v.sinc(sm["vt"], 1)
        if upto <= -4.3:
            dump()
            return
        for r in range(2):
            y.wait_cur(sm["vt"])
            y.wait_cur(sm["yt"])
            y.dma_start(out=wSf[0:1, :], in_=io["rt2"][r:r + 1, :]); y.inc(sm["yt"], 16)
            t.wait_cur(sm["yt"])
            t.wait_cur(sm["vt"])
            for n in range(2):
                t.matmul(pw[:, 512 * n:512 * n + 512], onesr[0:1, 0:128], wSf[0:1, 512 * n:512 * n + 512], start=True, stop=True)
            t.inc(sm["tv"], 1)
            v.wait_cur(sm["tv"])
            v.tensor_scalar(out=rtrep[:, 1024 * r:1024 * r + 1024], in0=pw[:, :], scalar1=1.0, scalar2=None, op0=ALU.mult)
            v.barrier(); v.sinc(sm["vt"], 1)

        if upto <= -4:
            dump()
            return

        def brow_to_cols(src_ap_row, dst, nchunk):
            for c in range(nchunk):
                y.wait_cur(sm["vt"])
                y.wait_cur(sm["yt"])
                y.dma_start(out=sq[0:1, 0:128], in_=src_ap_row[0:1, 128 * c:128 * c + 128])
                y.inc(sm["yt"], 16)
                t.wait_cur(sm["yt"])
                t.wait_cur(sm["vt"])
                t.transpose(ptb[0:128, 0:1], sq[0:1, 0:128], identf[0:1, 0:1])
                t.inc(sm["tv"], 1)
                v.wait_cur(sm["tv"])
                v.tensor_scalar(out=dst[:, c:c + 1], in0=ptb[0:128, 0:1], scalar1=1.0, scalar2=None, op0=ALU.mult)
                v.barrier(); v.sinc(sm["vt"], 1)
        brow_to_cols(io["b_qkv"], bcol_qkv, 16)
        brow_to_cols(io["ln_g"], grow, 8)
        brow_to_cols(io["ln_b"], brow, 8)
        brow_to_cols(io["rms_g"], grmsc, 8)
        v.tensor_scalar(out=bcolq8[:, 0:8], in0=bcol_qkv[:, 0:8], scalar1=0.125, scalar2=None, op0=ALU.mult)
        v.barrier()
        v.sinc(sm["vt"], 1)
        t.wait_cur(sm["vt"])
        s.e.wait_ge(sm["vt"].h, sm["vt"].final)

        def ln_tile(xt_ap):
            v.reduce_sum(out=acc1[:, :], in_=xt_ap, axis=AX.X)
            v.barrier()
            v.tensor_scalar(out=acc1[:, :], in0=acc1[:, :], scalar1=1.0 / D, scalar2=None, op0=ALU.mult)
            v.inc(sm["vs"], 1)
            s.wait(sm["vs"], sm["vs"].final)
            s.activation(out=sq[:, :], in_=xt_ap, func=AF.Square, bias=0.0, scale=1.0, accum_out=acc2[:, :])
            s.inc(sm["sv"], 1)
            v.wait(sm["sv"], sm["sv"].final)
            v.tensor_tensor(out=acc3[:, :], in0=acc1[:, :], in1=acc1[:, :], op=ALU.mult)
            v.barrier()
            v.scalar_tensor_tensor(out=acc3[:, :], in0=acc2[:, :], scalar=1.0 / D, in1=acc3[:, :], op0=ALU.mult, op1=ALU.subtract)
            v.barrier()
            v.tensor_scalar(out=acc3[:, :], in0=acc3[:, :], scalar1=1e-5, scalar2=None, op0=ALU.add)
            v.inc(sm["vs"], 1)
            s.wait(sm["vs"], sm["vs"].final)
            s.activation(out=acc4[:, :], in_=acc3[:, :], func=AF.Sqrt, bias=0.0, scale=1.0)
            s.inc(sm["sv"], 1)
            v.wait(sm["sv"], sm["sv"].final)
            v.reciprocal(out=acc2[:, :], in_=acc4[:, :])
            v.barrier()
            v.tensor_tensor(out=acc4[:, :], in0=acc1[:, :], in1=acc2[:, :], op=ALU.mult)
            v.barrier()
            v.tensor_scalar(out=acc4[:, :], in0=acc4[:, :], scalar1=-1.0, scalar2=None, op0=ALU.mult)
            v.barrier()
            v.tensor_scalar(out=sq[:, :], in0=xt_ap, scalar1=acc2[:, 0:1], scalar2=acc4[:, 0:1], op0=ALU.mult, op1=ALU.add)
            v.barrier()

        def tp_tile(dst, colbase, colstride, scale_chunks=None):
            v.sinc(sm["vt"], 1)
            t.wait(sm["vt"], sm["vt"].final)
            for c in range(8):
                t.wait(sm["pv"], sm["pv"].final)
                t.transpose(ptb[0:128, 0:128], sq[:, 128 * c:128 * c + 128], identf[:, :])
                t.inc(sm["ps"], 1)
                v.wait(sm["ps"], sm["ps"].final)
                dsl = dst[:, colstride * c + colbase:colstride * c + colbase + 128]
                if scale_chunks is None:
                    v.tensor_scalar(out=dsl, in0=ptb[0:128, 0:128], scalar1=1.0, scalar2=None, op0=ALU.mult)
                else:
                    gcol, bcol2 = scale_chunks
                    v.tensor_scalar(out=dsl, in0=ptb[0:128, 0:128], scalar1=gcol[:, c:c + 1], scalar2=bcol2[:, c:c + 1], op0=ALU.mult, op1=ALU.add)
                v.barrier()
                v.sinc(sm["pv"], 1)

        if upto <= -3:
            dump()
            return
        for j in range(NT0):
            buf = j % 2
            y.wait(sm["vy"], max(0, sm["vy"].final - 1))
            y.wait_cur(sm["xld"])
            y.dma_start(out=xt2[:, 1024 * buf:1024 * buf + 1024], in_=io["x0"][128 * j:128 * j + 128, :])
            y.inc(sm["xld"], 16)
            v.wait_cur(sm["xld"])
            ln_tile(xt2[:, 1024 * buf:1024 * buf + 1024])
            v.sinc(sm["vy"], 1)
            tp_tile(xn1T, 128 * j, 2304)

        if upto <= -2:
            dump()
            return
        wr_load([(512 * k, 512, io["w_qkv"][128 * k:128 * k + 128, 2048:2560]) for k in range(8)]
                + [(4096 + 512 * k, 512, io["w_qkv"][128 * k:128 * k + 128, 2560:3072]) for k in range(8)])

        if upto <= -1.8:
            dump()
            return
        ts_after_m = {}

        def do_half(half):
            stop = False
            if half == 0:
                kcr = list(range(0, 9))
                qc0, qc1, kc0, kc1, vt0, vt1 = 0, 1158, 0, 1152, 0, 9
            else:
                kcr = list(range(9, NT0))
                qc0, qc1, kc0, kc1, vt0, vt1 = 1146, 2188, 1152, 2304, 9, NT0
            qoff, koff = qc0, kc0
            n_mi = 16
            if upto == -1.55:
                n_mi = 1
            elif upto == -1.54:
                n_mi = 8
            elif upto == -1.53:
                n_mi = 6
            elif upto == -1.52:
                n_mi = 5
            for mi in range(n_mi):
                wbuf = mi % 2
                key = (half, mi - 2)
                stage_waits()
                y.dma_start(out=wSf[:, :].rearrange("p (k m) -> p k m", k=8),
                            in_=io["w_qkv"].rearrange("(k p) n -> p k n", p=128)[:, :, 128 * mi:128 * mi + 128])
                y.inc(sm["wld"], 16)
                if key in ts_after_m:
                    s.wait(sm["ts"], ts_after_m[key])
                s.wait_cur(sm["wld"])
                s.activation(out=wS[:, 1024 * wbuf:1024 * wbuf + 1024], in_=wSf[:, :], func=AF.Identity, bias=0.0, scale=1.0)
                s.inc(sm["wcv"], 1)
                isq = mi < 8
                c0, c1 = (qc0, qc1) if isq else (kc0, kc1)
                dst = qTh if isq else kTh
                off = qoff if isq else koff
                hc = mi if isq else mi - 8
                ncols = c1 - c0
                nt = (ncols + 511) // 512
                for n in range(nt):
                    nn0 = c0 + 512 * n
                    w = min(512, c1 - nn0)
                    pp = pk[n % 2]
                    t.wait(sm["wcv"], sm["wcv"].final)
                    t.wait(sm["tv"], sm["tv"].final)
                    t.wait(sm["sp"], sm["sp"].final)
                    for k in range(8):
                        t.matmul(pp[:, 0:w], wS[:, 1024 * wbuf + 128 * k:1024 * wbuf + 128 * k + 128],
                                 xn1T[:, 2304 * k + nn0:2304 * k + nn0 + w], start=(k == 0), stop=(k == 7))
                    t.inc(sm["ts"], 1)
                    dsl = dst[:, HW * hc + (nn0 - off):HW * hc + (nn0 - off) + w]
                    if upto == -1.50 or upto == -1.49:
                        v.wait(sm["ts"], sm["ts"].final)
                        v.tensor_scalar(out=dsl, in0=pp[:, 0:w], scalar1=(0.125 if isq else 1.0),
                                        scalar2=bcolq8[:, hc:hc + 1] if isq else bcol_qkv[:, 8 + hc:8 + hc + 1],
                                        op0=ALU.mult, op1=ALU.add)
                        v.barrier()
                        v.sinc(sm["sp"], 1)
                    else:
                        s.wait(sm["ts"], sm["ts"].final)
                        if isq:
                            s.activation(out=dsl, in_=pp[:, 0:w], func=AF.Identity, bias=bcolq8[:, hc:hc + 1], scale=0.125)
                        else:
                            s.activation(out=dsl, in_=pp[:, 0:w], func=AF.Identity, bias=bcol_qkv[:, 8 + hc:8 + hc + 1], scale=1.0)
                        s.inc(sm["sp"], 1)
                ts_after_m[(half, mi)] = sm["ts"].final
            if upto <= -1.49 and half == 0:
                dump()
                return True
            for j in range(vt0, vt1):
                jl = j - vt0
                t.wait(sm["wlr"], sm["wlr"].final)
                t.wait(sm["sp"], sm["sp"].final)
                t.wait(sm["vt"], sm["vt"].final)
                for nh in range(2):
                    for k in range(8):
                        t.matmul(pw[:, 512 * nh:512 * nh + 512],
                                 xn1T[:, 2304 * k + 128 * j:2304 * k + 128 * j + 128],
                                 wR[:, 4096 * nh + 512 * k:4096 * nh + 512 * k + 512],
                                 start=(k == 0), stop=False)
                    t.matmul(pw[:, 512 * nh:512 * nh + 512], onesrb[0:1, 0:128],
                             brow_v[0:1, 512 * nh:512 * nh + 512], start=False, stop=True)
                t.inc(sm["ts"], 1)
                s.wait(sm["ts"], sm["ts"].final)
                s.copy(out=vxh[:, 1040 * jl:1040 * jl + 1040].rearrange("p (h i) -> p h i", h=16)[:, :, 0:64],
                       in_=pw[:, 0:1024])
                s.inc(sm["sp"], 1)
            if upto <= -1.4 and half == 0:
                dump()
                return True
            v.wait(sm["sp"], sm["sp"].final)
            for jl in range(vt1 - vt0):
                v.memset(vxh[:, 1040 * jl:1040 * jl + 1040].rearrange("p (h i) -> p h i", h=16)[:, :, 64:65], 1.0)
            v.barrier()
            v.sinc(sm["vt"], 1)
            t.wait(sm["vt"], sm["vt"].final)
            nwin = 99
            if upto <= -1.3:
                nwin = 0
            elif upto <= -1.2:
                nwin = 1
            if nwin < 99 and half == 0:
                stop = True
            for b in (kcr[:nwin] if half == 0 else kcr):
                nb = 0
                for (h0, h1) in ((0, 7), (7, 14), (14, 16)):
                    hw_ = 140 * (h1 - h0)
                    y.wait(sm["vs"], max(0, sm["vs"].final - (16 - h1)))
                    stage_waits()
                    y.dma_start(out=wSf[:, 0:hw_].rearrange("p (h w2) -> p h w2", h=h1 - h0),
                                in_=io["na_bias"][b, h0:h1, :, :].rearrange("h p w2 -> p h w2"))
                    y.inc(sm["yv"], 16)
                    s.wait_cur(sm["yv"])
                    s.activation(out=nab2[:, 140 * h0:140 * h1], in_=wSf[:, 0:hw_], func=AF.Identity, bias=0.0, scale=1.0)
                    s.inc(sm["nbc"], 1)
                q0, q1 = wins[b]
                w = q1 - q0
                ov = 12 if b > 0 else 0
                for h in range(16):
                    hc = h // 2
                    hp = (h % 2) * 64
                    pidx = h % 2
                    pb_ = 0
                    t.wait(sm["sp"], sm["sp"].final)
                    t.matmul(pna[0:128, 0:w], kTh[hp:hp + 64, HW * hc + (128 * b - koff):HW * hc + (128 * b - koff) + 128],
                             qTh[hp:hp + 64, HW * hc + (q0 - qoff):HW * hc + (q0 - qoff) + w], start=True, stop=True)
                    t.inc(sm["tv"], 1)
                    v.wait(sm["tv"], sm["tv"].final)
                    v.wait(sm["nbc"], sm["nbc"].final)
                    v.tensor_tensor(out=pna[0:128, 0:w], in0=pna[0:128, 0:w],
                                    in1=nab2[:, 2240 * nb + 140 * h:2240 * nb + 140 * h + w], op=ALU.add)
                    v.inc(sm["vs"], 1)
                    s.wait(sm["vs"], sm["vs"].final)
                    s.wait(sm["ts"], sm["ts"].final)
                    s.activation(out=prb[:, 0:w], in_=pna[0:128, 0:w], func=AF.Exp, bias=0.0, scale=1.0)
                    s.inc(sm["sp"], 1)
                    t.wait(sm["sp"], sm["sp"].final)
                    t.wait(sm["pv2"], sm["pv2"].final)
                    t.matmul(ptb[0:16, 0:w], chb[:, 15 - h:31 - h], prb[:, 0:w], start=(h == 0), stop=(h == 15))
                    t.matmul(pav[pidx][0:65, 0:w], vxh[:, 1040 * (b - vt0) + 65 * h:1040 * (b - vt0) + 65 * h + 65],
                             prb[:, 0:w], start=True, stop=True)
                    t.inc(sm["ts"], 1)
                    v.wait(sm["ts"], sm["ts"].final)
                    rp = hp
                    cb = q0 - LAM0
                    if ov > 0:
                        v.tensor_tensor(out=oT[rp:rp + 64, 2176 * hc + cb:2176 * hc + cb + ov],
                                        in0=oT[rp:rp + 64, 2176 * hc + cb:2176 * hc + cb + ov],
                                        in1=pav[pidx][0:64, 0:ov], op=ALU.add)
                    v.tensor_scalar(out=oT[rp:rp + 64, 2176 * hc + cb + ov:2176 * hc + cb + w],
                                    in0=pav[pidx][0:64, ov:w], scalar1=1.0, scalar2=None, op0=ALU.mult)
                    v.barrier()
                    v.sinc(sm["pv2"], 1)
                cb = q0 - LAM0
                if ov > 0:
                    v.tensor_tensor(out=den[0:16, cb:cb + ov], in0=den[0:16, cb:cb + ov],
                                    in1=ptb[0:16, 0:ov], op=ALU.add)
                v.tensor_scalar(out=den[0:16, cb + ov:cb + w], in0=ptb[0:16, ov:w], scalar1=1.0, scalar2=None, op0=ALU.mult)
                v.barrier()
                v.sinc(sm["pv2"], 1)
            if stop:
                dump()
                return True

        if do_half(0):
            return
        if upto <= -1:
            dump()
            return
        do_half(1)

        if upto <= 0:
            dump()
            return
        v.wait(sm["pv2"], sm["pv2"].final)
        v.reciprocal(out=den[0:16, :], in_=den[0:16, :])
        v.barrier()
        v.sinc(sm["vt"], 1)
        for c in range(8):
            for n in range(5):
                w = min(512, 2176 - 512 * n)
                t.wait(sm["vt"], sm["vt"].final)
                t.wait(sm["sp"], sm["sp"].final)
                t.wait(sm["vs"], sm["vs"].final)
                t.matmul(pna[0:128, 0:w], Epair[0:16, 128 * c:128 * c + 128], den[0:16, 512 * n:512 * n + w], start=True, stop=True)
                t.inc(sm["tv"], 1)
                v.wait(sm["tv"], sm["tv"].final)
                v.tensor_tensor(out=oT[:, 2176 * c + 512 * n:2176 * c + 512 * n + w],
                                in0=oT[:, 2176 * c + 512 * n:2176 * c + 512 * n + w],
                                in1=pna[0:128, 0:w], op=ALU.mult)
                v.barrier()
                v.sinc(sm["vs"], 1)

        wr_load([(1024 * c, 1024, io["w_proj"][128 * c:128 * c + 128, :]) for c in range(8)])
        for j in range(NT):
            buf = j % 2
            par = sm["sta"] if buf == 0 else sm["stb"]
            y.wait(sm["vy"], max(0, sm["vy"].final - 1))
            y.wait(par, par.final)
            y.wait_cur(sm["xld"])
            y.dma_start(out=xt2[:, 1024 * buf:1024 * buf + 1024], in_=io["x0"][LAM0 + 128 * j:LAM0 + 128 * j + 128, :])
            y.inc(sm["xld"], 16)
            t.wait(sm["wlr"], sm["wlr"].final)
            t.wait(sm["vs"], sm["vs"].final)
            t.wait(sm["vy"], sm["vy"].final)
            for nh in range(2):
                for k in range(8):
                    t.matmul(pw[:, 512 * nh:512 * nh + 512], oT[:, 2176 * k + 128 * j:2176 * k + 128 * j + 128],
                             wR[:, 1024 * k + 512 * nh:1024 * k + 512 * nh + 512], start=(k == 0), stop=False)
                t.matmul(pw[:, 512 * nh:512 * nh + 512], onesrb[0:1, 0:128], brow_p[0:1, 512 * nh:512 * nh + 512], start=False, stop=True)
            t.inc(sm["tv"], 1)
            v.wait(sm["tv"], sm["tv"].final)
            v.wait(sm["xld"], sm["xld"].final)
            v.tensor_tensor(out=xt2[:, 1024 * buf:1024 * buf + 1024], in0=xt2[:, 1024 * buf:1024 * buf + 1024], in1=pw[:, :], op=ALU.add)
            v.barrier()
            v.sinc(sm["vt"], 1)
            v.sinc(sm["vy"], 1)
            y.wait(sm["vy"], sm["vy"].final)
            y.dma_start(out=io["d_x2"][128 * j:128 * j + 128, :], in_=xt2[:, 1024 * buf:1024 * buf + 1024])
            y.inc(par, 16)
        y.wait_cur(sm["sta"])
        y.wait_cur(sm["stb"])
        y.sinc(sm["ob"], 1)

        if upto <= 1:
            dump()
            return
        for j in range(NT):
            buf = j % 2
            y.wait(sm["ob"], sm["ob"].final)
            y.wait(sm["vy"], max(0, sm["vy"].final - 1))
            y.wait_cur(sm["xld"])
            y.dma_start(out=xt2[:, 1024 * buf:1024 * buf + 1024], in_=io["d_x2"][128 * j:128 * j + 128, :])
            y.inc(sm["xld"], 16)
            v.wait(sm["xld"], sm["xld"].final)
            xt = xt2[:, 1024 * buf:1024 * buf + 1024]
            ln_tile(xt)
            v.sinc(sm["vy"], 1)
            for r in range(2):
                v.tensor_tensor(out=wSf[:, :], in0=sq[:, :], in1=rtrep[:, 1024 * r:1024 * r + 1024], op=ALU.mult)
                v.barrier()
                v.reduce_sum(out=sownb[:, NT * r + j:NT * r + j + 1], in_=wSf[:, :], axis=AX.X)
                v.barrier()
            v.reduce_sum(out=acc1[:, :], in_=sq[:, :], axis=AX.X)
            v.barrier()
            v.tensor_scalar(out=acc1[:, :], in0=acc1[:, :], scalar1=1.0 / D, scalar2=None, op0=ALU.mult)
            v.inc(sm["vs"], 1)
            s.wait(sm["vs"], sm["vs"].final)
            s.activation(out=wSf[:, :], in_=sq[:, :], func=AF.Square, bias=0.0, scale=1.0, accum_out=acc2[:, :])
            s.inc(sm["sv"], 1)
            v.wait(sm["sv"], sm["sv"].final)
            v.tensor_tensor(out=acc3[:, :], in0=acc1[:, :], in1=acc1[:, :], op=ALU.mult)
            v.barrier()
            v.scalar_tensor_tensor(out=acc3[:, :], in0=acc2[:, :], scalar=1.0 / D, in1=acc3[:, :], op0=ALU.mult, op1=ALU.subtract)
            v.barrier()
            v.tensor_scalar(out=acc3[:, :], in0=acc3[:, :], scalar1=1e-5, scalar2=None, op0=ALU.add)
            v.inc(sm["vs"], 1)
            s.wait(sm["vs"], sm["vs"].final)
            s.activation(out=acc4[:, :], in_=acc3[:, :], func=AF.Sqrt, bias=0.0, scale=1.0)
            s.inc(sm["sv"], 1)
            v.wait(sm["sv"], sm["sv"].final)
            v.reciprocal(out=acc2[:, :], in_=acc4[:, :])
            v.barrier()
            v.tensor_tensor(out=acc4[:, :], in0=acc1[:, :], in1=acc2[:, :], op=ALU.mult)
            v.barrier()
            v.tensor_scalar(out=acc4[:, :], in0=acc4[:, :], scalar1=-1.0, scalar2=None, op0=ALU.mult)
            v.barrier()
            v.tensor_scalar(out=sq[:, :], in0=sq[:, :], scalar1=acc2[:, 0:1], scalar2=acc4[:, 0:1], op0=ALU.mult, op1=ALU.add)
            v.barrier()
            tp_tile(xnlT, 128 * j, 2176, scale_chunks=(grow, brow))

        v.sinc(sm["vy"], 1)
        y.wait(sm["vy"], sm["vy"].final)
        for r in range(2):
            for j in range(NT):
                y.dma_start(out=io["d_stile"][r, 128 * j:128 * j + 128].rearrange("(pp o) -> pp o", o=1),
                            in_=sownb[:, NT * r + j:NT * r + j + 1])
                y.inc(sm["stl"], 16)
        y.wait_cur(sm["stl"])
        for r in range(2):
            y.dma_start(out=io["d_sown"][r, :].rearrange("(a o) -> a o", o=1) if False else io["d_sown"][r:r+1, :],
                        in_=io["d_stile"][r:r+1, 64:2112])
            y.inc(sm["st2"], 16)
        y.wait_cur(sm["st2"])
        g.e.wait_ge(sm["st2"].h, sm["st2"].final)
        g.collective_compute("AllGather", ALU.bypass, replica_groups=[list(range(8))],
                             ins=[io["d_sown"].opt()], outs=[io["d_sag"].opt()])
        g.inc(sm["cc"], 1)
        y.wait_cur(sm["cc"])
        y.dma_start(out=io["d_sall"].rearrange("r (c j) -> r c j", c=8),
                    in_=io["d_sag"].rearrange("c r j -> r c j"))
        y.inc(sm["st2"], 16)
        y.wait_cur(sm["st2"])
        y.sinc(sm["gl"], 1)
        for c_ in [v, s, t]:
            c_.wait(sm["gl"], sm["gl"].final)
        emit(E, p, RB, io["d_sall"], io["d_sown"], onesr, ones128, identf, sm)
        for c_ in [v, s, t, y]:
            c_.wait(sm["done"], sm["done"].final)

        if upto <= 2:
            dump()
            return
        for mi in range(8):
            wbuf = mi % 2
            key = ("L", mi - 2)
            stage_waits(full=(mi == 0))
            y.dma_start(out=wSf[:, :].rearrange("p (k m) -> p k m", k=8),
                        in_=io["w_qkv_l"].rearrange("(k p) n -> p k n", p=128)[:, :, 128 * mi:128 * mi + 128])
            y.inc(sm["wld"], 16)
            if key in ts_after_m:
                s.wait(sm["ts"], ts_after_m[key])
            s.wait_cur(sm["wld"])
            s.activation(out=wS[:, 1024 * wbuf:1024 * wbuf + 1024], in_=wSf[:, :], func=AF.Identity, bias=0.0, scale=1.0)
            s.inc(sm["wcv"], 1)
            isq = mi < 4
            dst = qlT if isq else klT
            hc = mi % 4
            for n in range(5):
                nn0 = 512 * n
                w = min(512, 2176 - nn0)
                pp = pk[n % 2]
                t.wait(sm["wcv"], sm["wcv"].final)
                t.wait(sm["sp"], sm["sp"].final)
                for k in range(8):
                    t.matmul(pp[:, 0:w], wS[:, 1024 * wbuf + 128 * k:1024 * wbuf + 128 * k + 128],
                             xnlT[:, 2176 * k + nn0:2176 * k + nn0 + w], start=(k == 0), stop=(k == 7))
                t.inc(sm["ts"], 1)
                s.wait(sm["ts"], sm["ts"].final)
                if isq:
                    s.activation(out=dst[:, 2176 * hc + nn0:2176 * hc + nn0 + w], in_=pp[:, 0:w], func=AF.Copy, bias=0.0, scale=0.125)
                else:
                    s.copy(out=dst[:, 2176 * hc + nn0:2176 * hc + nn0 + w], in_=pp[:, 0:w])
                s.inc(sm["sp"], 1)
            ts_after_m[("L", mi)] = sm["ts"].final
        wr_load([(4096 + 512 * k, 512, io["w_qkv_l"][128 * k:128 * k + 128, 1024:1536]) for k in range(8)])
        for j in range(NT):
            t.wait(sm["wlr"], sm["wlr"].final)
            t.wait(sm["sp"], sm["sp"].final)
            for k in range(8):
                t.matmul(pw[:, 0:512], xnlT[:, 2176 * k + 128 * j:2176 * k + 128 * j + 128],
                         wR[:, 4096 + 512 * k:4096 + 512 * k + 512], start=(k == 0), stop=(k == 7))
            t.inc(sm["ts"], 1)
            s.wait(sm["ts"], sm["ts"].final)
            s.copy(out=vlx[:, 520 * j:520 * j + 520].rearrange("p (h i) -> p h i", h=8)[:, :, 0:64], in_=pw[:, 0:512])
            s.inc(sm["sp"], 1)
        v.wait(sm["sp"], sm["sp"].final)
        for j in range(NT):
            v.memset(vlx[:, 520 * j:520 * j + 520].rearrange("p (h i) -> p h i", h=8)[:, :, 64:65], 1.0)
        v.barrier()
        v.sinc(sm["vt"], 1)
        t.wait(sm["vt"], sm["vt"].final)

        if upto <= 3:
            dump()
            return
        for b in range(NT):
            nb = 0
            y.wait(sm["vs"], sm["vs"].final)
            stage_waits()
            y.dma_start(out=wSf[:, 0:256], in_=io["l_mask"][b, :, :])
            y.inc(sm["yv"], 16)
            s.wait_cur(sm["yv"])
            s.activation(out=lmk2[:, 256 * nb:256 * nb + 256], in_=wSf[:, 0:256], func=AF.Identity, bias=0.0, scale=1.0)
            s.inc(sm["nbc"], 1)
            q0, q1 = lwins[b]
            w = q1 - q0
            ov = 128 if b > 0 else 0
            for h in range(8):
                hc = h // 2
                hp = (h % 2) * 64
                pidx = h % 2
                t.wait(sm["sp"], sm["sp"].final)
                t.matmul(pna[0:128, 0:w], klT[hp:hp + 64, 2176 * hc + 128 * b:2176 * hc + 128 * b + 128],
                         qlT[hp:hp + 64, 2176 * hc + (q0 - LAM0):2176 * hc + (q0 - LAM0) + w], start=True, stop=True)
                t.inc(sm["tv"], 1)
                v.wait(sm["tv"], sm["tv"].final)
                v.wait(sm["nbc"], sm["nbc"].final)
                v.tensor_tensor(out=pna[0:128, 0:w], in0=pna[0:128, 0:w], in1=lmk2[:, 256 * nb:256 * nb + w], op=ALU.add)
                v.inc(sm["vs"], 1)
                s.wait(sm["vs"], sm["vs"].final)
                s.wait(sm["ts"], sm["ts"].final)
                s.activation(out=prb[:, 0:w], in_=pna[0:128, 0:w], func=AF.Exp, bias=0.0, scale=1.0)
                s.inc(sm["sp"], 1)
                t.wait(sm["sp"], sm["sp"].final)
                t.wait(sm["pv2"], sm["pv2"].final)
                t.matmul(ptb[0:8, 0:w], chb[:, 15 - h:23 - h], prb[:, 0:w], start=(h == 0), stop=(h == 7))
                t.matmul(pav[pidx][0:65, 0:w], vlx[:, 520 * b + 65 * h:520 * b + 65 * h + 65],
                         prb[:, 0:w], start=True, stop=True)
                t.inc(sm["ts"], 1)
                v.wait(sm["ts"], sm["ts"].final)
                rp = hp
                cb = q0 - LAM0
                if ov > 0:
                    v.tensor_tensor(out=olT[rp:rp + 64, 2176 * hc + cb:2176 * hc + cb + ov],
                                    in0=olT[rp:rp + 64, 2176 * hc + cb:2176 * hc + cb + ov],
                                    in1=pav[pidx][0:64, 0:ov], op=ALU.add)
                v.tensor_scalar(out=olT[rp:rp + 64, 2176 * hc + cb + ov:2176 * hc + cb + w],
                                in0=pav[pidx][0:64, ov:w], scalar1=1.0, scalar2=None, op0=ALU.mult)
                v.barrier()
                v.sinc(sm["pv2"], 1)
            cb = q0 - LAM0
            if ov > 0:
                v.tensor_tensor(out=denl[0:8, cb:cb + ov], in0=denl[0:8, cb:cb + ov],
                                in1=ptb[0:8, 0:ov], op=ALU.add)
            v.tensor_scalar(out=denl[0:8, cb + ov:cb + w], in0=ptb[0:8, ov:w], scalar1=1.0, scalar2=None, op0=ALU.mult)
            v.barrier()
            v.sinc(sm["pv2"], 1)
        v.wait(sm["pv2"], sm["pv2"].final)
        v.reciprocal(out=denl[0:8, :], in_=denl[0:8, :])
        v.barrier()
        v.sinc(sm["vt"], 1)
        for c in range(4):
            for n in range(5):
                w = min(512, 2176 - 512 * n)
                t.wait(sm["vt"], sm["vt"].final)
                t.wait(sm["vs"], sm["vs"].final)
                t.matmul(pna[0:128, 0:w], Epair[0:8, 128 * c:128 * c + 128], denl[0:8, 512 * n:512 * n + w], start=True, stop=True)
                t.inc(sm["tv"], 1)
                v.wait(sm["tv"], sm["tv"].final)
                v.tensor_tensor(out=olT[:, 2176 * c + 512 * n:2176 * c + 512 * n + w],
                                in0=olT[:, 2176 * c + 512 * n:2176 * c + 512 * n + w],
                                in1=pna[0:128, 0:w], op=ALU.mult)
                v.barrier()
                v.sinc(sm["vs"], 1)
        wr_load([(1024 * c, 1024, io["w_out_l"][128 * c:128 * c + 128, :]) for c in range(4)])
        for j in range(NT):
            buf = j % 2
            par = sm["sta"] if buf == 0 else sm["stb"]
            y.wait(sm["vy"], max(0, sm["vy"].final - 1))
            y.wait(par, par.final)
            y.wait_cur(sm["xld"])
            y.dma_start(out=xt2[:, 1024 * buf:1024 * buf + 1024], in_=io["d_x2"][128 * j:128 * j + 128, :])
            y.inc(sm["xld"], 16)
            t.wait(sm["wlr"], sm["wlr"].final)
            t.wait(sm["vs"], sm["vs"].final)
            t.wait(sm["vy"], sm["vy"].final)
            for nh in range(2):
                for k in range(4):
                    t.matmul(pw[:, 512 * nh:512 * nh + 512], olT[:, 2176 * k + 128 * j:2176 * k + 128 * j + 128],
                             wR[:, 1024 * k + 512 * nh:1024 * k + 512 * nh + 512], start=(k == 0), stop=(k == 3))
            t.inc(sm["tv"], 1)
            v.wait(sm["tv"], sm["tv"].final)
            v.wait(sm["xld"], sm["xld"].final)
            v.tensor_tensor(out=xt2[:, 1024 * buf:1024 * buf + 1024], in0=xt2[:, 1024 * buf:1024 * buf + 1024], in1=pw[:, :], op=ALU.add)
            v.barrier()
            v.tensor_tensor(out=xt2[:, 1024 * buf:1024 * buf + 1024], in0=xt2[:, 1024 * buf:1024 * buf + 1024], in1=nullqb[:, :], op=ALU.add)
            v.barrier()
            v.sinc(sm["vy"], 1)
            y.wait(sm["vy"], sm["vy"].final)
            y.dma_start(out=io["d_xl"][128 * j:128 * j + 128, :], in_=xt2[:, 1024 * buf:1024 * buf + 1024])
            y.inc(par, 16)

        if upto <= 4:
            dump()
            return
        v.wait(sm["done"], sm["done"].final)
        for tau in range(16):
            v.tensor_scalar(out=ohq[:, QPAD * tau:QPAD * tau + QPAD], in0=iota512[:, 0:QPAD],
                            scalar1=RB["rpT_q"][:, tau:tau + 1], scalar2=None, op0=ALU.is_equal)
            v.tensor_scalar(out=ohkv[:, KVPAD * tau:KVPAD * tau + KVPAD], in0=iota512[:, 0:KVPAD],
                            scalar1=RB["rpT_kv"][:, tau:tau + 1], scalar2=None, op0=ALU.is_equal)
            v.barrier()
        y.wait_cur(sm["sta"])
        y.wait_cur(sm["stb"])
        for tau in range(16):
            buf = tau % 2
            y.wait(sm["vy"], max(0, sm["vy"].final - 1))
            y.wait_cur(sm["xld"])
            y.dma_start(out=xt2[:, 1024 * buf:1024 * buf + 1024], in_=io["d_x2"][64 + 128 * tau:64 + 128 * tau + 128, :])
            y.inc(sm["xld"], 16)
            v.wait(sm["xld"], sm["xld"].final)
            xt = xt2[:, 1024 * buf:1024 * buf + 1024]
            ln_tile(xt)
            v.tensor_scalar(out=acc1[:, :], in0=acc3[:, :], scalar1=-1e-5, scalar2=None, op0=ALU.add)
            v.barrier()
            v.reciprocal(out=acc2[:, :], in_=acc3[:, :])
            v.barrier()
            v.tensor_tensor(out=acc1[:, :], in0=acc1[:, :], in1=acc2[:, :], op=ALU.mult)
            v.barrier()
            v.tensor_scalar(out=acc1[:, :], in0=acc1[:, :], scalar1=1024.0, scalar2=1e-24, op0=ALU.mult, op1=ALU.max)
            v.inc(sm["vs"], 1)
            s.wait(sm["vs"], sm["vs"].final)
            s.activation(out=acc1[:, :], in_=acc1[:, :], func=AF.Sqrt, bias=0.0, scale=1.0)
            s.inc(sm["sv"], 1)
            v.wait(sm["sv"], sm["sv"].final)
            v.reciprocal(out=acc1[:, :], in_=acc1[:, :])
            v.barrier()
            v.tensor_scalar(out=acc1[:, :], in0=acc1[:, :], scalar1=32.0, scalar2=None, op0=ALU.mult)
            v.barrier()
            v.tensor_scalar(out=xn1T[:, 1024 * tau:1024 * tau + 1024], in0=sq[:, :], scalar1=acc1[:, 0:1], scalar2=None, op0=ALU.mult)
            v.barrier()
            v.sinc(sm["vy"], 1)
        v.sinc(sm["vt"], 1)
        for (oh, cT, PAD) in [(ohq, cTq, QPAD), (ohkv, cTkv, KVPAD)]:
            for dch in range(8):
                t.wait(sm["vt"], sm["vt"].final)
                t.wait(sm["sp"], sm["sp"].final)
                for tau in range(16):
                    t.matmul(pna[0:128, 0:PAD],
                             xn1T[:, 1024 * tau + 128 * dch:1024 * tau + 128 * dch + 128],
                             oh[:, PAD * tau:PAD * tau + PAD], start=(tau == 0), stop=(tau == 15))
                t.inc(sm["ts"], 1)
                s.wait(sm["ts"], sm["ts"].final)
                s.copy(out=cT[:, PAD * dch:PAD * dch + PAD], in_=pna[0:128, 0:PAD])
                s.inc(sm["sp"], 1)
        hfirst = [True]
        for (wname, dst, nm_, scaleq, cTsrc, PAD) in [("w_q_h", qTo, 4, True, cTq, QPAD),
                                                      ("w_kv_h", kvTo, 8, False, cTkv, KVPAD)]:
            for mi in range(nm_):
                y.wait(sm["ts"], sm["ts"].final)
                stage_waits(full=hfirst[0])
                hfirst[0] = False
                y.dma_start(out=wSf[:, :].rearrange("p (k m) -> p k m", k=8),
                            in_=io[wname].rearrange("(k p) n -> p k n", p=128)[:, :, 128 * mi:128 * mi + 128])
                y.inc(sm["wld"], 16)
                v.wait(sm["wld"], sm["wld"].final)
                v.wait(sm["ts"], sm["ts"].final)
                for k in range(8):
                    v.tensor_scalar(out=wS[:, 128 * k:128 * k + 128], in0=wSf[:, 128 * k:128 * k + 128],
                                    scalar1=grmsc[:, k:k + 1], scalar2=None, op0=ALU.mult)
                v.barrier()
                v.sinc(sm["vt"], 1)
                t.wait(sm["vt"], sm["vt"].final)
                t.wait(sm["sp"], sm["sp"].final)
                for k in range(8):
                    t.matmul(pna[0:128, 0:PAD], wS[:, 128 * k:128 * k + 128],
                             cTsrc[:, PAD * k:PAD * k + PAD], start=(k == 0), stop=(k == 7))
                t.inc(sm["ts"], 1)
                s.wait(sm["ts"], sm["ts"].final)
                if scaleq:
                    s.activation(out=dst[:, PAD * mi:PAD * mi + PAD], in_=pna[0:128, 0:PAD], func=AF.Copy, bias=0.0, scale=0.125)
                else:
                    s.copy(out=dst[:, PAD * mi:PAD * mi + PAD], in_=pna[0:128, 0:PAD])
                s.inc(sm["sp"], 1)
        t.wait(sm["sp"], sm["sp"].final)
        for h in range(8):
            for sg in range(4):
                t.wait(sm["pv"], sm["pv"].final)
                t.transpose(ptbb[0:128, 0:64], kvTo[64:128, KVPAD * h + 128 * sg:KVPAD * h + 128 * sg + 128], identb[64:128, 64:128])
                t.inc(sm["ps"], 1)
                v.wait(sm["ps"], sm["ps"].final)
                v.tensor_scalar(out=vxo[:, 520 * sg + 65 * h:520 * sg + 65 * h + 64], in0=ptbb[0:128, 0:64],
                                scalar1=1.0, scalar2=None, op0=ALU.mult)
                v.barrier()
                v.sinc(sm["pv"], 1)
        for sg in range(4):
            v.memset(vxo[:, 520 * sg:520 * sg + 520].rearrange("p (h i) -> p h i", h=8)[:, :, 64:65], 1.0)
        v.barrier()
        v.tensor_scalar(out=kvmask[:, :], in0=iotap[:, :], scalar1=RB["cntb"][:, 1:2], scalar2=-1e9, op0=ALU.is_ge, op1=ALU.mult)
        v.barrier()
        v.sinc(sm["vy"], 1)
        y.wait(sm["vy"], sm["vy"].final)
        y.wait(sm["sp"], sm["sp"].final)
        y.dma_start(out=io["d_qown"].rearrange("(c p) n -> p c n", p=128), in_=qTo[:, :].rearrange("p (c n) -> p c n", c=4))
        y.inc(sm["st2"], 16)
        y.wait_cur(sm["st2"])
        g.e.wait_ge(sm["st2"].h, sm["st2"].final)
        g.collective_compute("AllGather", ALU.bypass, replica_groups=[list(range(8))],
                             ins=[io["d_qown"].opt()], outs=[io["d_qag"].opt()])
        g.inc(sm["cc"], 1)
        y.wait_cur(sm["cc"])
        for h in range(8):
            y.dma_start(out=qTall[0:64, 2048 * h:2048 * h + 2048].rearrange("p (rk l) -> p rk l", rk=8),
                        in_=io["d_qag"][:, 64 * h:64 * h + 64, :].rearrange("rk p l -> p rk l"))
            y.inc(sm["qld"], 16)
        y.wait_cur(sm["qld"])
        y.sinc(sm["gl"], 1)
        for c_ in [v, s, t]:
            c_.wait(sm["gl"], sm["gl"].final)
        for h in range(8):
            hc = h // 2
            hp = (h % 2) * 64
            for sg in range(4):
                for qc in range(4):
                    t.wait(sm["sp"], sm["sp"].final)
                    t.matmul(pna[0:128, 0:512], kvTo[0:64, KVPAD * h + 128 * sg:KVPAD * h + 128 * sg + 128],
                             qTall[0:64, 2048 * h + 512 * qc:2048 * h + 512 * qc + 512],
                             start=True, stop=True)
                    t.inc(sm["tv"], 1)
                    v.wait(sm["tv"], sm["tv"].final)
                    v.tensor_scalar(out=pna[0:128, 0:512], in0=pna[0:128, 0:512], scalar1=kvmask[:, sg:sg + 1], scalar2=None, op0=ALU.add)
                    v.inc(sm["vs"], 1)
                    s.wait(sm["vs"], sm["vs"].final)
                    s.activation(out=pTh[:, 2048 * sg + 512 * qc:2048 * sg + 512 * qc + 512], in_=pna[0:128, 0:512], func=AF.Exp, bias=0.0, scale=1.0)
                    s.inc(sm["sp"], 1)
            for qc in range(4):
                t.wait(sm["sp"], sm["sp"].final)
                t.matmul(pna[0:1, 0:512], nullk[0:64, h:h + 1],
                         qTall[0:64, 2048 * h + 512 * qc:2048 * h + 512 * qc + 512], start=True, stop=True)
                t.inc(sm["tv"], 1)
                v.wait(sm["tv"], sm["tv"].final)
                v.tensor_scalar(out=pna[0:1, 0:512], in0=pna[0:1, 0:512], scalar1=nullmask[0:1, 0:1], scalar2=None, op0=ALU.add)
                v.inc(sm["vs"], 1)
                s.wait(sm["vs"], sm["vs"].final)
                s.activation(out=pThn[0:1, 512 * qc:512 * qc + 512], in_=pna[0:1, 0:512], func=AF.Exp, bias=0.0, scale=1.0)
                s.inc(sm["sp"], 1)
            for qc in range(4):
                pidx = qc % 2
                t.wait(sm["sp"], sm["sp"].final)
                t.wait(sm["pv2"], sm["pv2"].final)
                for sg in range(4):
                    t.matmul(pav[pidx][0:65, 0:512], vxo[:, 520 * sg + 65 * h:520 * sg + 65 * h + 65],
                             pTh[:, 2048 * sg + 512 * qc:2048 * sg + 512 * qc + 512], start=(sg == 0), stop=False)
                t.matmul(pav[pidx][0:65, 0:512], nullvx[0:1, 65 * h:65 * h + 65],
                         pThn[0:1, 512 * qc:512 * qc + 512], start=False, stop=True)
                t.inc(sm["ts"], 1)
                v.wait(sm["ts"], sm["ts"].final)
                v.tensor_scalar(out=oHs[0:65, 512 * qc:512 * qc + 512], in0=pav[pidx][0:65, 0:512], scalar1=1.0, scalar2=None, op0=ALU.mult)
                v.barrier()
                v.sinc(sm["pv2"], 1)
            v.sinc(sm["vt"], 1)
            t.wait(sm["vt"], sm["vt"].final)
            for cc_ in range(16):
                t.wait(sm["pv"], sm["pv"].final)
                t.transpose(ptb[0:128, 0:65], oHs[0:65, 128 * cc_:128 * cc_ + 128], identf[0:65, 0:65])
                t.inc(sm["ps"], 1)
                par = sm["sta"] if cc_ % 2 == 0 else sm["stb"]
                v.wait(sm["ps"], sm["ps"].final)
                v.wait(par, par.final)
                v.tensor_scalar(out=partsb[:, 65 * (cc_ % 2):65 * (cc_ % 2) + 65], in0=ptb[0:128, 0:65],
                                scalar1=1.0, scalar2=None, op0=ALU.mult)
                v.barrier()
                v.sinc(sm["pv"], 1)
                v.sinc(sm["vy"], 1)
                y.wait(sm["vy"], sm["vy"].final)
                y.wait(par, par.final)
                y.dma_start(out=io["d_part"][128 * cc_:128 * cc_ + 128, 65 * h:65 * h + 65],
                            in_=partsb[:, 65 * (cc_ % 2):65 * (cc_ % 2) + 65])
                y.inc(par, 16)
        y.wait_cur(sm["st2"])
        y.wait_cur(sm["sta"])
        y.wait_cur(sm["stb"])
        g.e.wait_ge(sm["st2"].h, sm["st2"].final)
        g.e.wait_ge(sm["sta"].h, sm["sta"].final)
        g.e.wait_ge(sm["stb"].h, sm["stb"].final)
        g.collective_compute("ReduceScatter", ALU.add, replica_groups=[list(range(8))],
                             ins=[io["d_part"].opt()], outs=[io["d_rs"].opt()])
        g.inc(sm["cc"], 1)
        y.wait_cur(sm["cc"])
        y.wait_cur(sm["qld"])
        for rr in range(2):
            y.dma_start(out=rsb[:, 520 * rr:520 * rr + 520], in_=io["d_rs"][128 * rr:128 * rr + 128, :])
            y.inc(sm["qld"], 16)
        y.wait_cur(sm["qld"])
        y.sinc(sm["gl"], 1)
        for c_ in [v, s, t]:
            c_.wait(sm["gl"], sm["gl"].final)
        for rr in range(2):
            for h in range(8):
                v.reciprocal(out=recH[:, 8 * rr + h:8 * rr + h + 1], in_=rsb[:, 520 * rr + 65 * h + 64:520 * rr + 65 * h + 65])
                v.barrier()
                v.tensor_scalar(out=rsb[:, 520 * rr + 65 * h:520 * rr + 65 * h + 64],
                                in0=rsb[:, 520 * rr + 65 * h:520 * rr + 65 * h + 64],
                                scalar1=recH[:, 8 * rr + h:8 * rr + h + 1], scalar2=None, op0=ALU.mult)
                v.barrier()
        v.sinc(sm["vt"], 1)
        t.wait(sm["vt"], sm["vt"].final)
        for rr in range(2):
            for h in range(8):
                t.wait(sm["pv"], sm["pv"].final)
                t.transpose(ptb[0:64, 0:128], rsb[:, 520 * rr + 65 * h:520 * rr + 65 * h + 64], identf[:, :])
                t.inc(sm["ps"], 1)
                v.wait(sm["ps"], sm["ps"].final)
                v.tensor_scalar(out=oTh[(h % 2) * 64:(h % 2) * 64 + 64, 256 * (h // 2) + 128 * rr:256 * (h // 2) + 128 * rr + 128],
                                in0=ptb[0:64, 0:128], scalar1=1.0, scalar2=None, op0=ALU.mult)
                v.barrier()
                v.sinc(sm["pv"], 1)
        wr_load([(1024 * c, 1024, io["w_out_h"][128 * c:128 * c + 128, :]) for c in range(4)])
        v.sinc(sm["vt"], 1)
        for m2 in range(2):
            t.wait(sm["wlr"], sm["wlr"].final)
            t.wait(sm["vt"], sm["vt"].final)
            for nh in range(2):
                for k in range(4):
                    t.matmul(pw[:, 512 * nh:512 * nh + 512], oTh[:, 256 * k + 128 * m2:256 * k + 128 * m2 + 128],
                             wR[:, 1024 * k + 512 * nh:1024 * k + 512 * nh + 512], start=(k == 0), stop=(k == 3))
            t.inc(sm["tv"], 1)
            v.wait(sm["tv"], sm["tv"].final)
            v.tensor_tensor(out=Hrow[:, 1024 * m2:1024 * m2 + 1024], in0=pw[:, :], in1=nullqb[:, :], op=ALU.subtract)
            v.barrier()
            v.sinc(sm["vt"], 1)
        for tau in range(16):
            v.tensor_scalar(out=ohq[:, QPAD * tau:QPAD * tau + QPAD], in0=iota512[:, 0:QPAD],
                            scalar1=RB["rpT_q"][:, tau:tau + 1], scalar2=None, op0=ALU.is_equal)
        v.barrier()
        v.sinc(sm["vt"], 1)
        t.wait(sm["vt"], sm["vt"].final)
        y.wait_cur(sm["sta"])
        y.wait_cur(sm["stb"])
        for tau in range(16):
            buf = tau % 2
            par = sm["sta"] if buf == 0 else sm["stb"]
            y.wait(sm["vy"], max(0, sm["vy"].final - 1))
            y.wait(par, par.final)
            y.wait_cur(sm["xld"])
            y.dma_start(out=xt2[:, 1024 * buf:1024 * buf + 1024], in_=io["d_xl"][64 + 128 * tau:64 + 128 * tau + 128, :])
            y.inc(sm["xld"], 16)
            for sc in range(2):
                t.wait(sm["pv"], sm["pv"].final)
                t.transpose(ptbb[0:128, 0:128], ohq[:, QPAD * tau + 128 * sc:QPAD * tau + 128 * sc + 128], identb[:, :])
                t.inc(sm["ps"], 1)
                v.wait(sm["ps"], sm["ps"].final)
                v.wait(sm["tv"], sm["tv"].final)
                v.tensor_scalar(out=ohqT[:, 128 * sc:128 * sc + 128], in0=ptbb[0:128, 0:128],
                                scalar1=1.0, scalar2=None, op0=ALU.mult)
                v.barrier()
                v.sinc(sm["pv"], 1)
                v.sinc(sm["vs"], 1)
            t.wait(sm["vs"], sm["vs"].final)
            for nh in range(2):
                for sc in range(2):
                    t.matmul(pw[:, 512 * nh:512 * nh + 512], ohqT[:, 128 * sc:128 * sc + 128],
                             Hrow[:, 1024 * sc + 512 * nh:1024 * sc + 512 * nh + 512], start=(sc == 0), stop=(sc == 1))
            t.inc(sm["tv"], 1)
            v.wait(sm["tv"], sm["tv"].final)
            v.wait(sm["xld"], sm["xld"].final)
            v.tensor_tensor(out=xt2[:, 1024 * buf:1024 * buf + 1024], in0=xt2[:, 1024 * buf:1024 * buf + 1024], in1=pw[:, :], op=ALU.add)
            v.barrier()
            v.sinc(sm["vy"], 1)
            v.sinc(sm["vs"], 1)
            y.wait(sm["vy"], sm["vy"].final)
            y.dma_start(out=io["out"][128 * tau:128 * tau + 128, :], in_=xt2[:, 1024 * buf:1024 * buf + 1024])
            y.inc(par, 16)
        y.wait_cur(sm["sta"])
        y.wait_cur(sm["stb"])
        dump()

    return p.finalize()


_NC_CACHE = {}


def kernel(**inputs):
    if "nc" not in _NC_CACHE:
        _NC_CACHE["nc"] = build()
    nc = _NC_CACHE["nc"]
    in_maps = [host_inputs(cid, inputs) for cid in range(8)]
    res = run_bass_kernel_spmd(nc, in_maps, core_ids=list(range(8)))
    out = np.concatenate([np.asarray(res.results[c]["out"]) for c in range(8)], 0)
    return out.reshape(1, 16384, 1024).astype(np.float32)
